# version 03
# baseline (speedup 1.0000x reference)
import contextlib
import numpy as np
import concourse.bass as bass
import concourse.mybir as mybir
from concourse.bass_utils import run_bass_kernel_spmd

F32 = mybir.dt.float32
BF16 = mybir.dt.bfloat16
I32 = mybir.dt.int32
U8 = mybir.dt.uint8
AF = mybir.ActivationFunctionType
ALU = mybir.AluOpType
AX = mybir.AxisListType

B_SZ, SEQ, D = 4, 4096, 1024
T = 2048
HALO = 48
TE = T + HALO
DK = D // 128
LN_EPS = 1e-5
NST = 16
NCORE = 8

ENGINES = ("sync", "gpsimd", "tensor", "vector", "scalar")
DBG = False
DBG2 = False
SKIP = set()
L3DBG = False


def tchunks(total, step=512):
    out, o = [], 0
    while o < total:
        out.append((o, min(step, total - o)))
        o += step
    return out


class Prog:

    def __init__(self):
        self.nc = bass.Bass("TRN2", target_bir_lowering=False)
        self.es = contextlib.ExitStack()
        self.phases = []
        self.cur = None
        self.phase()

    def sb(self, name, shape, dtype):
        return self.es.enter_context(self.nc.sbuf_tensor(name, list(shape), dtype))

    def psum(self, name, shape, dtype=F32):
        return self.es.enter_context(self.nc.psum_tensor(name, list(shape), dtype))

    def din(self, name, shape, dtype=F32):
        return self.nc.dram_tensor(name, list(shape), dtype, kind="ExternalInput").ap()

    def dout(self, name, shape, dtype=F32):
        return self.nc.dram_tensor(name, list(shape), dtype, kind="ExternalOutput").ap()

    def phase(self):
        self.cur = {e: [] for e in ENGINES}
        self.phases.append(self.cur)

    def _q(self, eng, fn, dma=False):
        self.cur[eng].append((fn, dma))

    def dma(self, eng, out, in_, slow=False):
        if slow:
            self._q(eng, lambda e: e.dma_start(
                out=out, in_=in_, allow_slow_non_contiguous=True), dma=True)
        else:
            self._q(eng, lambda e: e.dma_start(out=out, in_=in_), dma=True)

    def mm(self, out, lhsT, rhs, start=True, stop=True):
        self._q("tensor", lambda e: e.matmul(out, lhsT, rhs, start=start, stop=stop))

    def transp(self, out_ps, in_sb, ident):
        self._q("tensor", lambda e: e.transpose(out_ps, in_sb, ident))

    def cpy(self, out, in_, eng="vector"):
        self._q(eng, lambda e: e.tensor_copy(out, in_))

    def tt(self, out, a, b, op, eng="vector"):
        self._q(eng, lambda e: e.tensor_tensor(out, a, b, op))

    def ts(self, out, a, s1, op, s2=None, op2=ALU.bypass, eng="vector",
           accum_out=None):
        if s2 is None and accum_out is None:
            self._q(eng, lambda e: e.tensor_single_scalar(out, a, s1, op))
        else:
            self._q(eng, lambda e: e.tensor_scalar(out, a, s1, s2, op, op2,
                                                   accum_out=accum_out))

    def stt(self, out, in0, scalar, in1, op0, op1, eng="vector"):
        self._q(eng, lambda e: e.scalar_tensor_tensor(out, in0, scalar, in1,
                                                      op0, op1))

    def act(self, out, in_, func, bias=0.0, scale=1.0, accum_out=None):
        self._q("scalar", lambda e: e.activation(out, in_, func, bias=bias,
                                                 scale=scale,
                                                 accum_out=accum_out))

    def red(self, out, in_, op, absval=None, negate=None, eng="vector"):
        self._q(eng, lambda e: e.tensor_reduce(out, in_, AX.X, op,
                                               apply_absolute_value=absval,
                                               negate=negate))

    def scan(self, out, d0, d1, init, op0, op1, eng="vector"):
        self._q(eng, lambda e: e.tensor_tensor_scan(out, d0, d1, init, op0, op1))

    def memset(self, ap, v, eng="vector"):
        self._q(eng, lambda e: e.memset(ap, v))

    def recip(self, out, in_):
        self._q("vector", lambda e: e.reciprocal(out, in_))

    def select(self, out, mask, a, b):
        self._q("vector", lambda e: e.select(out, mask, a, b))

    def iota(self, out, pattern, base=0, cmul=0):
        self._q("gpsimd", lambda e: e.iota(out, pattern, base=base,
                                           channel_multiplier=cmul,
                                           allow_small_or_imprecise_dtypes=True))

    def affsel(self, out, in_, pattern, cmp, fill, base=0, cmul=0):
        self._q("gpsimd", lambda e: e.affine_select(out, in_, pattern, cmp,
                                                    fill, base=base,
                                                    channel_multiplier=cmul))

    def finish(self):
        nc = self.nc
        barrier = self.es.enter_context(nc.semaphore("B"))
        sbar = self.es.enter_context(nc.semaphore("BS"))
        gbar = self.es.enter_context(nc.semaphore("BG"))
        incs, sincs, gincs = [], [], []
        for ph in self.phases:
            inc, sinc, ginc = 0, 0, 0
            for e in ENGINES:
                ops = ph[e]
                if not ops:
                    continue
                nd = 16 * sum(1 for (_f, d) in ops if d)
                if e == "gpsimd":
                    ginc += nd
                else:
                    sinc += nd
                if not ops[-1][1]:
                    inc += 1
            incs.append(inc)
            sincs.append(sinc)
            gincs.append(ginc)
        cum = np.cumsum([0] + incs)
        scum = np.cumsum([0] + sincs)
        gcum = np.cumsum([0] + gincs)
        blk = self.es.enter_context(nc.Block())

        def body_for(ename):
            def body(eng):
                lastb = lasts = lastg = -1
                for pi, ph in enumerate(self.phases):
                    ops = ph[ename]
                    if not ops:
                        continue
                    if pi > 0:
                        if cum[pi] > 0 and cum[pi] != lastb:
                            eng.wait_ge(barrier, int(cum[pi]))
                            lastb = cum[pi]
                        if scum[pi] > 0 and scum[pi] != lasts:
                            eng.wait_ge(sbar, int(scum[pi]))
                            lasts = scum[pi]
                        if gcum[pi] > 0 and gcum[pi] != lastg:
                            eng.wait_ge(gbar, int(gcum[pi]))
                            lastg = gcum[pi]
                    n = len(ops)
                    for i, (fn, is_dma) in enumerate(ops):
                        inst = fn(eng)
                        if is_dma:
                            inst.then_inc(gbar if ename == "gpsimd" else sbar, 16)
                        elif i == n - 1:
                            inst.then_inc(barrier, 1)
                if ename == "sync":
                    eng.wait_ge(barrier, int(cum[len(self.phases)]))
                    eng.wait_ge(sbar, int(scum[len(self.phases)]))
                    eng.wait_ge(gbar, int(gcum[len(self.phases)]))
            return body

        blk.sync(body_for("sync"))
        blk.gpsimd(body_for("gpsimd"))
        blk.tensor(body_for("tensor"))
        blk.vector(body_for("vector"))
        blk.scalar(body_for("scalar"))
        return nc


class Arena:

    def __init__(self, p, nbytes):
        self.t = p.sb("arena", [128, nbytes], U8)
        self.nbytes = nbytes

    def f32(self, off, cols, rows=128):
        return self.t[0:rows, off:off + 4 * cols].bitcast(F32)

    def bf16(self, off, cols, rows=128):
        return self.t[0:rows, off:off + 2 * cols].bitcast(BF16)

    def i32(self, off, cols, rows=128):
        return self.t[0:rows, off:off + 4 * cols].bitcast(I32)


class Bump:
    def __init__(self, limit):
        self.off, self.limit = 0, limit

    def take(self, nbytes):
        o = (self.off + 63) & ~63
        self.off = o + nbytes
        assert self.off <= self.limit, (self.off, self.limit)
        return o

    def mark(self):
        return self.off

    def reset(self, m):
        self.off = m



def emit_row_bcast(p, items, pp, ones_f32):
    for g in range(0, len(items), 8):
        sub = items[g:g + 8]
        p.phase()
        for i, (src, dst, n) in enumerate(sub):
            p.mm(pp[i][0:128, 0:n], ones_f32[0:1, 0:128], src)
        p.phase()
        for i, (src, dst, n) in enumerate(sub):
            p.cpy(dst, pp[i][0:128, 0:n])


def emit_layernorm(p, lb, TT, x_of, xbf_of, xsq_of, out_of, pp, ones_col,
                   ones_row, gamma_col=None, beta_col=None, x_is_fp32=False,
                   t2=None, xsq_group=False):
    cks = tchunks(TT)
    if not xsq_group:
        p.phase()
        for k in range(DK):
            p.act(xsq_of(k), x_of(k), AF.Square)
            if x_is_fp32:
                p.dma("gpsimd", xbf_of(k), x_of(k))
    elif x_is_fp32:
        p.phase()
        for k in range(DK):
            p.dma("gpsimd", xbf_of(k), x_of(k))
    for g in range(0, len(cks), 2):
        sub = cks[g:g + 2]
        gbase = sub[0][0]
        if xsq_group:
            p.phase()
            for k in range(DK):
                gw = sum(n for (_o, n) in sub)
                p.act(xsq_of(k)[:, 0:gw],
                      x_of(k)[:, gbase:gbase + gw], AF.Square)
        p.phase()
        for i, (o, n) in enumerate(sub):
            lo = (o - gbase) if xsq_group else o
            for k in range(DK):
                p.mm(pp[2 * i][0:1, 0:n], ones_col[:, 0:1],
                     xbf_of(k)[:, o:o + n], start=(k == 0), stop=(k == DK - 1))
            for k in range(DK):
                p.mm(pp[2 * i + 1][0:1, 0:n], ones_col[:, 0:1],
                     xsq_of(k)[:, lo:lo + n], start=(k == 0),
                     stop=(k == DK - 1))
        p.phase()
        for i, (o, n) in enumerate(sub):
            p.cpy(lb.srow[0:1, o:o + n], pp[2 * i][0:1, 0:n])
            p.cpy(lb.srow[0:1, TT + o:TT + o + n], pp[2 * i + 1][0:1, 0:n])
    p.phase()
    p.ts(lb.mu[0:1, 0:TT], lb.srow[0:1, 0:TT], 1.0 / D, ALU.mult)
    p.ts(lb.srow[0:1, TT:2 * TT], lb.srow[0:1, TT:2 * TT], 1.0 / D, ALU.mult)
    p.phase()
    p.tt(lb.srow[0:1, 0:TT], lb.mu[0:1, 0:TT], lb.mu[0:1, 0:TT], ALU.mult)
    p.phase()
    p.tt(lb.srow[0:1, TT:2 * TT], lb.srow[0:1, TT:2 * TT], lb.srow[0:1, 0:TT],
         ALU.subtract)
    p.phase()
    p.ts(lb.srow[0:1, TT:2 * TT], lb.srow[0:1, TT:2 * TT], LN_EPS, ALU.add)
    p.phase()
    p.act(lb.srow[0:1, 0:TT], lb.srow[0:1, TT:2 * TT], AF.Ln)
    p.phase()
    p.act(lb.ab[0:1, 0:TT], lb.srow[0:1, 0:TT], AF.Exp, scale=-0.5)
    p.phase()
    p.stt(lb.ab[0:1, TT:2 * TT], lb.mu[0:1, 0:TT], -1.0, lb.ab[0:1, 0:TT],
          ALU.mult, ALU.mult)
    items = []
    for h in range(2):
        for (o, n) in cks:
            items.append((lb.ab[0:1, h * TT + o:h * TT + o + n],
                          lb.abc[:, h * TT + o:h * TT + o + n], n))
    emit_row_bcast(p, items, pp, ones_row)
    for k in range(DK):
        p.phase()
        src = xbf_of(k) if x_is_fp32 else x_of(k)
        p.tt(t2[:, 0:TT], src, lb.abc[:, 0:TT], ALU.mult)
        p.phase()
        if gamma_col is None:
            p.tt(out_of(k), t2[:, 0:TT], lb.abc[:, TT:2 * TT], ALU.add)
        else:
            p.tt(t2[:, 0:TT], t2[:, 0:TT], lb.abc[:, TT:2 * TT], ALU.add)
            p.phase()
            p.ts(out_of(k), t2[:, 0:TT], gamma_col[:, k:k + 1], ALU.mult,
                 s2=beta_col[:, k:k + 1], op2=ALU.add)


def load_col(p, dst, src_dram):
    for k in range(DK):
        p.dma("sync", dst[:, k:k + 1], src_dram[k * 128:(k + 1) * 128, 0:1])


class NS:
    def __init__(self, **kw):
        self.__dict__.update(kw)



def build_L01():
    p = Prog()
    x0e = p.din("x0e", [D, TE])
    a_log = p.din("a_log", [D, NST])
    bwT = p.din("bwT", [D, NST])
    cwT = p.din("cwT", [D, NST])
    d_vec = p.din("d_vec", [D, 1])
    gwT = p.din("gwT", [D, D])
    gate_b = p.din("gate_b", [D, 1])
    ln_g = p.din("ln_g", [D, 1])
    ln_b = p.din("ln_b", [D, 1])
    ones_c = p.din("ones_c", [1, 128])
    ones_col_in = p.din("ones_col", [128, 1])
    sel16 = p.din("sel16", [NST, NST * 128])
    x1_out = p.dout("x1", [D, T])
    if DBG:
        dbg_xn = p.dout("dbg_xn", [D, TE])
        dbg_yscan = p.dout("dbg_yscan", [D, T])
        dbg_y = p.dout("dbg_y", [D, T])

    pp = [p.psum(f"pp{i}", [128, 512]) for i in range(8)]
    A = Arena(p, 209536)

    F1o, B1o, B2o, B3o = 0, 65536, 99072, 132608
    YO = 166144
    F1 = [A.f32(F1o + k * T * 4, T) for k in range(DK)]
    B1 = [A.bf16(B1o + k * TE * 2, TE) for k in range(DK)]
    B2 = [A.bf16(B2o + k * TE * 2, TE) for k in range(DK)]
    B3 = [A.bf16(B3o + k * TE * 2, TE) for k in range(DK)]
    lb = NS(srow=A.f32(YO, 2 * TE, rows=1),
            mu=A.f32(YO + 16768, TE, rows=1),
            ab=A.bf16(YO + 25152, 2 * TE, rows=1),
            abc=A.bf16(YO + 33536, 2 * TE))
    bx = A.bf16(YO, TE, rows=NST)
    cx = A.bf16(YO + 4224, TE, rows=NST)
    brep = A.bf16(YO + 8448, TE)
    crep = A.bf16(YO + 12672, TE)
    sel_bf = A.bf16(YO + 16896, NST * 128, rows=NST)
    t2_ln1 = A.bf16(F1o, TE)
    t2_ln2 = A.bf16(B3o, T)
    gwm = [A.bf16(B3o, DK * 128), A.bf16(B3o + 2048, DK * 128)]
    gsm = A.bf16(B3o + 4096, T)
    sm1 = A.bf16(B3o + 8192, T)
    CO = YO + 41920
    a_sb = A.f32(CO, DK * NST)
    bw_sb = A.bf16(CO + 512, DK * NST)
    cw_sb = A.bf16(CO + 768, DK * NST)
    dcol = A.f32(CO + 1024, DK)
    gbcol = A.f32(CO + 1056, DK)
    lgcol = A.f32(CO + 1088, DK)
    lbcol = A.f32(CO + 1120, DK)
    onebf = A.bf16(CO + 1152, 128, rows=1)
    onecol = A.bf16(CO + 1440, 1)

    for k in range(DK):
        p.dma("gpsimd", B2[k], x0e[k * 128:(k + 1) * 128, :])
        p.dma("sync", a_sb[:, k * NST:(k + 1) * NST],
              a_log[k * 128:(k + 1) * 128, :])
        p.dma("gpsimd", bw_sb[:, k * NST:(k + 1) * NST],
              bwT[k * 128:(k + 1) * 128, :])
        p.dma("gpsimd", cw_sb[:, k * NST:(k + 1) * NST],
              cwT[k * 128:(k + 1) * 128, :])
    load_col(p, dcol, d_vec)
    load_col(p, gbcol, gate_b)
    load_col(p, lgcol, ln_g)
    load_col(p, lbcol, ln_b)
    p.dma("gpsimd", onebf[0:1, :], ones_c)
    p.dma("gpsimd", onecol[:, 0:1], ones_col_in)
    p.phase()
    for k in range(DK):
        p.act(a_sb[:, k * NST:(k + 1) * NST], a_sb[:, k * NST:(k + 1) * NST],
              AF.Exp)

    emit_layernorm(p, lb, TE,
                   x_of=lambda k: B2[k], xbf_of=lambda k: B2[k],
                   xsq_of=lambda k: B3[k], out_of=lambda k: B1[k],
                   pp=pp, ones_col=onecol, ones_row=onebf, t2=t2_ln1)

    if DBG:
        p.phase()
        for k in range(DK):
            p.dma("gpsimd", dbg_xn[k * 128:(k + 1) * 128, :], B1[k])
    p.phase()
    p.dma("gpsimd", sel_bf[:, :], sel16)
    cks = tchunks(TE)
    for (rows_w, dst) in ((bw_sb, bx), (cw_sb, cx)):
        p.phase()
        for i, (o, n) in enumerate(cks):
            for k in range(DK):
                p.mm(pp[i][0:NST, 0:n], rows_w[:, k * NST:(k + 1) * NST],
                     B1[k][:, o:o + n], start=(k == 0), stop=(k == DK - 1))
        p.phase()
        for i, (o, n) in enumerate(cks):
            p.cpy(dst[0:NST, o:o + n], pp[i][0:NST, 0:n])

    p.memset(A.f32(F1o, DK * T), 0.0, eng="gpsimd")
    for ns in range(NST):
        for (rows_src, dst) in ((bx, brep), (cx, crep)):
            p.phase()
            for i, (o, n) in enumerate(cks):
                p.mm(pp[i][0:128, 0:n], sel_bf[:, ns * 128:(ns + 1) * 128],
                     rows_src[0:NST, o:o + n])
            p.phase()
            for i, (o, n) in enumerate(cks):
                if i % 2 == 0:
                    p.cpy(dst[:, o:o + n], pp[i][0:128, 0:n])
                else:
                    p.act(dst[:, o:o + n], pp[i][0:128, 0:n], AF.Copy)
        p.phase()
        for k in range(DK):
            p.tt(B2[k], B1[k], brep[:, :], ALU.mult,
                 eng=("vector" if k % 2 == 0 else "gpsimd"))
        p.phase()
        for k in range(DK):
            p.scan(B3[k],
                   a_sb[:, k * NST + ns:k * NST + ns + 1].to_broadcast((128, TE)),
                   B2[k], 0.0, ALU.mult, ALU.add)
        p.phase()
        for k in range(DK):
            p.tt(B2[k][:, 0:T], B3[k][:, HALO:TE], crep[:, HALO:TE], ALU.mult,
                 eng=("vector" if k % 2 == 0 else "gpsimd"))
        p.phase()
        for k in range(DK):
            p.tt(F1[k], F1[k], B2[k][:, 0:T], ALU.add,
                 eng=("vector" if k % 2 == 1 else "gpsimd"))

    if DBG:
        p.phase()
        for k in range(DK):
            p.dma("sync", dbg_yscan[k * 128:(k + 1) * 128, :], F1[k])
    p.phase()
    for k in range(DK):
        p.stt(F1[k], B1[k][:, HALO:TE], dcol[:, k:k + 1], F1[k],
              ALU.mult, ALU.add)

    if DBG:
        p.phase()
        for k in range(DK):
            p.dma("sync", dbg_y[k * 128:(k + 1) * 128, :], F1[k])
    ct = tchunks(T)
    p.phase()
    for k in range(DK):
        p.dma("gpsimd", gwm[0][:, k * 128:(k + 1) * 128],
              gwT[k * 128:(k + 1) * 128, 0:128])
    for m in range(DK):
        p.phase()
        if m + 1 < DK:
            for k in range(DK):
                p.dma("gpsimd", gwm[(m + 1) % 2][:, k * 128:(k + 1) * 128],
                      gwT[k * 128:(k + 1) * 128, (m + 1) * 128:(m + 2) * 128])
        for i, (o, n) in enumerate(ct):
            for k in range(DK):
                p.mm(pp[i][0:128, 0:n], gwm[m % 2][:, k * 128:(k + 1) * 128],
                     B1[k][:, HALO + o:HALO + o + n],
                     start=(k == 0), stop=(k == DK - 1))
        p.phase()
        for i, (o, n) in enumerate(ct):
            p.act(gsm[:, o:o + n], pp[i][0:128, 0:n], AF.Sigmoid,
                  bias=gbcol[:, m:m + 1])
        p.phase()
        p.tt(sm1[:, :], F1[m], B1[m][:, HALO:TE], ALU.subtract)
        p.phase()
        p.tt(sm1[:, :], sm1[:, :], gsm[:, :], ALU.mult)
        p.phase()
        p.tt(F1[m], sm1[:, :], B1[m][:, HALO:TE], ALU.add)

    B2t = [A.bf16(B2o + k * T * 2, T) for k in range(DK)]
    B3t = [A.bf16(B3o + k * T * 2, T) for k in range(DK)]
    B1t = [A.bf16(B1o + k * T * 2, T) for k in range(DK)]
    emit_layernorm(p, lb, T,
                   x_of=lambda k: F1[k], xbf_of=lambda k: B2t[k],
                   xsq_of=lambda k: B3t[k], out_of=lambda k: B1t[k],
                   pp=pp, ones_col=onecol, ones_row=onebf,
                   gamma_col=lgcol, beta_col=lbcol, x_is_fp32=True,
                   t2=t2_ln2)

    p.phase()
    for k in range(DK):
        p.dma("sync", F1[k], x0e[k * 128:(k + 1) * 128, HALO:TE])
    p.phase()
    for k in range(DK):
        p.tt(F1[k], F1[k], B1t[k], ALU.add)
    p.phase()
    for k in range(DK):
        p.dma("sync", x1_out[k * 128:(k + 1) * 128, :], F1[k])
    return p.finish()




FH = 1536
TL = FH + T
QO = 1024
NQ = TL - QO
KO = 512
NK = TL - KO
DH = 64
INNER = 512
HIDF = 2730
NEG_BIG = -30000.0


def build_L0Fa(with_preln, upto=99):
    p = Prog()
    x1e = p.din("x1e", [D, TL])
    wqkv = p.din("wqkv", [D, 3 * INNER])
    wqrot = p.din("wqrot", [D, INNER])
    wkrot = p.din("wkrot", [D, INNER])
    wo_in = p.din("wo_in", [INNER, D])
    qs2_in = p.din("qs2", [128, 1])
    ks2_in = p.din("ks2", [128, 1])
    qs2p_in = p.din("qs2p", [128, 1])
    ks2p_in = p.din("ks2p", [128, 1])
    c1q_in = p.din("c1q", [128, NQ])
    c2q_in = p.din("c2q", [128, NQ])
    c1k_in = p.din("c1k", [128, NK])
    c2k_in = p.din("c2k", [128, NK])
    amask_in = p.din("amask", [8 * 128, 512])
    ident_in = p.din("ident", [128, 128])
    kvt_in = p.din("kvt", [128, 24])
    ones_c = p.din("ones_c", [1, 128])
    ones_col_in = p.din("ones_col", [128, 1])
    e2_in = p.din("e2", [128, 2])
    e2t_in = p.din("e2t", [2, 128])
    u_out = p.dout("u", [D, T + 1])
    if DBG2 and not DBG:
        dbg_bits = p.dout("dbg_bits", [512, 256])
    if DBG:
        dbg_xh = p.dout("dbg_xh", [D, TL])
        dbg_qh = p.dout("dbg_qh", [512, NQ])
        dbg_kh = p.dout("dbg_kh", [512, NK])
        dbg_nb = p.dout("dbg_nb", [128, 1])
        dbg_ao = p.dout("dbg_ao", [512, 512])
        dbg_sexp = p.dout("dbg_sexp", [8 * 1024, 512])
        dbg_den = p.dout("dbg_den", [16, 512])
        dbg_bits = p.dout("dbg_bits", [512, 256])
        dbg_av2 = p.dout("dbg_av2", [64, 512])
        dbg_tr = p.dout("dbg_tr", [8, 512])

    pp = [p.psum(f"pp{i}", [128, 512]) for i in range(8)]
    A = Arena(p, 210816)

    xh = [A.bf16(0 + k * TL * 2, TL) for k in range(DK)]
    qh = [A.bf16(57344 + i * NQ * 2, NQ) for i in range(4)]
    kh = [A.bf16(77824 + i * NK * 2, NK) for i in range(4)]
    vt = [A.bf16(102400 + i * 1024, 512) for i in range(24)]
    c1q = A.bf16(126976, NQ)
    c2q = A.bf16(126976 + 5120, NQ)
    c1k = A.bf16(126976 + 10240, NK)
    c2k = A.bf16(126976 + 16384, NK)
    masks = [A.bf16(149504 + i * 1024, 512) for i in range(8)]
    wvb = A.bf16(157696, 8 * 512)
    wob = A.bf16(157696, 4 * 1024)
    wmain = A.bf16(165888, DK * 128)
    wrot = A.bf16(167936, DK * 128)
    identb = A.bf16(169984, 128)
    identf = A.f32(170496, 128)
    qt = A.bf16(172032, NK)
    qrt = A.bf16(178176, NK)
    qsq = A.bf16(184320, NK)
    nsr = A.bf16(190464, NK, rows=2)
    inb = A.bf16(196608, NK)
    sexp = [A.bf16(172032 + i * 1024, 512) for i in range(8)]
    recb = A.f32(180224, 512, rows=64)
    drow = A.f32(182272, 512, rows=1)
    drow2 = A.f32(182272 + 2048, 512, rows=1)
    aow = [A.bf16(186368 + i * 1024, 512) for i in range(4)]
    av_sb = A.f32(190464, 512)
    av2_sb = A.f32(192512, 512)
    wst = [A.f32(194560, 512), A.f32(196608, 512)]
    x1buf = A.f32(126976, 8 * 512)
    ln_sq = [A.bf16(57344 + k * TL * 2, TL) for k in range(DK)]
    lnb = NS(srow=A.f32(126976, 2 * TL, rows=1),
             mu=A.f32(155648, TL, rows=1),
             ab=A.bf16(172032, 2 * TL, rows=1),
             abc=A.bf16(186368, 2 * TL))
    t2ln = A.bf16(200704, TL)
    qs2 = A.f32(207872, 1)
    ks2 = A.f32(207904, 1)
    qs2p = A.f32(210688, 1)
    ks2p = A.f32(210720, 1)
    onebf = A.bf16(207936, 128, rows=1)
    onecol = A.bf16(208192, 1)
    e2c = A.bf16(208224, 2)
    e2t = A.bf16(208256, 128, rows=2)
    nbcol = A.f32(208512, 1)
    kvt = A.bf16(208544, 24)
    babs = A.f32(208608, 1)
    brow = A.f32(208640, 384, rows=1)
    onef = A.f32(208640 + 1536, 128, rows=1)
    assert 208640 + 1536 + 512 <= 210752

    cq = tchunks(NQ)
    ck = tchunks(NK)

    for k in range(DK):
        p.dma("gpsimd", xh[k], x1e[k * 128:(k + 1) * 128, :])
    p.dma("sync", qs2[:, 0:1], qs2_in)
    p.dma("sync", ks2[:, 0:1], ks2_in)
    p.dma("sync", qs2p[:, 0:1], qs2p_in)
    p.dma("sync", ks2p[:, 0:1], ks2p_in)
    p.dma("gpsimd", onebf[0:1, :], ones_c)
    p.dma("sync", onef[0:1, :], ones_c)
    p.dma("gpsimd", onecol[:, 0:1], ones_col_in)
    p.dma("gpsimd", e2c[:, :], e2_in)
    p.dma("gpsimd", e2t[0:2, :], e2t_in)
    p.dma("gpsimd", kvt[:, :], kvt_in)

    if upto < 1:
        return p.finish()
    if with_preln:
        emit_layernorm(p, lnb, TL,
                       x_of=lambda k: xh[k], xbf_of=lambda k: xh[k],
                       xsq_of=lambda k: ln_sq[k],
                       out_of=lambda k: xh[k],
                       pp=pp, ones_col=onecol, ones_row=onebf, t2=t2ln)

    if DBG:
        p.phase()
        for k in range(DK):
            p.dma("gpsimd", dbg_xh[k * 128:(k + 1) * 128, :], xh[k])
    if upto < 2:
        return p.finish()
    p.phase()
    p.dma("gpsimd", identb[:, :], ident_in)
    p.dma("sync", identf[:, :], ident_in)
    for i in range(8):
        p.dma("gpsimd", masks[i], amask_in[i * 128:(i + 1) * 128, :])
    for k in range(DK):
        p.dma("gpsimd", wvb[:, k * 512:(k + 1) * 512],
              wqkv[k * 128:(k + 1) * 128, 2 * INNER:3 * INNER])
    p.dma("gpsimd", qt[:, 0:NQ], c1q_in)
    p.dma("gpsimd", qrt[:, 0:NQ], c2q_in)
    p.dma("gpsimd", qsq[:, 0:NK], c1k_in)
    p.dma("gpsimd", inb[:, 0:NK], c2k_in)
    p.phase()
    p.ts(c1q[:, :], qt[:, 0:NQ], qs2[:, 0:1], ALU.mult)
    p.ts(c2q[:, :], qrt[:, 0:NQ], qs2p[:, 0:1], ALU.mult)
    p.ts(c1k[:, :], qsq[:, 0:NK], ks2[:, 0:1], ALU.mult)
    p.ts(c2k[:, :], inb[:, 0:NK], ks2p[:, 0:1], ALU.mult)

    if upto < 3:
        return p.finish()
    dscq = p.nc.dram_tensor("dscq", [128, 1], F32, kind="Internal").ap()
    dsck = p.nc.dram_tensor("dsck", [128, 1], F32, kind="Internal").ap()
    p.phase()
    p.act(babs[:, 0:1], qs2[:, 0:1], AF.Abs)
    p.phase()
    p.dma("sync", dscq, babs[:, 0:1])
    p.phase()
    p.act(babs[:, 0:1], ks2[:, 0:1], AF.Abs)
    p.phase()
    p.dma("sync", dsck, babs[:, 0:1])
    p.phase()
    p.dma("sync", brow[0:1, 0:128], dscq.rearrange("p o -> o p"))
    p.dma("sync", brow[0:1, 128:256], dsck.rearrange("p o -> o p"))
    p.phase()
    p.red(brow[0:1, 256:257], brow[0:1, 0:128], ALU.max)
    p.red(brow[0:1, 257:258], brow[0:1, 128:256], ALU.max)
    p.phase()
    p.tt(brow[0:1, 258:259], brow[0:1, 256:257], brow[0:1, 257:258], ALU.mult)
    p.phase()
    p.ts(brow[0:1, 258:259], brow[0:1, 258:259], -8.0, ALU.mult)
    p.phase()
    p.mm(pp[1][0:128, 0:1], onef[0:1, 0:128], brow[0:1, 258:259])
    p.phase()
    p.cpy(nbcol[:, 0:1], pp[1][0:128, 0:1])

    def qk_build(pt, is_q):
        n = NQ if is_q else NK
        cks_ = cq if is_q else ck
        off = QO if is_q else KO
        dst = qh[pt] if is_q else kh[pt]
        c1 = c1q if is_q else c1k
        c2 = c2q if is_q else c2k
        p.phase()
        for i, (o, nn) in enumerate(cks_):
            for k in range(DK):
                rs = off + o - (1 if k >= 4 else 0)
                p.mm(pp[i][0:128, 0:nn], wmain[:, k * 128:(k + 1) * 128],
                     xh[k][:, rs:rs + nn], start=(k == 0), stop=(k == DK - 1))
        p.phase()
        for i, (o, nn) in enumerate(cks_):
            p.cpy(qt[:, o:o + nn], pp[i][0:128, 0:nn])
        p.phase()
        for i, (o, nn) in enumerate(cks_):
            p.act(qsq[:, o:o + nn], pp[i][0:128, 0:nn], AF.Square)
        p.phase()
        for i, (o, nn) in enumerate(cks_):
            for k in range(DK):
                rs = off + o - (1 if k >= 4 else 0)
                p.mm(pp[i][0:128, 0:nn], wrot[:, k * 128:(k + 1) * 128],
                     xh[k][:, rs:rs + nn], start=(k == 0), stop=(k == DK - 1))
        p.phase()
        for i, (o, nn) in enumerate(cks_):
            p.cpy(qrt[:, o:o + nn], pp[i][0:128, 0:nn])
        p.phase()
        for i, (o, nn) in enumerate(cks_):
            p.mm(pp[i][0:2, 0:nn], e2c[:, 0:2], qsq[:, o:o + nn])
        p.phase()
        for i, (o, nn) in enumerate(cks_):
            p.cpy(nsr[0:2, o:o + nn], pp[i][0:2, 0:nn])
        p.phase()
        p.ts(nsr[0:2, 0:n], nsr[0:2, 0:n], 1e-24, ALU.max)
        p.phase()
        p.act(nsr[0:2, 0:n], nsr[0:2, 0:n], AF.Ln)
        p.phase()
        p.act(nsr[0:2, 0:n], nsr[0:2, 0:n], AF.Exp, scale=-0.5)
        p.phase()
        for i, (o, nn) in enumerate(cks_):
            p.mm(pp[i][0:128, 0:nn], e2t[0:2, :], nsr[0:2, o:o + nn])
        p.phase()
        for i, (o, nn) in enumerate(cks_):
            p.cpy(inb[:, o:o + nn], pp[i][0:128, 0:nn])
        p.phase()
        p.tt(qt[:, 0:n], qt[:, 0:n], c1[:, 0:n], ALU.mult)
        p.tt(qrt[:, 0:n], qrt[:, 0:n], c2[:, 0:n], ALU.mult, eng="gpsimd")
        p.phase()
        p.tt(qt[:, 0:n], qt[:, 0:n], qrt[:, 0:n], ALU.add)
        p.phase()
        p.tt(dst, qt[:, 0:n], inb[:, 0:n], ALU.mult)

    if upto < 4:
        return p.finish()
    for pt in range(4):
        p.phase()
        for k in range(DK):
            p.dma("gpsimd", wmain[:, k * 128:(k + 1) * 128],
                  wqkv[k * 128:(k + 1) * 128, pt * 128:(pt + 1) * 128])
            p.dma("gpsimd", wrot[:, k * 128:(k + 1) * 128],
                  wqrot[k * 128:(k + 1) * 128, pt * 128:(pt + 1) * 128])
        qk_build(pt, True)
    for pt in range(4):
        p.phase()
        for k in range(DK):
            p.dma("gpsimd", wmain[:, k * 128:(k + 1) * 128],
                  wqkv[k * 128:(k + 1) * 128,
                       INNER + pt * 128:INNER + (pt + 1) * 128])
            p.dma("gpsimd", wrot[:, k * 128:(k + 1) * 128],
                  wkrot[k * 128:(k + 1) * 128, pt * 128:(pt + 1) * 128])
        qk_build(pt, False)

    if DBG:
        p.phase()
        for i in range(4):
            p.dma("gpsimd", dbg_qh[i * 128:(i + 1) * 128, :], qh[i])
            p.dma("gpsimd", dbg_kh[i * 128:(i + 1) * 128, :], kh[i])
        p.dma("sync", dbg_nb, nbcol[:, 0:1])
    if upto < 5:
        return p.finish()
    for grp in range(3):
        p.phase()
        for j in range(8):
            ti = grp * 8 + j
            tok0 = KO + ti * 128
            for k in range(DK):
                sh = tok0 - (1 if k >= 4 else 0)
                p.mm(pp[j][0:128, 0:512], xh[k][:, sh:sh + 128],
                     wvb[:, k * 512:(k + 1) * 512],
                     start=(k == 0), stop=(k == DK - 1))
        p.phase()
        for j in range(8):
            p.cpy(vt[grp * 8 + j], pp[j][0:128, 0:512])

    p.phase()
    for kt in range(4):
        p.dma("gpsimd", wob[:, kt * 1024:(kt + 1) * 1024],
              wo_in[kt * 128:(kt + 1) * 128, :])

    if upto < 6:
        return p.finish()
    for w in range(5):
        for h in range(8):
            pt, rb = h // 2, (h % 2) * 64
            p.phase()
            for kc in range(8):
                kcol = QO + (w - 1) * 512 + kc * 128 - KO
                p.mm(pp[kc][0:128, 0:512],
                     kh[pt][rb:rb + 64, kcol:kcol + 128],
                     qh[pt][rb:rb + 64, w * 512:(w + 1) * 512],
                     start=True, stop=False)
                p.mm(pp[kc][0:128, 0:512], identb[:, :], masks[kc],
                     start=False, stop=True)
            p.phase()
            for kc in range(8):
                p.act(sexp[kc], pp[kc][0:128, 0:512], AF.Exp,
                      bias=nbcol[:, 0:1])
            p.phase()
            for kc in range(8):
                ti = 4 * w + kc
                p.mm(pp[0][0:64, 0:512], vt[ti][:, h * 64:(h + 1) * 64],
                     sexp[kc], start=(kc == 0), stop=(kc == 7))
            for kc in range(8):
                ti = 4 * w + kc
                p.mm(pp[1][0:1, 0:512], kvt[:, ti:ti + 1], sexp[kc],
                     start=(kc == 0), stop=(kc == 7))
            p.phase()
            p.ts(drow[0:1, 0:512], pp[1][0:1, 0:512], 1e-30, ALU.max)
            p.phase()
            p.act(drow2[0:1, 0:512], drow[0:1, 0:512], AF.Ln)
            p.phase()
            p.act(drow2[0:1, 0:512], drow2[0:1, 0:512], AF.Exp, scale=-1.0)
            p.phase()
            p.mm(pp[2][0:64, 0:512], onef[0:1, 0:64], drow2[0:1, 0:512])
            p.phase()
            p.cpy(recb[0:64, 0:512], pp[2][0:64, 0:512])
            if DBG and w == 1:
                for kc in range(8):
                    p.dma("gpsimd",
                          dbg_sexp[h * 1024 + kc * 128:h * 1024 + (kc + 1) * 128, :],
                          sexp[kc])
                p.dma("sync", dbg_den[2 * h:2 * h + 1, :], drow[0:1, 0:512])
                p.dma("sync", dbg_den[2 * h + 1:2 * h + 2, :], drow2[0:1, 0:512])
            p.act(av_sb[0:64, 0:512], pp[0][0:64, 0:512], AF.Copy)
            p.phase()
            p.tt(av2_sb[0:64, 0:512], av_sb[0:64, 0:512],
                 recb[0:64, 0:512], ALU.mult)
            p.phase()
            p.act(aow[pt][rb:rb + 64, 0:512], av2_sb[0:64, 0:512], AF.Copy)
            if DBG and w == 1 and h == 0:
                p.dma("sync", dbg_av2, av2_sb[0:64, 0:512])
            if DBG and w == 1:
                p.phase()
                p.dma("gpsimd", dbg_tr[h:h + 1, :], aow[0][0:1, 0:512])
        if (DBG or DBG2) and w == 1:
            p.phase()
            for i in range(4):
                p.dma("sync", dbg_ao[i * 128:(i + 1) * 128, :].bitcast(BF16) if False else dbg_bits[i * 128:(i + 1) * 128, :], aow[i].bitcast(F32))
        lo = QO + 512 * w
        p.phase()
        for m in range(DK):
            for kt in range(4):
                p.mm(pp[m][0:128, 0:512],
                     wob[:, kt * 1024 + m * 128:kt * 1024 + (m + 1) * 128],
                     aow[kt], start=(kt == 0), stop=(kt == 3))
            if not with_preln:
                p.dma("sync", x1buf[:, m * 512:(m + 1) * 512],
                      x1e[m * 128:(m + 1) * 128, lo:lo + 512])
        for m in range(DK):
            wb_ = wst[m % 2]
            p.phase()
            if with_preln:
                p.tt(wb_[:, 0:512], pp[m][0:128, 0:512],
                     xh[m][:, lo:lo + 512], ALU.add)
            else:
                p.tt(wb_[:, 0:512], pp[m][0:128, 0:512],
                     x1buf[:, m * 512:(m + 1) * 512], ALU.add)
            p.phase()
            if w == 0:
                p.dma("sync", u_out[m * 128:(m + 1) * 128, 0:1],
                      wb_[:, 511:512], slow=True)
            else:
                p.dma("sync", u_out[m * 128:(m + 1) * 128,
                                    512 * (w - 1) + 1:512 * w + 1],
                      wb_[:, 0:512])
    return p.finish()


def build_L0Fb(add_residual_x, chain_faq=False):
    TT2 = T + 1
    MC = (HIDF + 127) // 128
    p = Prog()
    ue = p.din("ue", [D, TT2])
    w1a_in = p.din("w1a", [D, HIDF])
    w1g_in = p.din("w1g", [D, HIDF])
    b1a_in = p.din("b1a", [HIDF, 1])
    b1g_in = p.din("b1g", [HIDF, 1])
    w2_in = p.din("w2", [HIDF, D])
    b2_in = p.din("b2", [D, 1])
    lg_in = p.din("lg", [D, 1])
    lbt_in = p.din("lbt", [D, 1])
    ones_c = p.din("ones_c", [1, 128])
    ones_col_in = p.din("ones_col", [128, 1])
    if add_residual_x:
        x1own = p.din("x1own", [D, T])
    v_out = p.dout("v", [D, T])

    pp = [p.psum(f"pp{i}", [128, 512]) for i in range(8)]
    A = Arena(p, 209536)
    bu = Bump(209536)

    ub = [A.bf16(bu.take(TT2 * 2), TT2) for _ in range(DK)]
    xn = [A.bf16(bu.take(T * 2), T) for _ in range(DK)]
    h2m = bu.mark()
    h2 = [A.bf16(bu.take(1024 * 2), 1024) for _ in range(MC)]
    xsq = [A.bf16(h2m + ((k * T * 2 + 63) // 64) * 64, T) for k in range(DK)]
    lb = NS(srow=A.f32(bu.take(2 * T * 4), 2 * T, rows=1),
            mu=A.f32(bu.take(T * 4), T, rows=1),
            ab=A.bf16(bu.take(2 * T * 2), 2 * T, rows=1),
            abc=A.bf16(bu.take(2 * T * 2), 2 * T))
    t2ln = A.bf16(bu.take(T * 2), T)
    wsa = [A.bf16(bu.take(DK * 128 * 2), DK * 128) for _ in range(2)]
    wsg = [A.bf16(bu.take(DK * 128 * 2), DK * 128) for _ in range(2)]
    ws2 = [A.bf16(bu.take(MC * 128 * 2), MC * 128) for _ in range(2)]
    b1acol = A.f32(bu.take(MC * 4), MC)
    b1gcol = A.f32(bu.take(MC * 4), MC)
    b2col = A.f32(bu.take(DK * 4), DK)
    lgcol = A.f32(bu.take(DK * 4), DK)
    lbcol = A.f32(bu.take(DK * 4), DK)
    onebf = A.bf16(bu.take(256), 128, rows=1)
    onecol = A.bf16(bu.take(4), 1)
    ares = [A.f32(bu.take(512 * 4), 512) for _ in range(2)]
    wst = [A.f32(bu.take(512 * 4), 512) for _ in range(2)]
    gsc = A.bf16(bu.take(1024 * 2), 1024)

    def ush(k):
        return ub[k][:, 1:TT2] if k < 4 else ub[k][:, 0:T]

    for k in range(DK):
        p.dma("gpsimd", ub[k], ue[k * 128:(k + 1) * 128, :])
    for k in range(DK):
        p.dma("sync", b2col[:, k:k + 1], b2_in[k * 128:(k + 1) * 128, 0:1])
        p.dma("sync", lgcol[:, k:k + 1], lg_in[k * 128:(k + 1) * 128, 0:1])
        p.dma("sync", lbcol[:, k:k + 1], lbt_in[k * 128:(k + 1) * 128, 0:1])
    for m in range(MC):
        rm = min(128, HIDF - m * 128)
        p.dma("sync", b1acol[0:rm, m:m + 1], b1a_in[m * 128:m * 128 + rm, 0:1])
        p.dma("sync", b1gcol[0:rm, m:m + 1], b1g_in[m * 128:m * 128 + rm, 0:1])
    p.dma("gpsimd", onebf[0:1, :], ones_c)
    p.dma("gpsimd", onecol[:, 0:1], ones_col_in)

    emit_layernorm(p, lb, T,
                   x_of=ush, xbf_of=ush, xsq_of=lambda k: xsq[k],
                   out_of=lambda k: xn[k],
                   pp=pp, ones_col=onecol, ones_row=onebf,
                   gamma_col=lgcol, beta_col=lbcol, t2=t2ln)

    ct = tchunks(T)
    for half in range(2):
        hc = [(o - 1024 * half, n, o) for (o, n) in ct[2 * half:2 * half + 2]]
        p.phase()
        for k in range(DK):
            p.dma("gpsimd", wsa[0][:, k * 128:(k + 1) * 128],
                  w1a_in[k * 128:(k + 1) * 128, 0:128])
            p.dma("gpsimd", wsg[0][:, k * 128:(k + 1) * 128],
                  w1g_in[k * 128:(k + 1) * 128, 0:128])
        for m in range(MC):
            rm = min(128, HIDF - m * 128)
            p.phase()
            if m + 1 < MC:
                rm2 = min(128, HIDF - (m + 1) * 128)
                for k in range(DK):
                    p.dma("gpsimd", wsa[(m + 1) % 2][:, k * 128:k * 128 + rm2],
                          w1a_in[k * 128:(k + 1) * 128,
                                 (m + 1) * 128:(m + 1) * 128 + rm2])
                    p.dma("gpsimd", wsg[(m + 1) % 2][:, k * 128:k * 128 + rm2],
                          w1g_in[k * 128:(k + 1) * 128,
                                 (m + 1) * 128:(m + 1) * 128 + rm2])
            for i, (lo, n, o) in enumerate(hc):
                for k in range(DK):
                    p.mm(pp[2 * i][0:rm, 0:n],
                         wsa[m % 2][:, k * 128:k * 128 + rm],
                         xn[k][:, o:o + n], start=(k == 0), stop=(k == DK - 1))
                for k in range(DK):
                    p.mm(pp[2 * i + 1][0:rm, 0:n],
                         wsg[m % 2][:, k * 128:k * 128 + rm],
                         xn[k][:, o:o + n], start=(k == 0), stop=(k == DK - 1))
            p.phase()
            for i, (lo, n, o) in enumerate(hc):
                p.act(gsc[0:rm, lo:lo + n], pp[2 * i + 1][0:rm, 0:n],
                      AF.Gelu, bias=b1gcol[0:rm, m:m + 1])
                p.ts(h2[m][0:rm, lo:lo + n], pp[2 * i][0:rm, 0:n],
                     b1acol[0:rm, m:m + 1], ALU.add)
            p.phase()
            p.tt(h2[m][0:rm, 0:1024], h2[m][0:rm, 0:1024], gsc[0:rm, 0:1024],
                 ALU.mult)
        p.phase()
        for kt in range(MC):
            rk = min(128, HIDF - kt * 128)
            p.dma("gpsimd", ws2[0][0:rk, kt * 128:kt * 128 + 128],
                  w2_in[kt * 128:kt * 128 + rk, 0:128])
        for m2 in range(DK):
            p.phase()
            if m2 + 1 < DK:
                for kt in range(MC):
                    rk = min(128, HIDF - kt * 128)
                    p.dma("gpsimd",
                          ws2[(m2 + 1) % 2][0:rk, kt * 128:kt * 128 + 128],
                          w2_in[kt * 128:kt * 128 + rk,
                                (m2 + 1) * 128:(m2 + 2) * 128])
            for i, (lo, n, o) in enumerate(hc):
                for kt in range(MC):
                    rk = min(128, HIDF - kt * 128)
                    p.mm(pp[i][0:128, 0:n],
                         ws2[m2 % 2][0:rk, kt * 128:kt * 128 + 128],
                         h2[kt][0:rk, lo:lo + n],
                         start=(kt == 0), stop=(kt == MC - 1))
                p.dma("sync", ares[i][:, 0:n],
                      ue[m2 * 128:(m2 + 1) * 128, 1 + o:1 + o + n])
            p.phase()
            for i, (lo, n, o) in enumerate(hc):
                p.ts(wst[i][:, 0:n], pp[i][0:128, 0:n], b2col[:, m2:m2 + 1],
                     ALU.add)
            p.phase()
            for i, (lo, n, o) in enumerate(hc):
                p.tt(wst[i][:, 0:n], wst[i][:, 0:n], ares[i][:, 0:n], ALU.add)
            if add_residual_x:
                p.phase()
                for i, (lo, n, o) in enumerate(hc):
                    p.dma("sync", ares[i][:, 0:n],
                          x1own[m2 * 128:(m2 + 1) * 128, o:o + n])
                p.phase()
                for i, (lo, n, o) in enumerate(hc):
                    p.tt(wst[i][:, 0:n], wst[i][:, 0:n], ares[i][:, 0:n],
                         ALU.add)
            p.phase()
            for i, (lo, n, o) in enumerate(hc):
                p.dma("sync", v_out[m2 * 128:(m2 + 1) * 128, o:o + n],
                      wst[i][:, 0:n])
    if chain_faq:
        build_FAq(1, p=p, A=A, pp=pp, x_src=v_out, pre="q1_")
    return p.finish()



def build_FAq(blk, p=None, A=None, pp=None, x_src=None, pre=""):
    has_bias = blk == 1
    has_g2 = blk == 2
    qsc = 1.0 if blk == 1 else 0.125
    standalone = p is None
    if standalone:
        p = Prog()
        x_in = p.din("x_in", [D, T])
    else:
        x_in = x_src
    wq_in = p.din(pre + "wq", [D, D])
    wk_in = p.din(pre + "wk", [D, D])
    wv_in = p.din(pre + "wv", [D, D])
    ones_c = p.din(pre + "ones_c", [1, 128])
    ones_col_in = p.din(pre + "ones_col", [128, 1])
    if has_bias:
        bqk_in = p.din(pre + "bqk", [2 * D, 1])
        bv_in = p.din(pre + "bv", [1, D])
    if has_g2:
        g2_in = p.din(pre + "g2", [D, 1])
        b2_in = p.din(pre + "b2", [D, 1])
    q_out = p.dout(pre + "q", [D, T], BF16)
    k_out = p.dout(pre + "k", [D, T], BF16)
    v_out = p.dout(pre + "v", [T, D], BF16)

    if standalone:
        pp = [p.psum(f"pp{i}", [128, 512]) for i in range(8)]
        A = Arena(p, 209536)
    else:
        p.phase()
    bu = Bump(209536)
    xb = [A.bf16(bu.take(T * 2), T) for _ in range(DK)]
    xn = [A.bf16(bu.take(T * 2), T) for _ in range(DK)]
    xsq = [A.bf16(bu.take(T * 2), T) for _ in range(DK)]
    lb = NS(srow=A.f32(bu.take(2 * T * 4), 2 * T, rows=1),
            mu=A.f32(bu.take(T * 4), T, rows=1),
            ab=A.bf16(bu.take(2 * T * 2), 2 * T, rows=1),
            abc=A.bf16(bu.take(2 * T * 2), 2 * T))
    t2ln = A.bf16(bu.take(T * 2), T)
    ws = [A.bf16(bu.take(DK * 128 * 2), DK * 128) for _ in range(2)]
    wvb = [A.bf16(bu.take(D * 2), D) for _ in range(DK)]
    qkbuf = A.bf16(bu.take(T * 2), T)
    vbt = [A.bf16(bu.take(512 * 2), 512) for _ in range(8)]
    bqkc = A.f32(bu.take(16 * 4), 16)
    bvrow = A.f32(bu.take(D * 4), D, rows=1)
    bvc = A.f32(bu.take(D * 4), D)
    g2c = A.f32(bu.take(DK * 4), DK)
    b2c = A.f32(bu.take(DK * 4), DK)
    onebf = A.bf16(bu.take(256), 128, rows=1)
    onef = A.f32(bu.take(512), 128, rows=1)
    onecol = A.bf16(bu.take(4), 1)

    for k in range(DK):
        p.dma("gpsimd", xb[k], x_in[k * 128:(k + 1) * 128, :])
    p.dma("gpsimd", onebf[0:1, :], ones_c)
    p.dma("sync", onef[0:1, :], ones_c)
    p.dma("gpsimd", onecol[:, 0:1], ones_col_in)
    if has_bias:
        for m in range(16):
            p.dma("sync", bqkc[:, m:m + 1], bqk_in[m * 128:(m + 1) * 128, 0:1])
        p.dma("sync", bvrow[0:1, :], bv_in)
        emit_row_bcast(p, [(bvrow[0:1, 0:512], bvc[:, 0:512], 512),
                           (bvrow[0:1, 512:1024], bvc[:, 512:1024], 512)],
                       pp, onef)
    if has_g2:
        load_col(p, g2c, g2_in)
        load_col(p, b2c, b2_in)

    emit_layernorm(p, lb, T,
                   x_of=lambda k: xb[k], xbf_of=lambda k: xb[k],
                   xsq_of=lambda k: xsq[k], out_of=lambda k: xn[k],
                   pp=pp, ones_col=onecol, ones_row=onebf, t2=t2ln)
    if has_g2:
        emit_layernorm(p, lb, T,
                       x_of=lambda k: xn[k], xbf_of=lambda k: xn[k],
                       xsq_of=lambda k: xsq[k], out_of=lambda k: xn[k],
                       pp=pp, ones_col=onecol, ones_row=onebf,
                       gamma_col=g2c, beta_col=b2c, t2=t2ln)

    ct = tchunks(T)
    for m in range(16):
        is_q = m < 8
        mm_ = m % 8
        w_in = wq_in if is_q else wk_in
        p.phase()
        for k in range(DK):
            p.dma("gpsimd", ws[m % 2][:, k * 128:(k + 1) * 128],
                  w_in[k * 128:(k + 1) * 128, mm_ * 128:(mm_ + 1) * 128])
        p.phase()
        for i, (o, n) in enumerate(ct):
            for k in range(DK):
                p.mm(pp[i][0:128, 0:n], ws[m % 2][:, k * 128:(k + 1) * 128],
                     xn[k][:, o:o + n], start=(k == 0), stop=(k == DK - 1))
        p.phase()
        for i, (o, n) in enumerate(ct):
            if has_bias:
                p.ts(qkbuf[:, o:o + n], pp[i][0:128, 0:n],
                     bqkc[:, m:m + 1], ALU.add)
            elif is_q:
                p.ts(qkbuf[:, o:o + n], pp[i][0:128, 0:n], qsc, ALU.mult)
            else:
                p.cpy(qkbuf[:, o:o + n], pp[i][0:128, 0:n])
        p.phase()
        dst = q_out if is_q else k_out
        p.dma("gpsimd", dst[mm_ * 128:(mm_ + 1) * 128, :], qkbuf[:, 0:T])

    p.phase()
    for k in range(DK):
        p.dma("gpsimd", wvb[k], wv_in[k * 128:(k + 1) * 128, :])
    for half in range(2):
        for grp in range(2):
            p.phase()
            for j in range(8):
                tt0 = (grp * 8 + j) * 128
                for k in range(DK):
                    p.mm(pp[j][0:128, 0:512], xn[k][:, tt0:tt0 + 128],
                         wvb[k][:, half * 512:(half + 1) * 512],
                         start=(k == 0), stop=(k == DK - 1))
            p.phase()
            for j in range(8):
                if has_bias:
                    p.tt(vbt[j][0:128, 0:512], pp[j][0:128, 0:512],
                         bvc[:, half * 512:(half + 1) * 512], ALU.add)
                else:
                    p.cpy(vbt[j][0:128, 0:512], pp[j][0:128, 0:512])
            p.phase()
            for j in range(8):
                tt0 = (grp * 8 + j) * 128
                p.dma("gpsimd", v_out[tt0:tt0 + 128,
                                      half * 512:(half + 1) * 512],
                      vbt[j][0:128, 0:512])
    if standalone:
        return p.finish()
    return None


def build_FAatt1():
    p = Prog()
    nc = p.nc
    qb_in = p.din("qb", [D, T], BF16)
    ko_in = p.din("ko", [D, SEQ], BF16)
    vtl_in = p.din("vtl", [16 * 128, 32 * 65], BF16)
    x3_in = p.din("x3o", [D, T])
    wp_in = p.din("wp", [D, D])
    bp_in = p.din("bp", [D, 1])
    temp_in = p.din("temp", [1, 1])
    r2c_in = p.din("r2c", [16, 2560])
    ones_c = p.din("ones_c", [1, 128])
    ones_col_in = p.din("ones_col", [128, 1])
    x4_out = p.dout("x4", [D, T])
    xn4_out = p.dout("xn4", [D, T], BF16)
    pmax_out = p.dout("pmax", [1, 1])
    r3d = nc.dram_tensor("r3d", [16, 2560], BF16, kind="Internal")
    scq = nc.dram_tensor("scq", [128, 1], F32, kind="Internal").ap()

    pp = [p.psum(f"pp{i}", [128, 512]) for i in range(8)]
    A = Arena(p, 209536)
    qb = [A.bf16(k * 4096, T) for k in range(DK)]
    Kb = [A.bf16(32768 + i * 8192, SEQ) for i in range(2)]
    Vb = [A.bf16(49152 + i * 8320, 2 * 32 * 65) for i in range(2)]
    sexp = [A.bf16(65792 + i * 1024, 512) for i in range(32)]
    ao = [A.bf16(98560 + k * 4096, T) for k in range(DK)]
    wpb = [A.bf16(131328 + k * 2048, DK * 128) for k in range(DK)]
    SC = 147712
    av_sb = A.f32(SC, 512)
    av2 = A.f32(SC + 2048, 512)
    recb = A.f32(SC + 4096, 512, rows=64)
    hk = [A.bf16(SC + 6144 + i * 1024, 512) for i in range(2)]
    dr2 = A.t[64:65, SC + 8192:SC + 8192 + 2048].bitcast(F32)
    onef64 = A.t[64:65, SC + 10240:SC + 10240 + 512].bitcast(F32)
    wst = [A.f32(SC + 10752, 512), A.f32(SC + 12800, 512)]
    xres = A.f32(SC + 14848, T)
    r2s = A.f32(170752, 2560, rows=16)
    r3f = A.f32(181056, 2560, rows=16)
    r3b = A.bf16(191296, 2560, rows=16)
    SM = 196416
    onebf = A.bf16(SM, 128, rows=1)
    onef = A.f32(SM + 256, 128, rows=1)
    tcell = A.f32(SM + 768, 4, rows=1)
    invccol = A.f32(SM + 784, 1)
    invc16 = A.f32(SM + 788, 1, rows=16)
    bpcol = A.f32(SM + 792, DK)
    pcol = A.f32(SM + 832, 2)
    brow = A.f32(SM + 840, 132, rows=1)
    onecol = A.bf16(SM + 1376, 1)
    pcol8 = A.f32(SM + 1440, 8)
    assert SM + 1440 + 32 <= 209536
    x4f = [A.f32(k * 8192, T) for k in range(DK)]
    xbf = [A.bf16(98560 + k * 4096, T) for k in range(DK)]
    xsq2 = [A.bf16(65792 + k * 4096, T) for k in range(DK)]
    LB = 131328
    lb = NS(srow=A.f32(LB, 2 * T, rows=1),
            mu=A.f32(LB + 16384, T, rows=1),
            ab=A.bf16(LB + 24576, 2 * T, rows=1),
            abc=A.bf16(LB + 32768, 2 * T))
    t2ln = A.bf16(172288, T)
    xnb = [A.bf16(k * 4096, T) for k in range(DK)]

    for k in range(DK):
        p.dma("sync", qb[k], qb_in[k * 128:(k + 1) * 128, :])
        p.dma("gpsimd", wpb[k], wp_in[k * 128:(k + 1) * 128, :])
    p.dma("sync", r2s[0:16, :], r2c_in)
    p.dma("sync", tcell[0:1, 0:1], temp_in)
    p.dma("gpsimd", onebf[0:1, :], ones_c)
    p.dma("sync", onef[0:1, :], ones_c)
    p.dma("sync", onef64[0:1, 0:64], ones_c[0:1, 0:64])
    p.dma("gpsimd", onecol[:, 0:1], ones_col_in)
    load_col(p, bpcol, bp_in)
    p.phase()
    p.ts(tcell[0:1, 0:1], tcell[0:1, 0:1], 0.1, ALU.max)
    p.phase()
    p.ts(tcell[0:1, 0:1], tcell[0:1, 0:1], 10.0, ALU.min)
    p.phase()
    p.ts(tcell[0:1, 0:1], tcell[0:1, 0:1], 32.0, ALU.mult)
    p.phase()
    p.act(tcell[0:1, 1:2], tcell[0:1, 0:1], AF.Ln)
    p.phase()
    p.act(tcell[0:1, 2:3], tcell[0:1, 1:2], AF.Exp, scale=-1.0)
    p.phase()
    p.mm(pp[0][0:128, 0:1], onef[0:1, 0:128], tcell[0:1, 2:3])
    p.mm(pp[1][0:16, 0:1], onef[0:1, 0:16], tcell[0:1, 2:3])
    p.phase()
    p.cpy(invccol[:, 0:1], pp[0][0:128, 0:1])
    p.cpy(invc16[0:16, 0:1], pp[1][0:16, 0:1])
    p.phase()
    for k in range(DK):
        p.ts(qb[k], qb[k], invccol[:, 0:1], ALU.mult,
             eng=("vector" if k % 2 == 0 else "gpsimd"))
    p.phase()
    p.ts(r3f[0:16, :], r2s[0:16, :], r2s[0:16, 2558:2559], ALU.subtract)
    p.phase()
    p.ts(r3f[0:16, :], r3f[0:16, :], invc16[0:16, 0:1], ALU.mult)
    p.phase()
    p.act(r3b[0:16, :], r3f[0:16, :], AF.Exp)
    p.phase()
    p.dma("sync", r3d.ap(), r3b[0:16, :])

    p.phase()
    p.dma("sync", Kb[0], ko_in[0:128, :])
    p.dma("sync", Vb[0][:, 0:2080], vtl_in[0:128, :])
    p.dma("sync", Vb[0][:, 2080:4160], vtl_in[128:256, :])
    for m in range(DK):
        first = True
        for hl in range(2):
            rb = hl * 64
            for tqc in range(4):
                i0 = tqc * 512
                for g in range(4):
                    p.phase()
                    if first and m + 1 < DK:
                        nm = m + 1
                        p.dma("sync", Kb[nm % 2],
                              ko_in[nm * 128:(nm + 1) * 128, :])
                        p.dma("sync", Vb[nm % 2][:, 0:2080],
                              vtl_in[2 * nm * 128:(2 * nm + 1) * 128, :])
                        p.dma("sync", Vb[nm % 2][:, 2080:4160],
                              vtl_in[(2 * nm + 1) * 128:(2 * nm + 2) * 128, :])
                    first = False
                    for j in range(8):
                        kt = g * 8 + j
                        p.mm(pp[j][0:128, 0:512],
                             Kb[m % 2][rb:rb + 64, kt * 128:(kt + 1) * 128],
                             qb[m][rb:rb + 64, i0:i0 + 512])
                    p.phase()
                    for j in range(8):
                        kt = g * 8 + j
                        p.act(sexp[kt], pp[j][0:128, 0:512], AF.Exp)
                kts = [kt for kt in range(32) if kt * 128 - i0 >= 2048]
                for vi in range(len(kts) + 1):
                    p.phase()
                    if vi < len(kts):
                        kt = kts[vi]
                        V0 = i0 - kt * 128 + 3968
                        h = 2 * m + hl
                        p.dma("sync", hk[vi % 2],
                              bass.AP(r3d, h * 2560 + V0, [[1, 128], [1, 512]]))
                    if vi > 0:
                        kp = kts[vi - 1]
                        p.tt(sexp[kp], sexp[kp], hk[(vi - 1) % 2], ALU.mult)
                p.phase()
                for kt in range(32):
                    p.mm(pp[0][0:65, 0:512],
                         Vb[m % 2][:, hl * 2080 + kt * 65:hl * 2080 + (kt + 1) * 65],
                         sexp[kt], start=(kt == 0), stop=(kt == 31))
                p.phase()
                p.cpy(av_sb[0:65, :], pp[0][0:65, 0:512])
                p.phase()
                p.ts(av_sb[64:65, :], av_sb[64:65, :], 1e-30, ALU.max)
                p.phase()
                p.act(av_sb[64:65, :], av_sb[64:65, :], AF.Ln)
                p.phase()
                p.act(dr2[0:1, 0:512], av_sb[64:65, :], AF.Exp, scale=-1.0)
                p.phase()
                p.mm(pp[2][0:64, 0:512], onef64[0:1, 0:64], dr2[0:1, 0:512])
                p.phase()
                p.cpy(recb[0:64, :], pp[2][0:64, 0:512])
                p.phase()
                p.tt(av2[0:64, :], av_sb[0:64, :], recb[0:64, :], ALU.mult)
                p.phase()
                p.act(ao[m][rb:rb + 64, i0:i0 + 512], av2[0:64, :], AF.Copy)

    for mo in range(DK):
        p.phase()
        p.dma("sync", xres[:, :], x3_in[mo * 128:(mo + 1) * 128, :])
        for i, (o, n) in enumerate(tchunks(T)):
            for k in range(DK):
                p.mm(pp[i][0:128, 0:n],
                     wpb[k][:, mo * 128:(mo + 1) * 128],
                     ao[k][:, o:o + n], start=(k == 0), stop=(k == DK - 1))
        p.phase()
        for i, (o, n) in enumerate(tchunks(T)):
            p.ts(x4f[mo][:, o:o + n], pp[i][0:128, 0:n],
                 bpcol[:, mo:mo + 1], ALU.add)
        p.phase()
        p.tt(x4f[mo], x4f[mo], xres[:, :], ALU.add)
        p.phase()
        p.dma("sync", x4_out[mo * 128:(mo + 1) * 128, :], x4f[mo])

    emit_layernorm(p, lb, T,
                   x_of=lambda k: x4f[k], xbf_of=lambda k: xbf[k],
                   xsq_of=lambda k: xsq2[k], out_of=lambda k: xnb[k],
                   pp=pp, ones_col=onecol,
                   ones_row=onebf, x_is_fp32=True, t2=t2ln)
    p.phase()
    for k in range(DK):
        p.dma("gpsimd", xn4_out[k * 128:(k + 1) * 128, :], xnb[k])
        p.red(pcol8[:, k:k + 1], xnb[k], ALU.max, absval=True)
    p.phase()
    p.red(pcol[:, 0:1], pcol8[:, 0:8], ALU.max)
    p.phase()
    p.dma("sync", scq, pcol[:, 0:1])
    p.phase()
    p.dma("sync", brow[0:1, 0:128], scq.rearrange("p o -> o p"))
    p.phase()
    p.red(brow[0:1, 128:129], brow[0:1, 0:128], ALU.max)
    p.phase()
    p.dma("sync", pmax_out, brow[0:1, 128:129])
    return p.finish()


def stage_L01(inputs, core):
    b, h = core // 2, core % 2
    x = inputs["L0_attn_x"]
    s0 = h * T
    xe = np.zeros((TE, D), np.float32)
    if h == 0:
        xe[HALO:] = x[b, 0:T]
    else:
        xe[:] = x[b, s0 - HALO:s0 + T]
    sel = np.zeros((NST, NST * 128), np.float32)
    for n in range(NST):
        sel[n, n * 128:(n + 1) * 128] = 1.0
    return {
        "x0e": np.ascontiguousarray(xe.T),
        "a_log": inputs["L0_attn_A_log"],
        "bwT": np.ascontiguousarray(inputs["L0_attn_B_w"].T),
        "cwT": np.ascontiguousarray(inputs["L0_attn_C_w"].T),
        "d_vec": inputs["L0_attn_D"].reshape(D, 1),
        "gwT": np.ascontiguousarray(inputs["L0_attn_gate_w"].T),
        "gate_b": inputs["L0_attn_gate_b"].reshape(D, 1),
        "ln_g": inputs["L0_attn_ln_gamma"].reshape(D, 1),
        "ln_b": inputs["L0_attn_ln_beta"].reshape(D, 1),
        "ones_c": np.ones((1, 128), np.float32),
        "ones_col": np.ones((128, 1), np.float32),
        "sel16": sel,
    }


def _rope_tables(G0):
    j = np.arange(32)
    inv_freq = 1.0 / 10000.0 ** (2 * j / 64.0)
    base = (2 * j + 0.4 * 64) / (1.4 * 64)
    sign = np.concatenate([-np.ones(32), np.ones(32)])
    freq64 = inv_freq[np.arange(64) % 32]
    base64 = base[np.arange(64) % 32]
    sign128 = np.tile(sign, 2)

    def tables(t, qk_scale, inv_scale):
        f = freq64[:, None] * t[None, :]
        power = (t - SEQ // 2) / 256.0
        sc = base64[:, None] ** power[None, :]
        if inv_scale:
            sc = 1.0 / sc
        c1 = np.cos(f) * sc
        c2 = np.sin(f) * sc
        c1 = np.tile(c1, (2, 1)) * qk_scale
        c2 = np.tile(c2, (2, 1)) * qk_scale * sign128[:, None]
        return c1.astype(np.float32), c2.astype(np.float32)

    tq = G0 - FH + QO + np.arange(NQ)
    tk = G0 - FH + KO + np.arange(NK)
    c1q, c2q = tables(tq, 8.0, False)
    c1k, c2k = tables(tk, 1.0, True)
    return c1q, c2q, c1k, c2k


_ROT_PERM = None


def _rot_perm():
    global _ROT_PERM
    if _ROT_PERM is None:
        idx = np.arange(INNER)
        _ROT_PERM = (idx // 64) * 64 + ((idx % 64) + 32) % 64
    return _ROT_PERM


def stage_L0Fa(inputs, core, l, xfull):
    b, h = core // 2, core % 2
    G0 = h * T
    xe = np.zeros((TL, D), np.float32)
    if h == 0:
        xe[FH:] = xfull[b, 0:T]
    else:
        xe[:] = xfull[b, G0 - FH:G0 + T]
    wqkv = np.ascontiguousarray(inputs["L0_ffn_wqkv"][l])
    perm = _rot_perm()
    c1q, c2q, c1k, c2k = _rope_tables(G0)
    jj = np.arange(1024)[:, None]
    rr = np.arange(512)[None, :]
    amask = np.where((jj >= rr) & (jj <= rr + 512), 0.0, NEG_BIG
                     ).astype(np.float32)
    kvt = np.zeros((128, 24), np.float32)
    pidx = np.arange(128)[:, None]
    tidx = np.arange(24)[None, :]
    gk = G0 - FH + KO + tidx * 128 + pidx
    kvt[:] = (gk >= 0).astype(np.float32)
    e2 = np.zeros((128, 2), np.float32)
    e2[0:64, 0] = 1.0
    e2[64:128, 1] = 1.0
    e2t = np.ascontiguousarray(e2.T)
    return {
        "x1e": np.ascontiguousarray(xe.T),
        "wqkv": wqkv,
        "wqrot": np.ascontiguousarray(wqkv[:, 0:INNER][:, perm]),
        "wkrot": np.ascontiguousarray(wqkv[:, INNER:2 * INNER][:, perm]),
        "wo_in": np.ascontiguousarray(inputs["L0_ffn_wo"][l]),
        "qs2": np.tile(inputs["L0_ffn_q_scale"][l], 2).reshape(128, 1),
        "ks2": np.tile(inputs["L0_ffn_k_scale"][l], 2).reshape(128, 1),
        "qs2p": np.tile(inputs["L0_ffn_q_scale"][l][(np.arange(64) + 32) % 64],
                        2).reshape(128, 1),
        "ks2p": np.tile(inputs["L0_ffn_k_scale"][l][(np.arange(64) + 32) % 64],
                        2).reshape(128, 1),
        "c1q": c1q, "c2q": c2q, "c1k": c1k, "c2k": c2k,
        "amask": amask,
        "ident": np.eye(128, dtype=np.float32),
        "kvt": kvt,
        "ones_c": np.ones((1, 128), np.float32),
        "ones_col": np.ones((128, 1), np.float32),
        "e2": e2, "e2t": e2t,
    }


def stage_L0Fb(inputs, core, l, u_core, xfull=None):
    w1 = inputs["L0_ffn_w1"][l]
    b1 = inputs["L0_ffn_b1"][l]
    d = {
        "ue": u_core,
        "w1a": np.ascontiguousarray(w1[:, 0:HIDF]),
        "w1g": np.ascontiguousarray(w1[:, HIDF:]),
        "b1a": np.ascontiguousarray(b1[0:HIDF]).reshape(HIDF, 1),
        "b1g": np.ascontiguousarray(b1[HIDF:]).reshape(HIDF, 1),
        "w2": np.ascontiguousarray(inputs["L0_ffn_w2"][l]),
        "b2": inputs["L0_ffn_b2"][l].reshape(D, 1),
        "lg": inputs["L0_ffn_ln_g"][l].reshape(D, 1),
        "lbt": inputs["L0_ffn_ln_b"][l].reshape(D, 1),
        "ones_c": np.ones((1, 128), np.float32),
        "ones_col": np.ones((128, 1), np.float32),
    }
    if xfull is not None:
        b, h = core // 2, core % 2
        d["x1own"] = np.ascontiguousarray(xfull[b, h * T:(h + 1) * T].T)
    return d


QMAX = 32767.0
HID3 = 3072
MC3 = 24


def emit_scale_from_partials(p, prow, n, scol, rcol, pp, onef, eps=1e-12,
                             qdiv=QMAX):
    p.phase()
    p.red(prow[0:1, n:n + 1], prow[0:1, 0:n], ALU.max)
    p.phase()
    p.ts(prow[0:1, n:n + 1], prow[0:1, n:n + 1], 1.0 / qdiv, ALU.mult)
    p.phase()
    p.ts(prow[0:1, n:n + 1], prow[0:1, n:n + 1], eps, ALU.max)
    p.phase()
    p.act(prow[0:1, n + 1:n + 2], prow[0:1, n:n + 1], AF.Ln)
    p.phase()
    p.act(prow[0:1, n + 2:n + 3], prow[0:1, n + 1:n + 2], AF.Exp, scale=-1.0)
    p.phase()
    p.mm(pp[0][0:128, 0:1], onef[0:1, 0:128], prow[0:1, n:n + 1])
    p.mm(pp[1][0:128, 0:1], onef[0:1, 0:128], prow[0:1, n + 2:n + 3])
    p.phase()
    p.cpy(scol[:, 0:1], pp[0][0:128, 0:1])
    p.cpy(rcol[:, 0:1], pp[1][0:128, 0:1])


def emit_qdq(p, tf, ti, src, scol, lo, hi, dst=None, rcol=None):
    p.phase()
    p.ts(tf, src, rcol[:, 0:1], ALU.mult)
    p.phase()
    p.cpy(ti, tf)
    p.phase()
    p.cpy(tf, ti)
    p.phase()
    p.ts(tf, tf, lo, ALU.max)
    p.phase()
    p.ts(tf, tf, hi, ALU.min)
    p.phase()
    out = dst if dst is not None else tf
    p.ts(out, tf, scol[:, 0:1], ALU.mult)


def emit_absmax_acc(p, acc_col, tmp_col, src, first):
    p.red(tmp_col, src, ALU.max, absval=True)
    p.phase()
    if first:
        p.cpy(acc_col, tmp_col)
    else:
        p.tt(acc_col, acc_col, tmp_col, ALU.max)


def emit_pmax_out(p, acc_col, brow, scq, out):
    p.phase()
    p.dma("sync", scq, acc_col)
    p.phase()
    p.dma("sync", brow[0:1, 0:128], scq.rearrange("p o -> o p"))
    p.phase()
    p.red(brow[0:1, 128:129], brow[0:1, 0:128], ALU.max)
    p.phase()
    p.dma("sync", out, brow[0:1, 128:129])


def emit_lpbq_chunk(p, wnat, sm, rm, tf, ti, wqb, ptb, identb, nblk):
    C = 32 * nblk
    w3 = wnat[0:128, 0:C].rearrange("p (a b) -> p a b", b=32)
    t3 = tf[0:128, 0:C].rearrange("p (a b) -> p a b", b=32)
    smb = sm[:, 0:nblk].unsqueeze(2).to_broadcast((128, nblk, 32))
    rmb = rm[:, 0:nblk].unsqueeze(2).to_broadcast((128, nblk, 32))
    p.phase()
    p.red(sm[:, 0:nblk], w3, ALU.max, absval=True)
    p.phase()
    p.ts(sm[:, 0:nblk], sm[:, 0:nblk], 1.0 / 7.0, ALU.mult)
    p.phase()
    p.ts(sm[:, 0:nblk], sm[:, 0:nblk], 1e-12, ALU.max)
    p.phase()
    p.act(rm[:, 0:nblk], sm[:, 0:nblk], AF.Ln)
    p.phase()
    p.act(rm[:, 0:nblk], rm[:, 0:nblk], AF.Exp, scale=-1.0)
    p.phase()
    p.tt(t3, w3, rmb, ALU.mult)
    p.phase()
    p.cpy(ti[:, 0:C], tf[:, 0:C])
    p.phase()
    p.cpy(tf[:, 0:C], ti[:, 0:C])
    p.phase()
    p.ts(tf[:, 0:C], tf[:, 0:C], -8.0, ALU.max)
    p.phase()
    p.ts(tf[:, 0:C], tf[:, 0:C], 7.0, ALU.min)
    p.phase()
    p.tt(t3, t3, smb, ALU.mult)
    p.phase()
    p.dma("gpsimd", wqb[:, 0:C], tf[:, 0:C])


def build_L1FB():
    p = Prog()
    xn_in = p.din("xn", [D, T], BF16)
    pmax8_in = p.din("pmax8", [1, 8])
    wup_in = p.din("wup", [HID3, D])
    wg_in = p.din("wg", [HID3, D])
    ident_in = p.din("ident", [128, 128])
    ones_c = p.din("ones_c", [1, 128])
    up_out = p.dout("up_raw", [HID3, T])
    g_out = p.dout("gate_raw", [HID3, T])
    pmu_out = p.dout("pmu", [1, 1])
    pmg_out = p.dout("pmg", [1, 1])
    scq = p.nc.dram_tensor("scq", [128, 1], F32, kind="Internal").ap()

    pp = [p.psum(f"pp{i}", [128, 512]) for i in range(6)]
    ptb = [p.psum(f"ptb{i}", [128, 512], BF16) for i in range(2)]
    A = Arena(p, 209536)
    bu = Bump(209536)
    xqb = [A.bf16(bu.take(T * 2), T) for _ in range(DK)]
    wupT = [A.bf16(bu.take(HID3 * 2), HID3) for _ in range(DK)]
    wgT = [A.bf16(bu.take(HID3 * 2), HID3) for _ in range(DK)]
    wnat = [A.f32(bu.take(D * 4), D) for _ in range(2)]
    tf = A.f32(bu.take(T * 4), T)
    ti = A.i32(bu.take(T * 4), T)
    wqb = A.bf16(bu.take(D * 2), D)
    sm = A.f32(bu.take(32 * 4), 32)
    rm = A.f32(bu.take(32 * 4), 32)
    of = [A.f32(bu.take(T * 4), T) for _ in range(2)]
    identb = A.bf16(bu.take(256), 128)
    onef = A.f32(bu.take(512), 128, rows=1)
    prow = A.f32(bu.take(16 * 4), 16, rows=1)
    scol = A.f32(bu.take(4), 1)
    rcol = A.f32(bu.take(4), 1)
    ucol = A.f32(bu.take(8), 2)
    gcol = A.f32(bu.take(8), 2)
    brow = A.f32(bu.take(132 * 4), 132, rows=1)

    for k in range(DK):
        p.dma("sync", xqb[k], xn_in[k * 128:(k + 1) * 128, :])
    p.dma("sync", prow[0:1, 0:8], pmax8_in)
    p.dma("sync", onef[0:1, :], ones_c)
    p.dma("gpsimd", identb[:, :], ident_in)
    emit_scale_from_partials(p, prow, 8, scol, rcol, pp, onef)
    for k in range(DK):
        emit_qdq(p, tf[:, 0:T], ti[:, 0:T], xqb[k], scol,
                 -(QMAX + 1), QMAX, rcol=rcol)
        p.phase()
        p.dma("gpsimd", xqb[k], tf[:, 0:T])
    for wi, (w_in, wT) in enumerate(((wup_in, wupT), (wg_in, wgT))):
        p.phase()
        p.dma("sync", wnat[0][:, :], w_in[0:128, :])
        for mc in range(MC3):
            if mc + 1 < MC3:
                p.phase()
                p.dma("sync", wnat[(mc + 1) % 2][:, :],
                      w_in[(mc + 1) * 128:(mc + 2) * 128, :])
            emit_lpbq_chunk(p, wnat[mc % 2], sm, rm, tf, ti, wqb, ptb, identb, 32)
            p.phase()
            for j in range(DK):
                p.transp(ptb[j // 4][:, (j % 4) * 128:(j % 4 + 1) * 128],
                         wqb[:, j * 128:(j + 1) * 128], identb[:, :])
            p.phase()
            for j in range(DK):
                p.cpy(wT[j][:, mc * 128:(mc + 1) * 128],
                      ptb[j // 4][:, (j % 4) * 128:(j % 4 + 1) * 128])
    ct = tchunks(T)
    for wi, (wT, out_d, acc) in enumerate(((wupT, up_out, ucol),
                                           (wgT, g_out, gcol))):
        for mo in range(MC3):
            p.phase()
            for i, (o, n) in enumerate(ct):
                for k in range(DK):
                    p.mm(pp[i][0:128, 0:n], wT[k][:, mo * 128:(mo + 1) * 128],
                         xqb[k][:, o:o + n], start=(k == 0), stop=(k == DK - 1))
            p.phase()
            ob = of[mo % 2]
            for i, (o, n) in enumerate(ct):
                p.cpy(ob[:, o:o + n], pp[i][0:128, 0:n])
            p.phase()
            emit_absmax_acc(p, acc[:, 0:1], acc[:, 1:2], ob[:, 0:T], mo == 0)
            p.phase()
            p.dma("sync", out_d[mo * 128:(mo + 1) * 128, :], ob[:, 0:T])
    emit_pmax_out(p, ucol[:, 0:1], brow, scq, pmu_out)
    emit_pmax_out(p, gcol[:, 0:1], brow, scq, pmg_out)
    return p.finish()


def build_L1FC():
    p = Prog()
    g_in = p.din("gate_raw", [HID3, T])
    pmg8_in = p.din("pmg8", [1, 8])
    ones_c = p.din("ones_c", [1, 128])
    g2_out = p.dout("gate2", [HID3, T])
    pm2_out = p.dout("pm2", [1, 1])
    scq = p.nc.dram_tensor("scq", [128, 1], F32, kind="Internal").ap()

    pp = [p.psum(f"pp{i}", [128, 512]) for i in range(2)]
    A = Arena(p, 209536)
    bu = Bump(209536)
    gr = [A.f32(bu.take(T * 4), T) for _ in range(2)]
    tf = A.f32(bu.take(T * 4), T)
    ti = A.i32(bu.take(T * 4), T)
    gq = A.f32(bu.take(T * 4), T)
    sg = A.f32(bu.take(T * 4), T)
    g2 = [A.f32(bu.take(T * 4), T) for _ in range(2)]
    onef = A.f32(bu.take(512), 128, rows=1)
    prow = A.f32(bu.take(16 * 4), 16, rows=1)
    scol = A.f32(bu.take(4), 1)
    rcol = A.f32(bu.take(4), 1)
    acc = A.f32(bu.take(8), 2)
    brow = A.f32(bu.take(132 * 4), 132, rows=1)

    p.dma("sync", prow[0:1, 0:8], pmg8_in)
    p.dma("sync", onef[0:1, :], ones_c)
    p.dma("sync", gr[0][:, :], g_in[0:128, :])
    emit_scale_from_partials(p, prow, 8, scol, rcol, pp, onef)
    for mc in range(MC3):
        if mc + 1 < MC3:
            p.phase()
            p.dma("sync", gr[(mc + 1) % 2][:, :],
                  g_in[(mc + 1) * 128:(mc + 2) * 128, :])
        emit_qdq(p, tf[:, 0:T], ti[:, 0:T], gr[mc % 2][:, 0:T], scol,
                 -(QMAX + 1), QMAX, dst=gq[:, 0:T], rcol=rcol)
        p.phase()
        p.act(sg[:, 0:T], gq[:, 0:T], AF.Sigmoid)
        p.phase()
        p.ts(tf[:, 0:T], sg[:, 0:T], 65536.0, ALU.mult)
        p.phase()
        p.cpy(ti[:, 0:T], tf[:, 0:T])
        p.phase()
        p.cpy(tf[:, 0:T], ti[:, 0:T])
        p.phase()
        p.ts(tf[:, 0:T], tf[:, 0:T], 0.0, ALU.max)
        p.phase()
        p.ts(tf[:, 0:T], tf[:, 0:T], 65535.0, ALU.min)
        p.phase()
        p.ts(tf[:, 0:T], tf[:, 0:T], 1.0 / 65536.0, ALU.mult)
        p.phase()
        gb = g2[mc % 2]
        p.tt(gb[:, 0:T], gq[:, 0:T], tf[:, 0:T], ALU.mult)
        p.phase()
        emit_absmax_acc(p, acc[:, 0:1], acc[:, 1:2], gb[:, 0:T], mc == 0)
        p.phase()
        p.dma("sync", g2_out[mc * 128:(mc + 1) * 128, :], gb[:, 0:T])
    emit_pmax_out(p, acc[:, 0:1], brow, scq, pm2_out)
    return p.finish()


def build_L1FD():
    p = Prog()
    u_in = p.din("up_raw", [HID3, T])
    g2_in = p.din("gate2", [HID3, T])
    pmu8_in = p.din("pmu8", [1, 8])
    pm28_in = p.din("pm28", [1, 8])
    ones_c = p.din("ones_c", [1, 128])
    o_out = p.dout("o_raw", [HID3, T])
    pmo_out = p.dout("pmo", [1, 1])
    scq = p.nc.dram_tensor("scq", [128, 1], F32, kind="Internal").ap()

    pp = [p.psum(f"pp{i}", [128, 512]) for i in range(2)]
    A = Arena(p, 209536)
    bu = Bump(209536)
    ur = [A.f32(bu.take(T * 4), T) for _ in range(2)]
    g2r = [A.f32(bu.take(T * 4), T) for _ in range(2)]
    tf = A.f32(bu.take(T * 4), T)
    ti = A.i32(bu.take(T * 4), T)
    uq = A.f32(bu.take(T * 4), T)
    ob = [A.f32(bu.take(T * 4), T) for _ in range(2)]
    onef = A.f32(bu.take(512), 128, rows=1)
    prow = A.f32(bu.take(16 * 4), 16, rows=1)
    prow2 = A.f32(bu.take(16 * 4), 16, rows=1)
    scol = A.f32(bu.take(4), 1)
    rcol = A.f32(bu.take(4), 1)
    scol2 = A.f32(bu.take(4), 1)
    rcol2 = A.f32(bu.take(4), 1)
    acc = A.f32(bu.take(8), 2)
    brow = A.f32(bu.take(132 * 4), 132, rows=1)

    p.dma("sync", prow[0:1, 0:8], pmu8_in)
    p.dma("sync", prow2[0:1, 0:8], pm28_in)
    p.dma("sync", onef[0:1, :], ones_c)
    p.dma("sync", ur[0][:, :], u_in[0:128, :])
    p.dma("sync", g2r[0][:, :], g2_in[0:128, :])
    emit_scale_from_partials(p, prow, 8, scol, rcol, pp, onef)
    emit_scale_from_partials(p, prow2, 8, scol2, rcol2, pp, onef)
    for mc in range(MC3):
        if mc + 1 < MC3:
            p.phase()
            p.dma("sync", ur[(mc + 1) % 2][:, :],
                  u_in[(mc + 1) * 128:(mc + 2) * 128, :])
            p.dma("sync", g2r[(mc + 1) % 2][:, :],
                  g2_in[(mc + 1) * 128:(mc + 2) * 128, :])
        emit_qdq(p, tf[:, 0:T], ti[:, 0:T], ur[mc % 2][:, 0:T], scol,
                 -(QMAX + 1), QMAX, dst=uq[:, 0:T], rcol=rcol)
        emit_qdq(p, tf[:, 0:T], ti[:, 0:T], g2r[mc % 2][:, 0:T], scol2,
                 -(QMAX + 1), QMAX, rcol=rcol2)
        p.phase()
        o2 = ob[mc % 2]
        p.tt(o2[:, 0:T], tf[:, 0:T], uq[:, 0:T], ALU.mult)
        p.phase()
        emit_absmax_acc(p, acc[:, 0:1], acc[:, 1:2], o2[:, 0:T], mc == 0)
        p.phase()
        p.dma("sync", o_out[mc * 128:(mc + 1) * 128, :], o2[:, 0:T])
    emit_pmax_out(p, acc[:, 0:1], brow, scq, pmo_out)
    return p.finish()


def build_L1FE():
    p = Prog()
    o_in = p.din("o_raw", [HID3, T])
    pmo8_in = p.din("pmo8", [1, 8])
    wd_in = p.din("wd", [D, HID3])
    x4_in = p.din("x4o", [D, T])
    ident_in = p.din("ident", [128, 128])
    ones_c = p.din("ones_c", [1, 128])
    x5_out = p.dout("x5", [D, T])

    pp = [p.psum(f"pp{i}", [128, 512]) for i in range(6)]
    ptb = [p.psum(f"ptb{i}", [128, 512], BF16) for i in range(2)]
    A = Arena(p, 209536)
    bu = Bump(209536)
    oqb = [A.bf16(bu.take(T * 2), T) for _ in range(MC3)]
    wdT = [A.bf16(bu.take(D * 2), D) for _ in range(MC3)]
    wnat = [A.f32(bu.take(HID3 * 4), HID3)]
    tf = A.f32(bu.take(HID3 * 4), HID3)
    ti = A.i32(bu.take(HID3 * 4), HID3)
    wqb = A.bf16(bu.take(HID3 * 2), HID3)
    sm = A.f32(bu.take(96 * 4), 96)
    rm = A.f32(bu.take(96 * 4), 96)
    xres = A.f32(bu.take(T * 4), T)
    wst = [A.f32(bu.take(512 * 4), 512) for _ in range(4)]
    identb = A.bf16(bu.take(256), 128)
    onef = A.f32(bu.take(512), 128, rows=1)
    prow = A.f32(bu.take(16 * 4), 16, rows=1)
    scol = A.f32(bu.take(4), 1)
    rcol = A.f32(bu.take(4), 1)

    p.dma("sync", prow[0:1, 0:8], pmo8_in)
    p.dma("sync", onef[0:1, :], ones_c)
    p.dma("gpsimd", identb[:, :], ident_in)
    emit_scale_from_partials(p, prow, 8, scol, rcol, pp, onef)
    for mc in range(MC3):
        p.phase()
        p.dma("sync", wnat[0][:, 0:T], o_in[mc * 128:(mc + 1) * 128, :])
        emit_qdq(p, tf[:, 0:T], ti[:, 0:T], wnat[0][:, 0:T], scol,
                 -(QMAX + 1), QMAX, rcol=rcol)
        p.phase()
        p.dma("gpsimd", oqb[mc], tf[:, 0:T])
    for m in range(DK):
        p.phase()
        p.dma("sync", wnat[0][:, :], wd_in[m * 128:(m + 1) * 128, :])
        emit_lpbq_chunk(p, wnat[0], sm, rm, tf, ti, wqb, ptb, identb, 96)
        for jg in range(3):
            p.phase()
            for j in range(8):
                kt = jg * 8 + j
                p.transp(ptb[j // 4][:, (j % 4) * 128:(j % 4 + 1) * 128],
                         wqb[:, kt * 128:(kt + 1) * 128], identb[:, :])
            p.phase()
            for j in range(8):
                kt = jg * 8 + j
                p.cpy(wdT[kt][:, m * 128:(m + 1) * 128],
                      ptb[j // 4][:, (j % 4) * 128:(j % 4 + 1) * 128])
    ct = tchunks(T)
    for mo in range(DK):
        p.phase()
        p.dma("sync", xres[:, :], x4_in[mo * 128:(mo + 1) * 128, :])
        for i, (o, n) in enumerate(ct):
            for kt in range(MC3):
                p.mm(pp[i][0:128, 0:n], wdT[kt][:, mo * 128:(mo + 1) * 128],
                     oqb[kt][:, o:o + n], start=(kt == 0), stop=(kt == MC3 - 1))
        p.phase()
        for i, (o, n) in enumerate(ct):
            p.cpy(wst[i][:, 0:n], pp[i][0:128, 0:n])
        p.phase()
        for i, (o, n) in enumerate(ct):
            p.tt(wst[i][:, 0:n], wst[i][:, 0:n], xres[:, o:o + n], ALU.add)
        p.phase()
        for i, (o, n) in enumerate(ct):
            p.dma("sync", x5_out[mo * 128:(mo + 1) * 128, o:o + n],
                  wst[i][:, 0:n])
    return p.finish()



def build_FAatt2():
    p = Prog()
    qh_in = p.din("qh", [512, SEQ], BF16)
    kh_in = p.din("kh", [512, SEQ], BF16)
    vtl_in = p.din("vtl", [8 * 128, 32 * 65], BF16)
    slc_in = p.din("slc", [128, 8])
    slr_in = p.din("slr", [1, 8])
    drow_in = p.din("drow", [128, 512])
    i2_in = p.din("i2", [128, 32])
    ir_in = p.din("ir", [1, 512])
    ones_c = p.din("ones_c", [1, 128])
    ao_out = p.dout("ao", [512, SEQ], BF16)

    pp = [p.psum(f"pp{i}", [128, 512]) for i in range(8)]
    A = Arena(p, 209536)
    qhb = [A.bf16(k * 8192, SEQ) for k in range(4)]
    khb = [A.bf16(32768 + k * 8192, SEQ) for k in range(4)]
    sexp = [A.bf16(65536 + i * 1024, 512) for i in range(32)]
    aob = [A.bf16(98304 + k * 8192, SEQ) for k in range(4)]
    Vb = [A.bf16(131072 + i * 4160, 32 * 65) for i in range(2)]
    Pc = [A.bf16(139392 + i * 1024, 512) for i in range(4)]
    frow_bel = A.f32(143488, 512)
    frow_abv = A.f32(145536, 512)
    drf = A.f32(147584, 512)
    tp = A.f32(149632, 512)
    colm_bel = A.f32(151680, 32)
    colm_abv = A.f32(151808, 32)
    t2c = A.f32(151936, 32)
    i2f = A.f32(152064, 32)
    av_sb = A.f32(152192, 512)
    av2 = A.f32(154240, 512)
    recb = A.f32(156288, 512, rows=64)
    SM = 158336
    irow = A.f32(SM, 512, rows=1)
    rt = A.f32(SM + 2048, 512, rows=1)
    rt2 = A.f32(SM + 4096, 512, rows=1)
    dr2 = A.t[64:65, SM + 6144:SM + 6144 + 2048].bitcast(F32)
    onef64 = A.t[64:65, SM + 8192:SM + 8192 + 512].bitcast(F32)
    onef = A.f32(SM + 8704, 128, rows=1)
    slc = A.f32(SM + 9216, 8)
    slr = A.f32(SM + 9248, 8, rows=1)

    for k in range(4):
        p.dma("sync", qhb[k], qh_in[k * 128:(k + 1) * 128, :])
        p.dma("sync", khb[k], kh_in[k * 128:(k + 1) * 128, :])
    p.dma("sync", drf[:, :], drow_in)
    p.dma("sync", i2f[:, :], i2_in)
    p.dma("sync", irow[0:1, :], ir_in)
    p.dma("sync", slc[:, :], slc_in)
    p.dma("sync", slr[0:1, :], slr_in)
    p.dma("sync", onef[0:1, :], ones_c)
    p.dma("sync", onef64[0:1, 0:64], ones_c[0:1, 0:64])
    p.dma("sync", Vb[0][:, :], vtl_in[0:128, :])

    for s in range(8):
        m, rb = s // 2, (s % 2) * 64
        s_keep = 2.0 ** (-8.0 / (8 - s)) if s < 8 else None
        p.phase()
        if s + 1 < 8:
            p.dma("sync", Vb[(s + 1) % 2][:, :],
                  vtl_in[(s + 1) * 128:(s + 2) * 128, :])
        for di in range(4):
            p.phase()
            p.ts(tp[:, :], drf[:, :], float(di * 128), ALU.add)
            p.phase()
            p.act(tp[:, :], tp[:, :], AF.Abs)
            p.phase()
            p.ts(tp[:, :], tp[:, :], slc[:, s:s + 1], ALU.mult)
            p.phase()
            p.act(Pc[di], tp[:, :], AF.Exp, scale=-1.0)
        p.phase()
        p.ts(rt[0:1, :], irow[0:1, :], slr[0:1, s:s + 1], ALU.mult)
        p.ts(rt2[0:1, :], irow[0:1, :], 511.0, ALU.subtract, eng="gpsimd")
        p.phase()
        p.act(rt[0:1, :], rt[0:1, :], AF.Exp, scale=-1.0)
        p.ts(rt2[0:1, :], rt2[0:1, :], slr[0:1, s:s + 1], ALU.mult)
        p.phase()
        p.act(rt2[0:1, :], rt2[0:1, :], AF.Exp)
        p.phase()
        p.mm(pp[4][0:128, 0:512], onef[0:1, 0:128], rt[0:1, :])
        p.mm(pp[5][0:128, 0:512], onef[0:1, 0:128], rt2[0:1, :])
        p.phase()
        p.cpy(frow_bel[:, :], pp[4][0:128, 0:512])
        p.cpy(frow_abv[:, :], pp[5][0:128, 0:512])
        for tqc in range(8):
            i0 = tqc * 512
            p.phase()
            p.ts(t2c[:, :], i2f[:, :], float(-i0), ALU.add)
            p.phase()
            p.ts(t2c[:, :], t2c[:, :], slc[:, s:s + 1], ALU.mult)
            p.phase()
            p.ts(t2c[:, :], t2c[:, :], 60.0, ALU.min)
            p.phase()
            p.act(colm_bel[:, :], t2c[:, :], AF.Exp)
            p.phase()
            p.ts(t2c[:, :], i2f[:, :], float(-(i0 + 511)), ALU.add)
            p.phase()
            p.ts(t2c[:, :], t2c[:, :], slc[:, s:s + 1], ALU.mult)
            p.phase()
            p.ts(t2c[:, :], t2c[:, :], -60.0, ALU.max)
            p.phase()
            p.act(colm_abv[:, :], t2c[:, :], AF.Exp, scale=-1.0)
            kept = []
            for kc in range(32):
                dd = kc * 128 - i0
                if dd in (0, 128, 256, 384):
                    kept.append((kc, "x", dd // 128))
                elif dd < 0:
                    mind = i0 - (kc * 128 + 127)
                    if s_keep * mind <= 103.0:
                        kept.append((kc, "b", 0))
                else:
                    mind = kc * 128 - (i0 + 511)
                    if s_keep * mind <= 103.0:
                        kept.append((kc, "a", 0))
            for g0_ in range(0, len(kept), 8):
                sub = kept[g0_:g0_ + 8]
                p.phase()
                for j, (kc, tag, di) in enumerate(sub):
                    p.mm(pp[j][0:128, 0:512],
                         khb[m][rb:rb + 64, kc * 128:(kc + 1) * 128],
                         qhb[m][rb:rb + 64, i0:i0 + 512])
                p.phase()
                for j, (kc, tag, di) in enumerate(sub):
                    p.act(sexp[kc], pp[j][0:128, 0:512], AF.Exp)
                p.phase()
                for j, (kc, tag, di) in enumerate(sub):
                    if tag == "x":
                        p.tt(sexp[kc], sexp[kc], Pc[di], ALU.mult)
                    elif tag == "b":
                        p.stt(sexp[kc], sexp[kc], colm_bel[:, kc:kc + 1],
                              frow_bel[:, :], ALU.mult, ALU.mult)
                    else:
                        p.stt(sexp[kc], sexp[kc], colm_abv[:, kc:kc + 1],
                              frow_abv[:, :], ALU.mult, ALU.mult)
            p.phase()
            nk = len(kept)
            for ki, (kc, tag, di) in enumerate(kept):
                p.mm(pp[0][0:65, 0:512],
                     Vb[s % 2][:, kc * 65:(kc + 1) * 65],
                     sexp[kc], start=(ki == 0), stop=(ki == nk - 1))
            p.phase()
            p.cpy(av_sb[0:65, :], pp[0][0:65, 0:512])
            p.phase()
            p.ts(av_sb[64:65, :], av_sb[64:65, :], 1e-30, ALU.max)
            p.phase()
            p.act(av_sb[64:65, :], av_sb[64:65, :], AF.Ln)
            p.phase()
            p.act(dr2[0:1, 0:512], av_sb[64:65, :], AF.Exp, scale=-1.0)
            p.phase()
            p.mm(pp[2][0:64, 0:512], onef64[0:1, 0:64], dr2[0:1, 0:512])
            p.phase()
            p.cpy(recb[0:64, :], pp[2][0:64, 0:512])
            p.phase()
            p.tt(av2[0:64, :], av_sb[0:64, :], recb[0:64, :], ALU.mult)
            p.phase()
            p.act(aob[m][rb:rb + 64, i0:i0 + 512], av2[0:64, :], AF.Copy)
    p.phase()
    for k in range(4):
        p.dma("gpsimd", ao_out[k * 128:(k + 1) * 128, :], aob[k])
    return p.finish()


CAP = 2560.0


def build_M1():
    p = Prog()
    x5_in = p.din("x5o", [D, T])
    ao_in = p.din("aoR", [D, T], BF16)
    wo_in = p.din("wo", [D, D])
    wr_in = p.din("wrT", [D, 8])
    tri_in = p.din("tri", [128, 128])
    i8_in = p.din("i8r", [1, 128])
    ones_c = p.din("ones_c", [1, 128])
    ones_col_in = p.din("ones_col", [128, 1])
    x6_out = p.dout("x6", [D, T])
    xn6_out = p.dout("xn6", [D, T], BF16)
    gcols_out = p.nc.dram_tensor("gcols", [128, 64], F32,
                                 kind="ExternalOutput")
    pos_out = p.nc.dram_tensor("pos_loc", [T, 8], F32, kind="ExternalOutput")
    counts_out = p.dout("counts", [1, 8])

    pp = [p.psum(f"pp{i}", [128, 512]) for i in range(8)]
    A = Arena(p, 209536)
    x6f = [A.f32(k * 8192, T) for k in range(DK)]
    aor = [A.bf16(65536 + k * 4096, T) for k in range(DK)]
    wob = [A.bf16(98304 + k * 2048, D) for k in range(DK)]
    xnb = [A.bf16(114688 + k * 4096, T) for k in range(DK)]
    xbf = [A.bf16(65536 + k * 4096, T) for k in range(DK)]
    xsqg = [A.bf16(98304 + k * 2048, 1024) for k in range(DK)]
    LB = 147456
    lb = NS(srow=A.f32(LB, 2 * T, rows=1),
            mu=A.f32(LB + 16384, T, rows=1),
            ab=A.bf16(LB + 24576, 2 * T, rows=1),
            abc=A.bf16(LB + 32768, 2 * T))
    t2ln = A.bf16(188416, T)
    wrb = A.bf16(192512, 8 * DK)
    lrow = A.f32(192704, T, rows=8)
    SM = 200896
    gl = A.f32(SM, 128)
    identf = A.f32(SM + 512, 128)
    gmax = A.f32(SM + 1024, 16)
    gsum = A.f32(SM + 1088, 16)
    eqb = A.f32(SM + 1152, 128)
    t1b = A.f32(SM + 1664, 128)
    gates2 = A.f32(SM + 2176, 128)
    eq0 = A.f32(SM + 2688, 128)
    eq1 = A.f32(SM + 3200, 128)
    maskf = A.f32(SM + 3712, 128)
    maskb = A.bf16(SM + 4224, 128)
    gquad = A.f32(SM + 4480, 64)
    iot8b = A.f32(SM + 4736, 128)
    iotm = A.f32(SM + 5248, 128)
    trib = A.bf16(SM + 5760, 128)
    posf = A.f32(SM + 6016, 8)
    runbc = A.f32(SM + 6048, 8)
    runrow = A.f32(SM + 6080, 8, rows=1)
    onebf = A.bf16(SM + 6144, 128, rows=1)
    onef = A.f32(SM + 6400, 128, rows=1)
    onecol = A.bf16(SM + 6912, 1)
    i8row = A.f32(SM + 6976, 128, rows=1)
    assert SM + 6976 + 512 <= 209536

    for k in range(DK):
        p.dma("sync", x6f[k], x5_in[k * 128:(k + 1) * 128, :])
        p.dma("sync", aor[k], ao_in[k * 128:(k + 1) * 128, :])
        p.dma("gpsimd", wob[k], wo_in[k * 128:(k + 1) * 128, :])
        p.dma("gpsimd", wrb[:, k * 8:(k + 1) * 8],
              wr_in[k * 128:(k + 1) * 128, :])
    p.dma("gpsimd", trib[:, :], tri_in)
    p.dma("sync", i8row[0:1, :], i8_in)
    p.dma("gpsimd", onebf[0:1, :], ones_c)
    p.dma("sync", onef[0:1, :], ones_c)
    p.dma("gpsimd", onecol[:, 0:1], ones_col_in)

    ident_in = p.din("ident", [128, 128])
    zr_in = p.din("zrow", [1, 8])
    p.dma("sync", identf[:, :], ident_in)
    p.dma("sync", runrow[0:1, :], zr_in)

    ct = tchunks(T)
    for mo in range(DK):
        p.phase()
        for i, (o, n) in enumerate(ct):
            for k in range(DK):
                p.mm(pp[i][0:128, 0:n], wob[k][:, mo * 128:(mo + 1) * 128],
                     aor[k][:, o:o + n], start=(k == 0), stop=(k == DK - 1))
        p.phase()
        for i, (o, n) in enumerate(ct):
            p.tt(x6f[mo][:, o:o + n], x6f[mo][:, o:o + n], pp[i][0:128, 0:n],
                 ALU.add)
        p.phase()
        p.dma("sync", x6_out[mo * 128:(mo + 1) * 128, :], x6f[mo])
    emit_layernorm(p, lb, T,
                   x_of=lambda k: x6f[k], xbf_of=lambda k: xbf[k],
                   xsq_of=lambda k: xsqg[k], out_of=lambda k: xnb[k],
                   pp=pp, ones_col=onecol, ones_row=onebf,
                   x_is_fp32=True, t2=t2ln, xsq_group=True)
    p.phase()
    for k in range(DK):
        p.dma("gpsimd", xn6_out[k * 128:(k + 1) * 128, :], xnb[k])
    p.phase()
    for i, (o, n) in enumerate(ct):
        for k in range(DK):
            p.mm(pp[i][0:8, 0:n], wrb[:, k * 8:(k + 1) * 8],
                 xnb[k][:, o:o + n], start=(k == 0), stop=(k == DK - 1))
    p.phase()
    for i, (o, n) in enumerate(ct):
        p.cpy(lrow[0:8, o:o + n], pp[i][0:8, 0:n])
    p.phase()
    for tt in range(16):
        p.transp(pp[4][0:128, tt * 8:(tt + 1) * 8],
                 lrow[0:8, tt * 128:(tt + 1) * 128], identf[0:8, 0:8])
    p.phase()
    p.cpy(gl[:, :], pp[4][0:128, 0:128])
    p.phase()
    p.mm(pp[5][0:128, 0:128], onef[0:1, 0:128], i8row[0:1, :])
    p.phase()
    p.cpy(iot8b[:, :], pp[5][0:128, 0:128])
    p.phase()
    p.ts(iotm[:, :], iot8b[:, :], 999.0, ALU.subtract)
    g3 = gl[:, :].rearrange("p (a b) -> p a b", b=8)

    def b16(x):
        return x.unsqueeze(2).to_broadcast((128, 16, 8))

    p.phase()
    p.red(gmax[:, :], g3, ALU.max)
    p.phase()
    p.tt(g3, g3, b16(gmax[:, :]), ALU.subtract)
    p.phase()
    p.act(gl[:, :], gl[:, :], AF.Exp)
    p.phase()
    p.red(gsum[:, :], g3, ALU.add)
    p.phase()
    p.act(gsum[:, :], gsum[:, :], AF.Ln)
    p.phase()
    p.act(gsum[:, :], gsum[:, :], AF.Exp, scale=-1.0)
    p.phase()
    p.tt(g3, g3, b16(gsum[:, :]), ALU.mult)
    g0c, g1c = gquad[:, 0:16], gquad[:, 16:32]
    i0c, i1c = gquad[:, 32:48], gquad[:, 48:64]
    e3 = eqb[:, :].rearrange("p (a b) -> p a b", b=8)
    t3 = t1b[:, :].rearrange("p (a b) -> p a b", b=8)
    g23 = gates2[:, :].rearrange("p (a b) -> p a b", b=8)
    e03 = eq0[:, :].rearrange("p (a b) -> p a b", b=8)
    e13 = eq1[:, :].rearrange("p (a b) -> p a b", b=8)
    i83 = iot8b[:, :].rearrange("p (a b) -> p a b", b=8)
    p.phase()
    p.red(g0c, g3, ALU.max)
    p.phase()
    p.tt(e3, g3, b16(g0c), ALU.is_equal)
    p.phase()
    p.tt(t1b[:, :], eqb[:, :], iotm[:, :], ALU.mult)
    p.phase()
    p.ts(t1b[:, :], t1b[:, :], 999.0, ALU.add)
    p.phase()
    p.red(i0c, t3, ALU.min)
    p.phase()
    p.tt(e03, i83, b16(i0c), ALU.is_equal)
    p.phase()
    p.stt(gates2[:, :], eq0[:, :], -2.0, gl[:, :], ALU.mult, ALU.add)
    p.phase()
    p.red(g1c, g23, ALU.max)
    p.phase()
    p.tt(e3, g23, b16(g1c), ALU.is_equal)
    p.phase()
    p.tt(t1b[:, :], eqb[:, :], iotm[:, :], ALU.mult)
    p.phase()
    p.ts(t1b[:, :], t1b[:, :], 999.0, ALU.add)
    p.phase()
    p.red(i1c, t3, ALU.min)
    p.phase()
    p.tt(e13, i83, b16(i1c), ALU.is_equal)
    p.phase()
    p.tt(maskf[:, :], eq0[:, :], eq1[:, :], ALU.add)
    p.phase()
    p.dma("gpsimd", maskb[:, :], maskf[:, :])
    p.phase()
    p.dma("sync", gcols_out.ap(), gquad[:, :])
    for tt in range(16):
        p.phase()
        p.mm(pp[5][0:128, 0:8], trib[:, :], maskb[:, tt * 8:(tt + 1) * 8])
        p.mm(pp[6][0:128, 0:8], onef[0:1, 0:128], runrow[0:1, :])
        p.phase()
        p.cpy(posf[:, :], pp[5][0:128, 0:8])
        p.cpy(runbc[:, :], pp[6][0:128, 0:8])
        p.phase()
        p.tt(posf[:, :], posf[:, :], runbc[:, :], ALU.add)
        p.phase()
        p.dma("sync", pos_out.ap()[tt * 128:(tt + 1) * 128, :], posf[:, :])
        p.dma("gpsimd", runrow[0:1, :], posf[127:128, :])
    p.phase()
    p.dma("sync", counts_out, runrow[0:1, :])
    return p.finish()


def build_M2(chain_faq=False):
    p = Prog()
    xn_in = p.din("xn6", [D, T], BF16)
    x6_in = p.din("x6o", [D, T])
    gcols_in = p.nc.dram_tensor("gcols", [128, 64], F32,
                                kind="ExternalInput")
    pos_in = p.nc.dram_tensor("pos_loc", [T, 8], F32, kind="ExternalInput")
    prevc_in = p.din("prevc", [8, 8])
    wgT_in = p.din("wgT", [D, 2048])
    wdT_in = p.din("wdTs", [8 * 2048, D])
    i8_in = p.din("i8r", [1, 128])
    ident_in = p.din("ident", [128, 128])
    ones_c = p.din("ones_c", [1, 128])
    ones_col_in = p.din("ones_col", [128, 1])
    x7_out = p.dout("x7", [D, T])

    pp = [p.psum(f"pp{i}", [128, 512]) for i in range(8)]
    A = Arena(p, 209536)
    hidb = [A.bf16(k * 4096, T) for k in range(16)]
    yacc = [A.f32(65536 + k * 8192, T) for k in range(DK)]
    WZ = 131072
    xnb = [A.bf16(WZ + k * 4096, T) for k in range(DK)]
    wgTb = [A.bf16(WZ + 32768 + k * 4096, 2048) for k in range(DK)]
    wdT = [A.bf16(WZ + kt * 2048, D) for kt in range(16)]
    crows = A.f32(WZ + 32768, T, rows=8)
    combbc = A.f32(WZ + 40960, T)
    crow0 = A.f32(WZ + 49152, T, rows=1)
    xres = A.f32(WZ + 40960, T)
    SM = WZ + 57344
    posb = A.f32(SM, 128)
    gq = A.f32(SM + 512, 64)
    posg = A.f32(SM + 768, 128)
    keb = A.f32(SM + 1280, 128)
    eq0 = A.f32(SM + 1792, 128)
    eq1 = A.f32(SM + 2304, 128)
    me = A.f32(SM + 2816, 128)
    comb = A.f32(SM + 3328, 128)
    iot8b = A.f32(SM + 3840, 128)
    identf = A.f32(SM + 4352, 128)
    offb = A.f32(SM + 4864, 8)
    kp0 = A.f32(SM + 4896, 16)
    kp1 = A.f32(SM + 4960, 16)
    g0k = A.f32(SM + 5024, 16)
    g1k = A.f32(SM + 5088, 16)
    den = A.f32(SM + 5152, 16)
    ok8 = A.f32(SM + 5216, 1, rows=8)
    prevcb = A.f32(SM + 5248, 8, rows=8)
    offrow = A.f32(SM + 5280, 8, rows=1)
    onef = A.f32(SM + 5312, 128, rows=1)
    i8row = A.f32(SM + 5824, 128, rows=1)
    tmpf = [A.f32(196608 + i * 2048, 512) for i in range(4)]
    assert 196608 + 4 * 2048 <= 209536

    for k in range(DK):
        p.dma("sync", xnb[k], xn_in[k * 128:(k + 1) * 128, :])
        p.dma("gpsimd", wgTb[k], wgT_in[k * 128:(k + 1) * 128, :])
    ct = tchunks(T)
    for hm in range(16):
        p.phase()
        for i, (o, n) in enumerate(ct):
            for k in range(DK):
                p.mm(pp[i][0:128, 0:n],
                     wgTb[k][:, hm * 128:(hm + 1) * 128],
                     xnb[k][:, o:o + n], start=(k == 0), stop=(k == DK - 1))
        p.phase()
        for i, (o, n) in enumerate(ct):
            p.act(hidb[hm][:, o:o + n], pp[i][0:128, 0:n], AF.Silu)
    p.phase()
    p.dma("sync", prevcb[0:8, :], prevc_in)
    p.dma("sync", ok8[0:8, 0:1], ones_col_in[0:8, 0:1])
    p.dma("sync", onef[0:1, :], ones_c)
    p.dma("sync", i8row[0:1, :], i8_in)
    p.dma("sync", identf[:, :], ident_in)
    p.dma("sync", posb[:, :],
          bass.AP(pos_in, 0, [[8, 128], [1024, 16], [1, 8]]))
    p.dma("sync", gq[:, :], gcols_in.ap())
    p.phase()
    p.mm(pp[0][0:1, 0:8], ok8[0:8, 0:1], prevcb[0:8, :])
    p.mm(pp[1][0:128, 0:128], onef[0:1, 0:128], i8row[0:1, :])
    p.phase()
    p.cpy(offrow[0:1, 0:8], pp[0][0:1, 0:8])
    p.cpy(iot8b[:, :], pp[1][0:128, 0:128])
    p.phase()
    p.mm(pp[2][0:128, 0:8], onef[0:1, 0:128], offrow[0:1, 0:8])
    p.phase()
    p.cpy(offb[:, :], pp[2][0:128, 0:8])
    pos3 = posb[:, :].rearrange("p (a b) -> p a b", b=8)
    pg3 = posg[:, :].rearrange("p (a b) -> p a b", b=8)
    i83 = iot8b[:, :].rearrange("p (a b) -> p a b", b=8)
    e03 = eq0[:, :].rearrange("p (a b) -> p a b", b=8)
    e13 = eq1[:, :].rearrange("p (a b) -> p a b", b=8)
    cb3 = comb[:, :].rearrange("p (a b) -> p a b", b=8)

    def b16(x):
        return x.unsqueeze(2).to_broadcast((128, 16, 8))

    p.phase()
    p.tt(pg3, pos3, offb[:, :].unsqueeze(1).to_broadcast((128, 16, 8)),
         ALU.add)
    p.phase()
    p.ts(keb[:, :], posg[:, :], CAP, ALU.is_le)
    p.phase()
    p.tt(e03, i83, gq[:, 32:48].unsqueeze(2).to_broadcast((128, 16, 8)),
         ALU.is_equal)
    p.phase()
    p.tt(e13, i83, gq[:, 48:64].unsqueeze(2).to_broadcast((128, 16, 8)),
         ALU.is_equal)
    p.phase()
    p.tt(me[:, :], eq0[:, :], eq1[:, :], ALU.add)
    p.phase()
    p.tt(me[:, :], me[:, :], keb[:, :], ALU.mult)
    p.phase()
    p.tt(eq0[:, :], eq0[:, :], me[:, :], ALU.mult)
    p.phase()
    p.tt(eq1[:, :], eq1[:, :], me[:, :], ALU.mult)
    p.phase()
    p.red(kp0[:, :], e03, ALU.max)
    p.phase()
    p.red(kp1[:, :], e13, ALU.max)
    p.phase()
    p.tt(g0k[:, :], gq[:, 0:16], kp0[:, :], ALU.mult)
    p.phase()
    p.tt(g1k[:, :], gq[:, 16:32], kp1[:, :], ALU.mult)
    p.phase()
    p.tt(den[:, :], g0k[:, :], g1k[:, :], ALU.add)
    p.phase()
    p.ts(den[:, :], den[:, :], 1e-6, ALU.add)
    p.phase()
    p.act(den[:, :], den[:, :], AF.Ln)
    p.phase()
    p.act(den[:, :], den[:, :], AF.Exp, scale=-1.0)
    p.phase()
    p.tt(g0k[:, :], g0k[:, :], den[:, :], ALU.mult)
    p.phase()
    p.tt(g1k[:, :], g1k[:, :], den[:, :], ALU.mult)
    p.phase()
    p.tt(e03, e03, b16(g0k[:, :]), ALU.mult)
    p.phase()
    p.tt(e13, e13, b16(g1k[:, :]), ALU.mult)
    p.phase()
    p.tt(cb3, e03, e13, ALU.add)
    for grp in range(4):
        p.phase()
        for j in range(4):
            tt = grp * 4 + j
            p.transp(pp[4][0:8, j * 128:(j + 1) * 128],
                     comb[:, tt * 8:(tt + 1) * 8], identf[:, :])
        p.phase()
        p.cpy(crows[0:8, grp * 512:(grp + 1) * 512], pp[4][0:8, 0:512])
    for e in range(8):
        p.phase()
        for kt in range(16):
            p.dma("gpsimd", wdT[kt],
                  wdT_in[e * 2048 + kt * 128:e * 2048 + (kt + 1) * 128, :])
        p.dma("sync", crow0[0:1, :], crows[e:e + 1, 0:T])
        p.phase()
        for i in range(4):
            p.mm(pp[4 + i][0:128, 0:512], onef[0:1, 0:128],
                 crow0[0:1, i * 512:(i + 1) * 512])
        p.phase()
        for i in range(4):
            p.cpy(combbc[:, i * 512:(i + 1) * 512], pp[4 + i][0:128, 0:512])
        for mo in range(DK):
            p.phase()
            for i, (o, n) in enumerate(ct):
                for kt in range(16):
                    p.mm(pp[i][0:128, 0:n],
                         wdT[kt][:, mo * 128:(mo + 1) * 128],
                         hidb[kt][:, o:o + n],
                         start=(kt == 0), stop=(kt == 15))
            if e == 0:
                p.phase()
                for i, (o, n) in enumerate(ct):
                    p.tt(yacc[mo][:, o:o + n], pp[i][0:128, 0:n],
                         combbc[:, o:o + n], ALU.mult)
            else:
                p.phase()
                for i, (o, n) in enumerate(ct):
                    p.tt(tmpf[i][:, 0:n], pp[i][0:128, 0:n],
                         combbc[:, o:o + n], ALU.mult)
                p.phase()
                for i, (o, n) in enumerate(ct):
                    p.tt(yacc[mo][:, o:o + n], yacc[mo][:, o:o + n],
                         tmpf[i][:, 0:n], ALU.add,
                         eng=("vector" if i % 2 == 0 else "gpsimd"))
    for mo in range(DK):
        p.phase()
        p.dma("sync", xres[:, :], x6_in[mo * 128:(mo + 1) * 128, :])
        p.phase()
        p.tt(yacc[mo], yacc[mo], xres[:, :], ALU.add)
        p.phase()
        p.dma("sync", x7_out[mo * 128:(mo + 1) * 128, :], yacc[mo])
    if chain_faq:
        build_FAq(3, p=p, A=A, pp=pp, x_src=x7_out, pre="q3_")
    return p.finish()



def build_FAatt3():
    p = Prog()
    qh_in = p.din("qh", [512, SEQ], BF16)
    kh_in = p.din("kh", [512, SEQ], BF16)
    vtl_in = p.din("vtl", [8 * 128, 32 * 65], BF16)
    cm_in = p.din("cm", [4 * 128, 512])
    ones_c = p.din("ones_c", [1, 128])
    ao_out = p.dout("ao", [512, SEQ], BF16)

    pp = [p.psum(f"pp{i}", [128, 512]) for i in range(8)]
    A = Arena(p, 209536)
    qhb = [A.bf16(k * 8192, SEQ) for k in range(4)]
    khb = [A.bf16(32768 + k * 8192, SEQ) for k in range(4)]
    sexp = [A.bf16(65536 + i * 1024, 512) for i in range(32)]
    aob = [A.bf16(98304 + k * 8192, SEQ) for k in range(4)]
    Vb = [A.bf16(131072 + i * 4160, 32 * 65) for i in range(2)]
    Pc = [A.bf16(139392 + i * 1024, 512) for i in range(4)]
    av_sb = A.f32(143488, 512)
    av2 = A.f32(145536, 512)
    recb = A.f32(147584, 512, rows=64)
    SM = 149632
    dr2 = A.t[64:65, SM:SM + 2048].bitcast(F32)
    onef64 = A.t[64:65, SM + 2048:SM + 2048 + 512].bitcast(F32)
    onef = A.f32(SM + 2560, 128, rows=1)

    for k in range(4):
        p.dma("sync", qhb[k], qh_in[k * 128:(k + 1) * 128, :])
        p.dma("sync", khb[k], kh_in[k * 128:(k + 1) * 128, :])
    for di in range(4):
        p.dma("gpsimd", Pc[di], cm_in[di * 128:(di + 1) * 128, :])
    p.dma("sync", onef[0:1, :], ones_c)
    p.dma("sync", onef64[0:1, 0:64], ones_c[0:1, 0:64])
    p.dma("sync", Vb[0][:, :], vtl_in[0:128, :])

    for s in range(8):
        m, rb = s // 2, (s % 2) * 64
        p.phase()
        if s + 1 < 8:
            p.dma("sync", Vb[(s + 1) % 2][:, :],
                  vtl_in[(s + 1) * 128:(s + 2) * 128, :])
        for tqc in range(8):
            i0 = tqc * 512
            kept = []
            for kc in range(32):
                dd = kc * 128 - i0
                if dd in (0, 128, 256, 384):
                    kept.append((kc, dd // 128))
                elif dd < 0:
                    kept.append((kc, -1))
            for g0_ in range(0, len(kept), 8):
                sub = kept[g0_:g0_ + 8]
                p.phase()
                for j, (kc, di) in enumerate(sub):
                    p.mm(pp[j][0:128, 0:512],
                         khb[m][rb:rb + 64, kc * 128:(kc + 1) * 128],
                         qhb[m][rb:rb + 64, i0:i0 + 512])
                p.phase()
                for j, (kc, di) in enumerate(sub):
                    p.act(sexp[kc], pp[j][0:128, 0:512], AF.Exp)
                if any(di >= 0 for (_kc, di) in sub):
                    p.phase()
                    for j, (kc, di) in enumerate(sub):
                        if di >= 0:
                            p.tt(sexp[kc], sexp[kc], Pc[di], ALU.mult)
            p.phase()
            nk = len(kept)
            for ki, (kc, di) in enumerate(kept):
                p.mm(pp[0][0:65, 0:512],
                     Vb[s % 2][:, kc * 65:(kc + 1) * 65],
                     sexp[kc], start=(ki == 0), stop=(ki == nk - 1))
            p.phase()
            p.cpy(av_sb[0:65, :], pp[0][0:65, 0:512])
            p.phase()
            p.ts(av_sb[64:65, :], av_sb[64:65, :], 1e-30, ALU.max)
            p.phase()
            p.act(av_sb[64:65, :], av_sb[64:65, :], AF.Ln)
            p.phase()
            p.act(dr2[0:1, 0:512], av_sb[64:65, :], AF.Exp, scale=-1.0)
            p.phase()
            p.mm(pp[2][0:64, 0:512], onef64[0:1, 0:64], dr2[0:1, 0:512])
            p.phase()
            p.cpy(recb[0:64, :], pp[2][0:64, 0:512])
            p.phase()
            p.tt(av2[0:64, :], av_sb[0:64, :], recb[0:64, :], ALU.mult)
            p.phase()
            p.act(aob[m][rb:rb + 64, i0:i0 + 512], av2[0:64, :], AF.Copy)
    p.phase()
    for k in range(4):
        p.dma("gpsimd", ao_out[k * 128:(k + 1) * 128, :], aob[k])
    return p.finish()


def build_L3F():
    p = Prog()
    ao_in = p.din("aoR", [D, T], BF16)
    wout_in = p.din("wout", [D, D])
    x7_in = p.din("x7o", [D, T])
    wtT_in = p.din("wtT", [D, D])
    alpha_in = p.din("alpha", [1, 1])
    bias_in = p.din("bias", [D, 1])
    ones_c = p.din("ones_c", [1, 128])
    ones_col_in = p.din("ones_col", [128, 1])
    x8_out = p.dout("x8", [D, T])
    if L3DBG:
        dbg_wt = p.dout("dbg_wt", [D, D], BF16)
        dbg_xa = p.dout("dbg_xa", [D, T])
        dbg_xn = p.dout("dbg_xn", [D, T], BF16)
        dbg_d = p.dout("dbg_d", [128, 4])

    pp = [p.psum(f"pp{i}", [128, 512]) for i in range(8)]
    A = Arena(p, 209536)
    xa = [A.f32(k * 8192, T) for k in range(DK)]
    aor = [A.bf16(65536 + k * 4096, T) for k in range(DK)]
    woutb = [A.bf16(98304 + k * 2048, D) for k in range(DK)]
    xnb = [A.bf16(114688 + k * 4096, T) for k in range(DK)]
    xbf = [A.bf16(65536 + k * 4096, T) for k in range(DK)]
    xsqg = [A.bf16(98304 + k * 2048, 1024) for k in range(DK)]
    LB = 147456
    lb = NS(srow=A.f32(LB, 2 * T, rows=1),
            mu=A.f32(LB + 16384, T, rows=1),
            ab=A.bf16(LB + 24576, 2 * T, rows=1),
            abc=A.bf16(LB + 32768, 2 * T))
    t2ln = A.bf16(LB, T)
    wnat = A.f32(LB, D)
    tw1 = A.f32(LB + 4096, D)
    tw2 = A.f32(LB + 8192, D)
    wtb = [A.bf16(188416 + k * 2048, D) for k in range(DK)]
    SM = 204800
    wsum = A.f32(SM, 2)
    dcol = A.f32(SM + 8, 1)
    ndcol = A.f32(SM + 12, 1)
    acol = A.f32(SM + 16, 1)
    bcol = A.f32(SM + 20, DK)
    onef = A.f32(SM + 64, 128, rows=1)
    onecolf = A.f32(SM + 576, 1)
    onebf = A.bf16(SM + 640, 128, rows=1)
    onecol = A.bf16(SM + 896, 1)
    wst4 = [A.f32(LB + 12288 + i * 2048, 512) for i in range(4)]

    for k in range(DK):
        p.dma("sync", xa[k], x7_in[k * 128:(k + 1) * 128, :])
        p.dma("sync", aor[k], ao_in[k * 128:(k + 1) * 128, :])
        p.dma("gpsimd", woutb[k], wout_in[k * 128:(k + 1) * 128, :])
    p.dma("sync", onef[0:1, :], ones_c)
    p.dma("sync", onecolf[:, 0:1], ones_col_in)
    p.dma("gpsimd", onebf[0:1, :], ones_c)
    p.dma("gpsimd", onecol[:, 0:1], ones_col_in)
    load_col(p, bcol, bias_in)
    for k in range(DK):
        p.phase()
        p.dma("sync", wnat[:, :], wtT_in[k * 128:(k + 1) * 128, :])
        p.phase()
        p.red(wsum[:, 1:2], wnat[:, :], ALU.add, absval=True)
        p.phase()
        if k == 0:
            p.cpy(wsum[:, 0:1], wsum[:, 1:2])
        else:
            p.tt(wsum[:, 0:1], wsum[:, 0:1], wsum[:, 1:2], ALU.add)
    p.phase()
    p.mm(pp[0][0:1, 0:1], wsum[:, 0:1], onecolf[:, 0:1])
    p.phase()
    p.ts(wsum[0:1, 0:1], pp[0][0:1, 0:1], 0.7 / (D * D), ALU.mult)
    p.phase()
    p.mm(pp[1][0:128, 0:1], onef[0:1, 0:128], wsum[0:1, 0:1])
    p.phase()
    p.cpy(dcol[:, 0:1], pp[1][0:128, 0:1])
    p.phase()
    p.ts(ndcol[:, 0:1], dcol[:, 0:1], -1.0, ALU.mult)
    acell = A.f32(SM + 900, 1, rows=1)
    p.dma("sync", acell[0:1, 0:1], alpha_in)
    p.phase()
    p.mm(pp[3][0:128, 0:1], onef[0:1, 0:128], acell[0:1, 0:1])
    p.phase()
    p.cpy(acol[:, 0:1], pp[3][0:128, 0:1])
    for k in range(DK):
        p.phase()
        p.dma("sync", wnat[:, :], wtT_in[k * 128:(k + 1) * 128, :])
        p.phase()
        p.ts(tw1[:, :], wnat[:, :], dcol[:, 0:1], ALU.subtract)
        p.ts(tw2[:, :], wnat[:, :], ndcol[:, 0:1], ALU.subtract, eng="gpsimd")
        p.phase()
        p.ts(tw1[:, :], tw1[:, :], 0.0, ALU.is_gt)
        p.ts(tw2[:, :], tw2[:, :], 0.0, ALU.is_lt, eng="gpsimd")
        p.phase()
        p.tt(tw1[:, :], tw1[:, :], tw2[:, :], ALU.subtract)
        p.phase()
        p.dma("gpsimd", wtb[k], tw1[:, :])
        if L3DBG:
            p.phase()
            p.dma("sync", dbg_wt[k * 128:(k + 1) * 128, :], wtb[k])
    ct = tchunks(T)
    for mo in range(DK):
        p.phase()
        for i, (o, n) in enumerate(ct):
            for k in range(DK):
                p.mm(pp[i][0:128, 0:n], woutb[k][:, mo * 128:(mo + 1) * 128],
                     aor[k][:, o:o + n], start=(k == 0), stop=(k == DK - 1))
        p.phase()
        for i, (o, n) in enumerate(ct):
            p.tt(xa[mo][:, o:o + n], xa[mo][:, o:o + n], pp[i][0:128, 0:n],
                 ALU.add)
    if L3DBG:
        p.phase()
        for k in range(DK):
            p.dma("sync", dbg_xa[k * 128:(k + 1) * 128, :], xa[k])
        p.dma("sync", dbg_d[:, 0:1], dcol[:, 0:1], slow=True)
        p.dma("sync", dbg_d[:, 1:2], acol[:, 0:1], slow=True)
        p.dma("sync", dbg_d[:, 2:3], wsum[:, 0:1], slow=True)
        p.dma("sync", dbg_d[:, 3:4], wsum[:, 1:2], slow=True)
    emit_layernorm(p, lb, T,
                   x_of=lambda k: xa[k], xbf_of=lambda k: xbf[k],
                   xsq_of=lambda k: xsqg[k], out_of=lambda k: xnb[k],
                   pp=pp, ones_col=onecol, ones_row=onebf,
                   x_is_fp32=True, t2=t2ln, xsq_group=True)
    if L3DBG:
        p.phase()
        for k in range(DK):
            p.dma("gpsimd", dbg_xn[k * 128:(k + 1) * 128, :], xnb[k])
    for mo in range(DK):
        p.phase()
        for i, (o, n) in enumerate(ct):
            for k in range(DK):
                p.mm(pp[i][0:128, 0:n], wtb[k][:, mo * 128:(mo + 1) * 128],
                     xnb[k][:, o:o + n], start=(k == 0), stop=(k == DK - 1))
        p.phase()
        for i, (o, n) in enumerate(ct):
            p.ts(wst4[i][:, 0:n], pp[i][0:128, 0:n], acol[:, 0:1],
                 ALU.mult, s2=bcol[:, mo:mo + 1], op2=ALU.add)
        p.phase()
        for i, (o, n) in enumerate(ct):
            p.tt(wst4[i][:, 0:n], wst4[i][:, 0:n], xa[mo][:, o:o + n],
                 ALU.add)
        p.phase()
        for i, (o, n) in enumerate(ct):
            p.dma("sync", x8_out[mo * 128:(mo + 1) * 128, o:o + n],
                  wst4[i][:, 0:n])
    return p.finish()


def stage_L1FB(inputs, core, xn8, pmax8):
    return {"xn": xn8[core], "pmax8": pmax8,
            "wup": inputs["L1_ffn_w_up"], "wg": inputs["L1_ffn_w_gate"],
            "ident": np.eye(128, dtype=np.float32),
            "ones_c": _ONES["ones_c"]}


def stage_L1FC(core, g8, pmg8):
    return {"gate_raw": g8[core], "pmg8": pmg8, "ones_c": _ONES["ones_c"]}


def stage_L1FD(core, u8, g28, pmu8, pm28):
    return {"up_raw": u8[core], "gate2": g28[core], "pmu8": pmu8,
            "pm28": pm28, "ones_c": _ONES["ones_c"]}


def stage_L1FE(inputs, core, o8, pmo8, x4_8):
    return {"o_raw": o8[core], "pmo8": pmo8, "wd": inputs["L1_ffn_w_down"],
            "x4o": x4_8[core], "ident": np.eye(128, dtype=np.float32),
            "ones_c": _ONES["ones_c"]}


_ONES = {"ones_c": np.ones((1, 128), np.float32),
         "ones_col": np.ones((128, 1), np.float32)}

_REV128_IDX = None


def _rev128():
    global _REV128_IDX
    if _REV128_IDX is None:
        idx = np.arange(SEQ)
        _REV128_IDX = (idx // 128) * 128 + (127 - idx % 128)
    return _REV128_IDX


def _vtl_ones(vfull, heads):
    nh = len(heads)
    out = np.zeros((nh * 128, 32 * 65), vfull.dtype)
    vr = vfull.reshape(32, 128, 16, 64)
    for i, hh in enumerate(heads):
        o3 = out[i * 128:(i + 1) * 128].reshape(128, 32, 65)
        o3[:, :, 0:64] = vr[:, :, hh, :].transpose(1, 0, 2)
        o3[:, :, 64] = 1.0
    return out


def stage_FAq(inputs, core, blk, xfull, pre="", skip_x=False):
    b, h = core // 2, core % 2
    d = {}
    if not skip_x:
        d["x_in"] = np.ascontiguousarray(xfull[b, h * T:(h + 1) * T].T)
    d.update(_ONES)
    if blk == 1:
        d["wq"] = inputs["L1_attn_Wq"]
        d["wk"] = inputs["L1_attn_Wk"]
        d["wv"] = inputs["L1_attn_Wv"]
        d["bqk"] = np.concatenate(
            [inputs["L1_attn_bq"], inputs["L1_attn_bk"]]).reshape(2 * D, 1)
        d["bv"] = inputs["L1_attn_bv"].reshape(1, D)
    elif blk == 2:
        d["wq"] = inputs["L2_attn_Wq"]
        d["wk"] = inputs["L2_attn_Wk"]
        d["wv"] = inputs["L2_attn_Wv"]
        d["g2"] = inputs["L2_attn_ln_gamma"].reshape(D, 1)
        d["b2"] = inputs["L2_attn_ln_beta"].reshape(D, 1)
    else:
        W = inputs["L3_attn_W_qkv"]
        d["wq"] = np.ascontiguousarray(W[:, 0:D])
        d["wk"] = np.ascontiguousarray(W[:, D:2 * D])
        d["wv"] = np.ascontiguousarray(W[:, 2 * D:3 * D])
    if pre:
        d = {pre + k: v for k, v in d.items()}
    return d


def stage_FAatt1(inputs, core, q8, k8, v8, x3full):
    b, h = core // 2, core % 2
    G0 = h * T
    rev = _rev128()
    kfull = np.concatenate([k8[2 * b], k8[2 * b + 1]], axis=1)[:, rev]
    vfull = np.concatenate([v8[2 * b], v8[2 * b + 1]], axis=0)[rev]
    table = inputs["L1_attn_bias_table"]
    idxs = np.minimum(G0 + np.arange(2560), 1998)
    r2c = np.ascontiguousarray(table[idxs].T).astype(np.float32)
    d = {"qb": q8[core],
         "ko": np.ascontiguousarray(kfull),
         "vtl": _vtl_ones(vfull, list(range(16))),
         "x3o": np.ascontiguousarray(x3full[b, G0:G0 + T].T),
         "wp": inputs["L1_attn_Wp"],
         "bp": inputs["L1_attn_bp"].reshape(D, 1),
         "temp": np.asarray(inputs["L1_attn_temperature"],
                            np.float32).reshape(1, 1),
         "r2c": r2c}
    d.update(_ONES)
    return d


def stage_FAatt2(inputs, core, q8, k8, v8):
    b, hh = core // 2, core % 2
    qfull = np.concatenate([q8[2 * b], q8[2 * b + 1]], axis=1)
    kfull = np.concatenate([k8[2 * b], k8[2 * b + 1]], axis=1)
    vfull = np.concatenate([v8[2 * b], v8[2 * b + 1]], axis=0)
    heads = [hh * 8 + s for s in range(8)]
    slopes = np.array([2.0 ** (-8.0 / (16 - hg)) for hg in heads], np.float32)
    jj = np.arange(128)[:, None]
    ii = np.arange(512)[None, :]
    d = {"qh": np.ascontiguousarray(qfull[hh * 512:(hh + 1) * 512]),
         "kh": np.ascontiguousarray(kfull[hh * 512:(hh + 1) * 512]),
         "vtl": _vtl_ones(vfull, heads),
         "slc": np.tile(slopes[None, :], (128, 1)),
         "slr": slopes[None, :].copy(),
         "drow": (jj - ii).astype(np.float32),
         "i2": (np.arange(128)[:, None]
                + 128 * np.arange(32)[None, :]).astype(np.float32),
         "ir": np.arange(512, dtype=np.float32)[None, :]}
    d.update(_ONES)
    return d


def stage_M1(inputs, core, ao2, x5full):
    b, h = core // 2, core % 2
    aoRf = np.concatenate([ao2[2 * b], ao2[2 * b + 1]], axis=0)
    tri = (np.arange(128)[:, None] <= np.arange(128)[None, :]
           ).astype(np.float32)
    d = {"x5o": np.ascontiguousarray(x5full[b, h * T:(h + 1) * T].T),
         "aoR": np.ascontiguousarray(aoRf[:, h * T:(h + 1) * T]),
         "wo": inputs["L2_attn_Wo"],
         "wrT": np.ascontiguousarray(inputs["L2_ffn_w_router"].T),
         "tri": tri,
         "i8r": (np.arange(128) % 8).astype(np.float32)[None, :],
         "ident": np.eye(128, dtype=np.float32),
         "zrow": np.zeros((1, 8), np.float32)}
    d.update(_ONES)
    return d


def stage_M2(inputs, core, r1, counts8):
    prevc = np.zeros((8, 8), np.float32)
    for cc in range(core):
        prevc[cc] = counts8[cc]
    d = {"xn6": r1[core]["xn6"], "x6o": r1[core]["x6"],
         "gcols": r1[core]["gcols"], "pos_loc": r1[core]["pos_loc"],
         "prevc": prevc,
         "wgT": np.ascontiguousarray(inputs["L2_ffn_w_gate"].T),
         "wdTs": np.ascontiguousarray(
             inputs["L2_ffn_w_down"].transpose(0, 2, 1).reshape(8 * 2048, D)),
         "i8r": (np.arange(128) % 8).astype(np.float32)[None, :],
         "ident": np.eye(128, dtype=np.float32)}
    d.update(_ONES)
    return d


def stage_FAatt3(inputs, core, q8, k8, v8):
    b, hh = core // 2, core % 2
    qfull = np.concatenate([q8[2 * b], q8[2 * b + 1]], axis=1)
    kfull = np.concatenate([k8[2 * b], k8[2 * b + 1]], axis=1)
    vfull = np.concatenate([v8[2 * b], v8[2 * b + 1]], axis=0)
    heads = [hh * 8 + s for s in range(8)]
    jc = np.arange(128)[:, None]
    ic = np.arange(512)[None, :]
    cm = np.concatenate(
        [(jc + di * 128 <= ic).astype(np.float32) for di in range(4)], 0)
    d = {"qh": np.ascontiguousarray(qfull[hh * 512:(hh + 1) * 512]),
         "kh": np.ascontiguousarray(kfull[hh * 512:(hh + 1) * 512]),
         "vtl": _vtl_ones(vfull, heads),
         "cm": cm}
    d.update(_ONES)
    return d


def stage_L3F(inputs, core, ao3, x7full):
    b, h = core // 2, core % 2
    aoRf = np.concatenate([ao3[2 * b], ao3[2 * b + 1]], axis=0)
    d = {"aoR": np.ascontiguousarray(aoRf[:, h * T:(h + 1) * T]),
         "wout": inputs["L3_attn_W_out"],
         "x7o": np.ascontiguousarray(x7full[b, h * T:(h + 1) * T].T),
         "wtT": np.ascontiguousarray(inputs["L3_ffn_weight"].T),
         "alpha": np.asarray(inputs["L3_ffn_alpha"],
                             np.float32).reshape(1, 1),
         "bias": inputs["L3_ffn_bias"].reshape(D, 1)}
    d.update(_ONES)
    return d


def gather_full(results, key):
    full = np.zeros((B_SZ, SEQ, D), np.float32)
    for c in range(NCORE):
        b, h = c // 2, c % 2
        full[b, h * T:(h + 1) * T] = results[c][key].T
    return full


def run_l0ffn(inputs, x1full):
    ra = run_launch(lambda: build_L0Fa(True),
                    lambda ii, c: stage_L0Fa(ii, c, 0, x1full), inputs)
    rb = run_launch(lambda: build_L0Fb(False),
                    lambda ii, c: stage_L0Fb(ii, c, 0, ra[c]["u"]), inputs)
    vfull = gather_full(rb, "v")
    ra2 = run_launch(lambda: build_L0Fa(False),
                     lambda ii, c: stage_L0Fa(ii, c, 1, vfull), inputs)
    rb2 = run_launch(lambda: build_L0Fb(True),
                     lambda ii, c: stage_L0Fb(ii, c, 1, ra2[c]["u"], x1full),
                     inputs)
    return gather_full(rb2, "v")


def run_launch(build_fn, stage_fn, inputs):
    nc = build_fn()
    in_maps = [stage_fn(inputs, c) for c in range(NCORE)]
    res = run_bass_kernel_spmd(nc, in_maps, core_ids=list(range(NCORE)))
    return res.results


def kernel(**inputs):
    r1 = run_launch(build_L01, stage_L01, inputs)
    x1 = gather_full(r1, "x1")
    ra0 = run_launch(lambda: build_L0Fa(True),
                     lambda ii, c: stage_L0Fa(ii, c, 0, x1), inputs)
    rb0 = run_launch(lambda: build_L0Fb(False),
                     lambda ii, c: stage_L0Fb(ii, c, 0, ra0[c]["u"]), inputs)
    vfull = gather_full(rb0, "v")
    ra1 = run_launch(lambda: build_L0Fa(False),
                     lambda ii, c: stage_L0Fa(ii, c, 1, vfull), inputs)
    rb1 = run_launch(
        lambda: build_L0Fb(True, chain_faq=True),
        lambda ii, c: {**stage_L0Fb(ii, c, 1, ra1[c]["u"], x1),
                       **stage_FAq(ii, c, 1, None, pre="q1_", skip_x=True)},
        inputs)
    x2 = gather_full(rb1, "v")
    q8 = [rb1[c]["q1_q"] for c in range(NCORE)]
    k8 = [rb1[c]["q1_k"] for c in range(NCORE)]
    v8 = [rb1[c]["q1_v"] for c in range(NCORE)]
    ra = run_launch(build_FAatt1,
                    lambda ii, c: stage_FAatt1(ii, c, q8, k8, v8, x2), inputs)
    x3_8 = [ra[c]["x4"] for c in range(NCORE)]
    xn8 = [ra[c]["xn4"] for c in range(NCORE)]
    pmax8 = np.concatenate([ra[c]["pmax"] for c in range(NCORE)], 1)
    rb = run_launch(build_L1FB,
                    lambda ii, c: stage_L1FB(ii, c, xn8, pmax8), inputs)
    pmu8 = np.concatenate([rb[c]["pmu"] for c in range(NCORE)], 1)
    pmg8 = np.concatenate([rb[c]["pmg"] for c in range(NCORE)], 1)
    g8 = [rb[c]["gate_raw"] for c in range(NCORE)]
    u8 = [rb[c]["up_raw"] for c in range(NCORE)]
    rc = run_launch(build_L1FC, lambda ii, c: stage_L1FC(c, g8, pmg8), inputs)
    pm28 = np.concatenate([rc[c]["pm2"] for c in range(NCORE)], 1)
    g28 = [rc[c]["gate2"] for c in range(NCORE)]
    rd = run_launch(build_L1FD,
                    lambda ii, c: stage_L1FD(c, u8, g28, pmu8, pm28), inputs)
    pmo8 = np.concatenate([rd[c]["pmo"] for c in range(NCORE)], 1)
    o8 = [rd[c]["o_raw"] for c in range(NCORE)]
    re_ = run_launch(build_L1FE,
                     lambda ii, c: stage_L1FE(ii, c, o8, pmo8, x3_8), inputs)
    x4 = gather_full(re_, "x5")
    rq2 = run_launch(lambda: build_FAq(2),
                     lambda ii, c: stage_FAq(ii, c, 2, x4), inputs)
    q8 = [rq2[c]["q"] for c in range(NCORE)]
    k8 = [rq2[c]["k"] for c in range(NCORE)]
    v8 = [rq2[c]["v"] for c in range(NCORE)]
    ra2 = run_launch(build_FAatt2,
                     lambda ii, c: stage_FAatt2(ii, c, q8, k8, v8), inputs)
    ao2 = [ra2[c]["ao"] for c in range(NCORE)]
    r1m = run_launch(build_M1, lambda ii, c: stage_M1(ii, c, ao2, x4), inputs)
    counts8 = [r1m[c]["counts"][0] for c in range(NCORE)]
    r2m = run_launch(
        lambda: build_M2(chain_faq=True),
        lambda ii, c: {**stage_M2(ii, c, r1m, counts8),
                       **stage_FAq(ii, c, 3, None, pre="q3_", skip_x=True)},
        inputs)
    x6 = gather_full(r2m, "x7")
    q8 = [r2m[c]["q3_q"] for c in range(NCORE)]
    k8 = [r2m[c]["q3_k"] for c in range(NCORE)]
    v8 = [r2m[c]["q3_v"] for c in range(NCORE)]
    ra3 = run_launch(build_FAatt3,
                     lambda ii, c: stage_FAatt3(ii, c, q8, k8, v8), inputs)
    ao3 = [ra3[c]["ao"] for c in range(NCORE)]
    rf = run_launch(build_L3F, lambda ii, c: stage_L3F(ii, c, ao3, x6),
                    inputs)
    return gather_full(rf, "x8")


# version 04 (final)
# speedup vs baseline: 1.0019x; 1.0019x over previous
import contextlib
import numpy as np
import concourse.bass as bass
import concourse.mybir as mybir
from concourse.bass_utils import run_bass_kernel_spmd

F32 = mybir.dt.float32
BF16 = mybir.dt.bfloat16
I32 = mybir.dt.int32
U8 = mybir.dt.uint8
AF = mybir.ActivationFunctionType
ALU = mybir.AluOpType
AX = mybir.AxisListType

B_SZ, SEQ, D = 4, 4096, 1024
T = 2048
HALO = 48
TE = T + HALO
DK = D // 128
LN_EPS = 1e-5
NST = 16
NCORE = 8

ENGINES = ("sync", "gpsimd", "tensor", "vector", "scalar")
DBG = False
DBG2 = False
SKIP = set()
L3DBG = False


def tchunks(total, step=512):
    out, o = [], 0
    while o < total:
        out.append((o, min(step, total - o)))
        o += step
    return out


class Prog:

    def __init__(self):
        self.nc = bass.Bass("TRN2", target_bir_lowering=False)
        self.es = contextlib.ExitStack()
        self.phases = []
        self.cur = None
        self.phase()

    def sb(self, name, shape, dtype):
        return self.es.enter_context(self.nc.sbuf_tensor(name, list(shape), dtype))

    def psum(self, name, shape, dtype=F32):
        return self.es.enter_context(self.nc.psum_tensor(name, list(shape), dtype))

    def din(self, name, shape, dtype=F32):
        return self.nc.dram_tensor(name, list(shape), dtype, kind="ExternalInput").ap()

    def dout(self, name, shape, dtype=F32):
        return self.nc.dram_tensor(name, list(shape), dtype, kind="ExternalOutput").ap()

    def phase(self):
        self.cur = {e: [] for e in ENGINES}
        self.phases.append(self.cur)

    def _q(self, eng, fn, dma=False):
        self.cur[eng].append((fn, dma))

    def dma(self, eng, out, in_, slow=False):
        if slow:
            self._q(eng, lambda e: e.dma_start(
                out=out, in_=in_, allow_slow_non_contiguous=True), dma=True)
        else:
            self._q(eng, lambda e: e.dma_start(out=out, in_=in_), dma=True)

    def mm(self, out, lhsT, rhs, start=True, stop=True):
        self._q("tensor", lambda e: e.matmul(out, lhsT, rhs, start=start, stop=stop))

    def transp(self, out_ps, in_sb, ident):
        self._q("tensor", lambda e: e.transpose(out_ps, in_sb, ident))

    def cpy(self, out, in_, eng="vector"):
        self._q(eng, lambda e: e.tensor_copy(out, in_))

    def tt(self, out, a, b, op, eng="vector"):
        self._q(eng, lambda e: e.tensor_tensor(out, a, b, op))

    def ts(self, out, a, s1, op, s2=None, op2=ALU.bypass, eng="vector",
           accum_out=None):
        if s2 is None and accum_out is None:
            self._q(eng, lambda e: e.tensor_single_scalar(out, a, s1, op))
        else:
            self._q(eng, lambda e: e.tensor_scalar(out, a, s1, s2, op, op2,
                                                   accum_out=accum_out))

    def stt(self, out, in0, scalar, in1, op0, op1, eng="vector"):
        self._q(eng, lambda e: e.scalar_tensor_tensor(out, in0, scalar, in1,
                                                      op0, op1))

    def act(self, out, in_, func, bias=0.0, scale=1.0, accum_out=None):
        self._q("scalar", lambda e: e.activation(out, in_, func, bias=bias,
                                                 scale=scale,
                                                 accum_out=accum_out))

    def red(self, out, in_, op, absval=None, negate=None, eng="vector"):
        self._q(eng, lambda e: e.tensor_reduce(out, in_, AX.X, op,
                                               apply_absolute_value=absval,
                                               negate=negate))

    def scan(self, out, d0, d1, init, op0, op1, eng="vector"):
        self._q(eng, lambda e: e.tensor_tensor_scan(out, d0, d1, init, op0, op1))

    def memset(self, ap, v, eng="vector"):
        self._q(eng, lambda e: e.memset(ap, v))

    def recip(self, out, in_):
        self._q("vector", lambda e: e.reciprocal(out, in_))

    def select(self, out, mask, a, b):
        self._q("vector", lambda e: e.select(out, mask, a, b))

    def iota(self, out, pattern, base=0, cmul=0):
        self._q("gpsimd", lambda e: e.iota(out, pattern, base=base,
                                           channel_multiplier=cmul,
                                           allow_small_or_imprecise_dtypes=True))

    def affsel(self, out, in_, pattern, cmp, fill, base=0, cmul=0):
        self._q("gpsimd", lambda e: e.affine_select(out, in_, pattern, cmp,
                                                    fill, base=base,
                                                    channel_multiplier=cmul))

    def finish(self):
        nc = self.nc
        barrier = self.es.enter_context(nc.semaphore("B"))
        sbar = self.es.enter_context(nc.semaphore("BS"))
        gbar = self.es.enter_context(nc.semaphore("BG"))
        incs, sincs, gincs = [], [], []
        for ph in self.phases:
            inc, sinc, ginc = 0, 0, 0
            for e in ENGINES:
                ops = ph[e]
                if not ops:
                    continue
                nd = 16 * sum(1 for (_f, d) in ops if d)
                if e == "gpsimd":
                    ginc += nd
                else:
                    sinc += nd
                if not ops[-1][1]:
                    inc += 1
            incs.append(inc)
            sincs.append(sinc)
            gincs.append(ginc)
        cum = np.cumsum([0] + incs)
        scum = np.cumsum([0] + sincs)
        gcum = np.cumsum([0] + gincs)
        blk = self.es.enter_context(nc.Block())

        def body_for(ename):
            def body(eng):
                lastb = lasts = lastg = -1
                for pi, ph in enumerate(self.phases):
                    ops = ph[ename]
                    if not ops:
                        continue
                    if pi > 0:
                        if cum[pi] > 0 and cum[pi] != lastb:
                            eng.wait_ge(barrier, int(cum[pi]))
                            lastb = cum[pi]
                        if scum[pi] > 0 and scum[pi] != lasts:
                            eng.wait_ge(sbar, int(scum[pi]))
                            lasts = scum[pi]
                        if gcum[pi] > 0 and gcum[pi] != lastg:
                            eng.wait_ge(gbar, int(gcum[pi]))
                            lastg = gcum[pi]
                    n = len(ops)
                    for i, (fn, is_dma) in enumerate(ops):
                        inst = fn(eng)
                        if is_dma:
                            inst.then_inc(gbar if ename == "gpsimd" else sbar, 16)
                        elif i == n - 1:
                            inst.then_inc(barrier, 1)
                if ename == "sync":
                    eng.wait_ge(barrier, int(cum[len(self.phases)]))
                    eng.wait_ge(sbar, int(scum[len(self.phases)]))
                    eng.wait_ge(gbar, int(gcum[len(self.phases)]))
            return body

        blk.sync(body_for("sync"))
        blk.gpsimd(body_for("gpsimd"))
        blk.tensor(body_for("tensor"))
        blk.vector(body_for("vector"))
        blk.scalar(body_for("scalar"))
        return nc


class Arena:

    def __init__(self, p, nbytes):
        self.t = p.sb("arena", [128, nbytes], U8)
        self.nbytes = nbytes

    def f32(self, off, cols, rows=128):
        return self.t[0:rows, off:off + 4 * cols].bitcast(F32)

    def bf16(self, off, cols, rows=128):
        return self.t[0:rows, off:off + 2 * cols].bitcast(BF16)

    def i32(self, off, cols, rows=128):
        return self.t[0:rows, off:off + 4 * cols].bitcast(I32)


class Bump:
    def __init__(self, limit):
        self.off, self.limit = 0, limit

    def take(self, nbytes):
        o = (self.off + 63) & ~63
        self.off = o + nbytes
        assert self.off <= self.limit, (self.off, self.limit)
        return o

    def mark(self):
        return self.off

    def reset(self, m):
        self.off = m



def emit_row_bcast(p, items, pp, ones_f32):
    npp = min(8, len(pp))
    for g in range(0, len(items), npp):
        sub = items[g:g + npp]
        p.phase()
        for i, (src, dst, n) in enumerate(sub):
            p.mm(pp[i][0:128, 0:n], ones_f32[0:1, 0:128], src)
        p.phase()
        for i, (src, dst, n) in enumerate(sub):
            p.cpy(dst, pp[i][0:128, 0:n])


def emit_layernorm(p, lb, TT, x_of, xbf_of, xsq_of, out_of, pp, ones_col,
                   ones_row, gamma_col=None, beta_col=None, x_is_fp32=False,
                   t2=None, xsq_group=False):
    cks = tchunks(TT)
    if not xsq_group:
        p.phase()
        for k in range(DK):
            p.act(xsq_of(k), x_of(k), AF.Square)
            if x_is_fp32:
                p.dma("gpsimd", xbf_of(k), x_of(k))
    elif x_is_fp32:
        p.phase()
        for k in range(DK):
            p.dma("gpsimd", xbf_of(k), x_of(k))
    for g in range(0, len(cks), 2):
        sub = cks[g:g + 2]
        gbase = sub[0][0]
        if xsq_group:
            p.phase()
            for k in range(DK):
                gw = sum(n for (_o, n) in sub)
                p.act(xsq_of(k)[:, 0:gw],
                      x_of(k)[:, gbase:gbase + gw], AF.Square)
        p.phase()
        for i, (o, n) in enumerate(sub):
            lo = (o - gbase) if xsq_group else o
            for k in range(DK):
                p.mm(pp[2 * i][0:1, 0:n], ones_col[:, 0:1],
                     xbf_of(k)[:, o:o + n], start=(k == 0), stop=(k == DK - 1))
            for k in range(DK):
                p.mm(pp[2 * i + 1][0:1, 0:n], ones_col[:, 0:1],
                     xsq_of(k)[:, lo:lo + n], start=(k == 0),
                     stop=(k == DK - 1))
        p.phase()
        for i, (o, n) in enumerate(sub):
            p.cpy(lb.srow[0:1, o:o + n], pp[2 * i][0:1, 0:n])
            p.cpy(lb.srow[0:1, TT + o:TT + o + n], pp[2 * i + 1][0:1, 0:n])
    p.phase()
    p.ts(lb.mu[0:1, 0:TT], lb.srow[0:1, 0:TT], 1.0 / D, ALU.mult)
    p.ts(lb.srow[0:1, TT:2 * TT], lb.srow[0:1, TT:2 * TT], 1.0 / D, ALU.mult)
    p.phase()
    p.tt(lb.srow[0:1, 0:TT], lb.mu[0:1, 0:TT], lb.mu[0:1, 0:TT], ALU.mult)
    p.phase()
    p.tt(lb.srow[0:1, TT:2 * TT], lb.srow[0:1, TT:2 * TT], lb.srow[0:1, 0:TT],
         ALU.subtract)
    p.phase()
    p.ts(lb.srow[0:1, TT:2 * TT], lb.srow[0:1, TT:2 * TT], LN_EPS, ALU.add)
    p.phase()
    p.act(lb.srow[0:1, 0:TT], lb.srow[0:1, TT:2 * TT], AF.Ln)
    p.phase()
    p.act(lb.ab[0:1, 0:TT], lb.srow[0:1, 0:TT], AF.Exp, scale=-0.5)
    p.phase()
    p.stt(lb.ab[0:1, TT:2 * TT], lb.mu[0:1, 0:TT], -1.0, lb.ab[0:1, 0:TT],
          ALU.mult, ALU.mult)
    items = []
    for h in range(2):
        for (o, n) in cks:
            items.append((lb.ab[0:1, h * TT + o:h * TT + o + n],
                          lb.abc[:, h * TT + o:h * TT + o + n], n))
    emit_row_bcast(p, items, pp, ones_row)
    for k in range(DK):
        p.phase()
        src = xbf_of(k) if x_is_fp32 else x_of(k)
        p.tt(t2[:, 0:TT], src, lb.abc[:, 0:TT], ALU.mult)
        p.phase()
        if gamma_col is None:
            p.tt(out_of(k), t2[:, 0:TT], lb.abc[:, TT:2 * TT], ALU.add)
        else:
            p.tt(t2[:, 0:TT], t2[:, 0:TT], lb.abc[:, TT:2 * TT], ALU.add)
            p.phase()
            p.ts(out_of(k), t2[:, 0:TT], gamma_col[:, k:k + 1], ALU.mult,
                 s2=beta_col[:, k:k + 1], op2=ALU.add)


def load_col(p, dst, src_dram):
    for k in range(DK):
        p.dma("sync", dst[:, k:k + 1], src_dram[k * 128:(k + 1) * 128, 0:1])


class NS:
    def __init__(self, **kw):
        self.__dict__.update(kw)



def build_L01():
    p = Prog()
    x0e = p.din("x0e", [D, TE])
    a_log = p.din("a_log", [D, NST])
    bwT = p.din("bwT", [D, NST])
    cwT = p.din("cwT", [D, NST])
    d_vec = p.din("d_vec", [D, 1])
    gwT = p.din("gwT", [D, D])
    gate_b = p.din("gate_b", [D, 1])
    ln_g = p.din("ln_g", [D, 1])
    ln_b = p.din("ln_b", [D, 1])
    ones_c = p.din("ones_c", [1, 128])
    ones_col_in = p.din("ones_col", [128, 1])
    sel16 = p.din("sel16", [NST, NST * 128])
    x1_out = p.dout("x1", [D, T])
    if DBG:
        dbg_xn = p.dout("dbg_xn", [D, TE])
        dbg_yscan = p.dout("dbg_yscan", [D, T])
        dbg_y = p.dout("dbg_y", [D, T])

    pp = [p.psum(f"pp{i}", [128, 512]) for i in range(8)]
    A = Arena(p, 209536)

    F1o, B1o, B2o, B3o = 0, 65536, 99072, 132608
    YO = 166144
    F1 = [A.f32(F1o + k * T * 4, T) for k in range(DK)]
    B1 = [A.bf16(B1o + k * TE * 2, TE) for k in range(DK)]
    B2 = [A.bf16(B2o + k * TE * 2, TE) for k in range(DK)]
    B3 = [A.bf16(B3o + k * TE * 2, TE) for k in range(DK)]
    lb = NS(srow=A.f32(YO, 2 * TE, rows=1),
            mu=A.f32(YO + 16768, TE, rows=1),
            ab=A.bf16(YO + 25152, 2 * TE, rows=1),
            abc=A.bf16(YO + 33536, 2 * TE))
    bx = A.bf16(YO, TE, rows=NST)
    cx = A.bf16(YO + 4224, TE, rows=NST)
    brep = A.bf16(YO + 8448, TE)
    crep = A.bf16(YO + 12672, TE)
    sel_bf = A.bf16(YO + 16896, NST * 128, rows=NST)
    t2_ln1 = A.bf16(F1o, TE)
    t2_ln2 = A.bf16(B3o, T)
    gwm = [A.bf16(B3o, DK * 128), A.bf16(B3o + 2048, DK * 128)]
    gsm = A.bf16(B3o + 4096, T)
    sm1 = A.bf16(B3o + 8192, T)
    CO = YO + 41920
    a_sb = A.f32(CO, DK * NST)
    bw_sb = A.bf16(CO + 512, DK * NST)
    cw_sb = A.bf16(CO + 768, DK * NST)
    dcol = A.f32(CO + 1024, DK)
    gbcol = A.f32(CO + 1056, DK)
    lgcol = A.f32(CO + 1088, DK)
    lbcol = A.f32(CO + 1120, DK)
    onebf = A.bf16(CO + 1152, 128, rows=1)
    onecol = A.bf16(CO + 1440, 1)

    for k in range(DK):
        p.dma("gpsimd", B2[k], x0e[k * 128:(k + 1) * 128, :])
        p.dma("sync", a_sb[:, k * NST:(k + 1) * NST],
              a_log[k * 128:(k + 1) * 128, :])
        p.dma("gpsimd", bw_sb[:, k * NST:(k + 1) * NST],
              bwT[k * 128:(k + 1) * 128, :])
        p.dma("gpsimd", cw_sb[:, k * NST:(k + 1) * NST],
              cwT[k * 128:(k + 1) * 128, :])
    load_col(p, dcol, d_vec)
    load_col(p, gbcol, gate_b)
    load_col(p, lgcol, ln_g)
    load_col(p, lbcol, ln_b)
    p.dma("gpsimd", onebf[0:1, :], ones_c)
    p.dma("gpsimd", onecol[:, 0:1], ones_col_in)
    p.phase()
    for k in range(DK):
        p.act(a_sb[:, k * NST:(k + 1) * NST], a_sb[:, k * NST:(k + 1) * NST],
              AF.Exp)

    emit_layernorm(p, lb, TE,
                   x_of=lambda k: B2[k], xbf_of=lambda k: B2[k],
                   xsq_of=lambda k: B3[k], out_of=lambda k: B1[k],
                   pp=pp, ones_col=onecol, ones_row=onebf, t2=t2_ln1)

    if DBG:
        p.phase()
        for k in range(DK):
            p.dma("gpsimd", dbg_xn[k * 128:(k + 1) * 128, :], B1[k])
    p.phase()
    p.dma("gpsimd", sel_bf[:, :], sel16)
    cks = tchunks(TE)
    for (rows_w, dst) in ((bw_sb, bx), (cw_sb, cx)):
        p.phase()
        for i, (o, n) in enumerate(cks):
            for k in range(DK):
                p.mm(pp[i][0:NST, 0:n], rows_w[:, k * NST:(k + 1) * NST],
                     B1[k][:, o:o + n], start=(k == 0), stop=(k == DK - 1))
        p.phase()
        for i, (o, n) in enumerate(cks):
            p.cpy(dst[0:NST, o:o + n], pp[i][0:NST, 0:n])

    p.memset(A.f32(F1o, DK * T), 0.0, eng="gpsimd")
    for ns in range(NST):
        for (rows_src, dst) in ((bx, brep), (cx, crep)):
            p.phase()
            for i, (o, n) in enumerate(cks):
                p.mm(pp[i][0:128, 0:n], sel_bf[:, ns * 128:(ns + 1) * 128],
                     rows_src[0:NST, o:o + n])
            p.phase()
            for i, (o, n) in enumerate(cks):
                if i % 2 == 0:
                    p.cpy(dst[:, o:o + n], pp[i][0:128, 0:n])
                else:
                    p.act(dst[:, o:o + n], pp[i][0:128, 0:n], AF.Copy)
        p.phase()
        for k in range(DK):
            p.tt(B2[k], B1[k], brep[:, :], ALU.mult,
                 eng=("vector" if k % 2 == 0 else "gpsimd"))
        p.phase()
        for k in range(DK):
            p.scan(B3[k],
                   a_sb[:, k * NST + ns:k * NST + ns + 1].to_broadcast((128, TE)),
                   B2[k], 0.0, ALU.mult, ALU.add)
        p.phase()
        for k in range(DK):
            p.tt(B2[k][:, 0:T], B3[k][:, HALO:TE], crep[:, HALO:TE], ALU.mult,
                 eng=("vector" if k % 2 == 0 else "gpsimd"))
        p.phase()
        for k in range(DK):
            p.tt(F1[k], F1[k], B2[k][:, 0:T], ALU.add,
                 eng=("vector" if k % 2 == 1 else "gpsimd"))

    if DBG:
        p.phase()
        for k in range(DK):
            p.dma("sync", dbg_yscan[k * 128:(k + 1) * 128, :], F1[k])
    p.phase()
    for k in range(DK):
        p.stt(F1[k], B1[k][:, HALO:TE], dcol[:, k:k + 1], F1[k],
              ALU.mult, ALU.add)

    if DBG:
        p.phase()
        for k in range(DK):
            p.dma("sync", dbg_y[k * 128:(k + 1) * 128, :], F1[k])
    ct = tchunks(T)
    p.phase()
    for k in range(DK):
        p.dma("gpsimd", gwm[0][:, k * 128:(k + 1) * 128],
              gwT[k * 128:(k + 1) * 128, 0:128])
    for m in range(DK):
        p.phase()
        if m + 1 < DK:
            for k in range(DK):
                p.dma("gpsimd", gwm[(m + 1) % 2][:, k * 128:(k + 1) * 128],
                      gwT[k * 128:(k + 1) * 128, (m + 1) * 128:(m + 2) * 128])
        for i, (o, n) in enumerate(ct):
            for k in range(DK):
                p.mm(pp[i][0:128, 0:n], gwm[m % 2][:, k * 128:(k + 1) * 128],
                     B1[k][:, HALO + o:HALO + o + n],
                     start=(k == 0), stop=(k == DK - 1))
        p.phase()
        for i, (o, n) in enumerate(ct):
            p.act(gsm[:, o:o + n], pp[i][0:128, 0:n], AF.Sigmoid,
                  bias=gbcol[:, m:m + 1])
        p.phase()
        p.tt(sm1[:, :], F1[m], B1[m][:, HALO:TE], ALU.subtract)
        p.phase()
        p.tt(sm1[:, :], sm1[:, :], gsm[:, :], ALU.mult)
        p.phase()
        p.tt(F1[m], sm1[:, :], B1[m][:, HALO:TE], ALU.add)

    B2t = [A.bf16(B2o + k * T * 2, T) for k in range(DK)]
    B3t = [A.bf16(B3o + k * T * 2, T) for k in range(DK)]
    B1t = [A.bf16(B1o + k * T * 2, T) for k in range(DK)]
    emit_layernorm(p, lb, T,
                   x_of=lambda k: F1[k], xbf_of=lambda k: B2t[k],
                   xsq_of=lambda k: B3t[k], out_of=lambda k: B1t[k],
                   pp=pp, ones_col=onecol, ones_row=onebf,
                   gamma_col=lgcol, beta_col=lbcol, x_is_fp32=True,
                   t2=t2_ln2)

    p.phase()
    for k in range(DK):
        p.dma("sync", F1[k], x0e[k * 128:(k + 1) * 128, HALO:TE])
    p.phase()
    for k in range(DK):
        p.tt(F1[k], F1[k], B1t[k], ALU.add)
    p.phase()
    for k in range(DK):
        p.dma("sync", x1_out[k * 128:(k + 1) * 128, :], F1[k])
    return p.finish()




FH = 1536
TL = FH + T
QO = 1024
NQ = TL - QO
KO = 512
NK = TL - KO
DH = 64
INNER = 512
HIDF = 2730
NEG_BIG = -30000.0


def build_L0Fa(with_preln, upto=99):
    p = Prog()
    x1e = p.din("x1e", [D, TL])
    wqkv = p.din("wqkv", [D, 3 * INNER])
    wqrot = p.din("wqrot", [D, INNER])
    wkrot = p.din("wkrot", [D, INNER])
    wo_in = p.din("wo_in", [INNER, D])
    qs2_in = p.din("qs2", [128, 1])
    ks2_in = p.din("ks2", [128, 1])
    qs2p_in = p.din("qs2p", [128, 1])
    ks2p_in = p.din("ks2p", [128, 1])
    c1q_in = p.din("c1q", [128, NQ])
    c2q_in = p.din("c2q", [128, NQ])
    c1k_in = p.din("c1k", [128, NK])
    c2k_in = p.din("c2k", [128, NK])
    amask_in = p.din("amask", [8 * 128, 512])
    ident_in = p.din("ident", [128, 128])
    kvt_in = p.din("kvt", [128, 24])
    ones_c = p.din("ones_c", [1, 128])
    ones_col_in = p.din("ones_col", [128, 1])
    e2_in = p.din("e2", [128, 2])
    e2t_in = p.din("e2t", [2, 128])
    u_out = p.dout("u", [D, T + 1])
    if DBG2 and not DBG:
        dbg_bits = p.dout("dbg_bits", [512, 256])
    if DBG:
        dbg_xh = p.dout("dbg_xh", [D, TL])
        dbg_qh = p.dout("dbg_qh", [512, NQ])
        dbg_kh = p.dout("dbg_kh", [512, NK])
        dbg_nb = p.dout("dbg_nb", [128, 1])
        dbg_ao = p.dout("dbg_ao", [512, 512])
        dbg_sexp = p.dout("dbg_sexp", [8 * 1024, 512])
        dbg_den = p.dout("dbg_den", [16, 512])
        dbg_bits = p.dout("dbg_bits", [512, 256])
        dbg_av2 = p.dout("dbg_av2", [64, 512])
        dbg_tr = p.dout("dbg_tr", [8, 512])

    pp = [p.psum(f"pp{i}", [128, 512]) for i in range(8)]
    A = Arena(p, 210816)

    xh = [A.bf16(0 + k * TL * 2, TL) for k in range(DK)]
    qh = [A.bf16(57344 + i * NQ * 2, NQ) for i in range(4)]
    kh = [A.bf16(77824 + i * NK * 2, NK) for i in range(4)]
    vt = [A.bf16(102400 + i * 1024, 512) for i in range(24)]
    c1q = A.bf16(126976, NQ)
    c2q = A.bf16(126976 + 5120, NQ)
    c1k = A.bf16(126976 + 10240, NK)
    c2k = A.bf16(126976 + 16384, NK)
    masks = [A.bf16(149504 + i * 1024, 512) for i in range(8)]
    wvb = A.bf16(157696, 8 * 512)
    wob = A.bf16(157696, 4 * 1024)
    wmain = A.bf16(165888, DK * 128)
    wrot = A.bf16(167936, DK * 128)
    identb = A.bf16(169984, 128)
    identf = A.f32(170496, 128)
    qt = A.bf16(172032, NK)
    qrt = A.bf16(178176, NK)
    qsq = A.bf16(184320, NK)
    nsr = A.bf16(190464, NK, rows=2)
    inb = A.bf16(196608, NK)
    sexp = [A.bf16(172032 + i * 1024, 512) for i in range(8)]
    recb = A.f32(180224, 512, rows=64)
    drow = A.f32(182272, 512, rows=1)
    drow2 = A.f32(182272 + 2048, 512, rows=1)
    aow = [A.bf16(186368 + i * 1024, 512) for i in range(4)]
    av_sb = A.f32(190464, 512)
    av2_sb = A.f32(192512, 512)
    wst = [A.f32(194560, 512), A.f32(196608, 512)]
    x1buf = A.f32(126976, 8 * 512)
    ln_sq = [A.bf16(57344 + k * TL * 2, TL) for k in range(DK)]
    lnb = NS(srow=A.f32(126976, 2 * TL, rows=1),
             mu=A.f32(155648, TL, rows=1),
             ab=A.bf16(172032, 2 * TL, rows=1),
             abc=A.bf16(186368, 2 * TL))
    t2ln = A.bf16(200704, TL)
    qs2 = A.f32(207872, 1)
    ks2 = A.f32(207904, 1)
    qs2p = A.f32(210688, 1)
    ks2p = A.f32(210720, 1)
    onebf = A.bf16(207936, 128, rows=1)
    onecol = A.bf16(208192, 1)
    e2c = A.bf16(208224, 2)
    e2t = A.bf16(208256, 128, rows=2)
    nbcol = A.f32(208512, 1)
    kvt = A.bf16(208544, 24)
    babs = A.f32(208608, 1)
    brow = A.f32(208640, 384, rows=1)
    onef = A.f32(208640 + 1536, 128, rows=1)
    assert 208640 + 1536 + 512 <= 210752

    cq = tchunks(NQ)
    ck = tchunks(NK)

    for k in range(DK):
        p.dma("gpsimd", xh[k], x1e[k * 128:(k + 1) * 128, :])
    p.dma("sync", qs2[:, 0:1], qs2_in)
    p.dma("sync", ks2[:, 0:1], ks2_in)
    p.dma("sync", qs2p[:, 0:1], qs2p_in)
    p.dma("sync", ks2p[:, 0:1], ks2p_in)
    p.dma("gpsimd", onebf[0:1, :], ones_c)
    p.dma("sync", onef[0:1, :], ones_c)
    p.dma("gpsimd", onecol[:, 0:1], ones_col_in)
    p.dma("gpsimd", e2c[:, :], e2_in)
    p.dma("gpsimd", e2t[0:2, :], e2t_in)
    p.dma("gpsimd", kvt[:, :], kvt_in)

    if upto < 1:
        return p.finish()
    if with_preln:
        emit_layernorm(p, lnb, TL,
                       x_of=lambda k: xh[k], xbf_of=lambda k: xh[k],
                       xsq_of=lambda k: ln_sq[k],
                       out_of=lambda k: xh[k],
                       pp=pp, ones_col=onecol, ones_row=onebf, t2=t2ln)

    if DBG:
        p.phase()
        for k in range(DK):
            p.dma("gpsimd", dbg_xh[k * 128:(k + 1) * 128, :], xh[k])
    if upto < 2:
        return p.finish()
    p.phase()
    p.dma("gpsimd", identb[:, :], ident_in)
    p.dma("sync", identf[:, :], ident_in)
    for i in range(8):
        p.dma("gpsimd", masks[i], amask_in[i * 128:(i + 1) * 128, :])
    for k in range(DK):
        p.dma("gpsimd", wvb[:, k * 512:(k + 1) * 512],
              wqkv[k * 128:(k + 1) * 128, 2 * INNER:3 * INNER])
    p.dma("gpsimd", qt[:, 0:NQ], c1q_in)
    p.dma("gpsimd", qrt[:, 0:NQ], c2q_in)
    p.dma("gpsimd", qsq[:, 0:NK], c1k_in)
    p.dma("gpsimd", inb[:, 0:NK], c2k_in)
    p.phase()
    p.ts(c1q[:, :], qt[:, 0:NQ], qs2[:, 0:1], ALU.mult)
    p.ts(c2q[:, :], qrt[:, 0:NQ], qs2p[:, 0:1], ALU.mult)
    p.ts(c1k[:, :], qsq[:, 0:NK], ks2[:, 0:1], ALU.mult)
    p.ts(c2k[:, :], inb[:, 0:NK], ks2p[:, 0:1], ALU.mult)

    if upto < 3:
        return p.finish()
    dscq = p.nc.dram_tensor("dscq", [128, 1], F32, kind="Internal").ap()
    dsck = p.nc.dram_tensor("dsck", [128, 1], F32, kind="Internal").ap()
    p.phase()
    p.act(babs[:, 0:1], qs2[:, 0:1], AF.Abs)
    p.phase()
    p.dma("sync", dscq, babs[:, 0:1])
    p.phase()
    p.act(babs[:, 0:1], ks2[:, 0:1], AF.Abs)
    p.phase()
    p.dma("sync", dsck, babs[:, 0:1])
    p.phase()
    p.dma("sync", brow[0:1, 0:128], dscq.rearrange("p o -> o p"))
    p.dma("sync", brow[0:1, 128:256], dsck.rearrange("p o -> o p"))
    p.phase()
    p.red(brow[0:1, 256:257], brow[0:1, 0:128], ALU.max)
    p.red(brow[0:1, 257:258], brow[0:1, 128:256], ALU.max)
    p.phase()
    p.tt(brow[0:1, 258:259], brow[0:1, 256:257], brow[0:1, 257:258], ALU.mult)
    p.phase()
    p.ts(brow[0:1, 258:259], brow[0:1, 258:259], -8.0, ALU.mult)
    p.phase()
    p.mm(pp[1][0:128, 0:1], onef[0:1, 0:128], brow[0:1, 258:259])
    p.phase()
    p.cpy(nbcol[:, 0:1], pp[1][0:128, 0:1])

    def qk_build(pt, is_q):
        n = NQ if is_q else NK
        cks_ = cq if is_q else ck
        off = QO if is_q else KO
        dst = qh[pt] if is_q else kh[pt]
        c1 = c1q if is_q else c1k
        c2 = c2q if is_q else c2k
        p.phase()
        for i, (o, nn) in enumerate(cks_):
            for k in range(DK):
                rs = off + o - (1 if k >= 4 else 0)
                p.mm(pp[i][0:128, 0:nn], wmain[:, k * 128:(k + 1) * 128],
                     xh[k][:, rs:rs + nn], start=(k == 0), stop=(k == DK - 1))
        p.phase()
        for i, (o, nn) in enumerate(cks_):
            p.cpy(qt[:, o:o + nn], pp[i][0:128, 0:nn])
        p.phase()
        for i, (o, nn) in enumerate(cks_):
            p.act(qsq[:, o:o + nn], pp[i][0:128, 0:nn], AF.Square)
        p.phase()
        for i, (o, nn) in enumerate(cks_):
            for k in range(DK):
                rs = off + o - (1 if k >= 4 else 0)
                p.mm(pp[i][0:128, 0:nn], wrot[:, k * 128:(k + 1) * 128],
                     xh[k][:, rs:rs + nn], start=(k == 0), stop=(k == DK - 1))
        p.phase()
        for i, (o, nn) in enumerate(cks_):
            p.cpy(qrt[:, o:o + nn], pp[i][0:128, 0:nn])
        p.phase()
        for i, (o, nn) in enumerate(cks_):
            p.mm(pp[i][0:2, 0:nn], e2c[:, 0:2], qsq[:, o:o + nn])
        p.phase()
        for i, (o, nn) in enumerate(cks_):
            p.cpy(nsr[0:2, o:o + nn], pp[i][0:2, 0:nn])
        p.phase()
        p.ts(nsr[0:2, 0:n], nsr[0:2, 0:n], 1e-24, ALU.max)
        p.phase()
        p.act(nsr[0:2, 0:n], nsr[0:2, 0:n], AF.Ln)
        p.phase()
        p.act(nsr[0:2, 0:n], nsr[0:2, 0:n], AF.Exp, scale=-0.5)
        p.phase()
        for i, (o, nn) in enumerate(cks_):
            p.mm(pp[i][0:128, 0:nn], e2t[0:2, :], nsr[0:2, o:o + nn])
        p.phase()
        for i, (o, nn) in enumerate(cks_):
            p.cpy(inb[:, o:o + nn], pp[i][0:128, 0:nn])
        p.phase()
        p.tt(qt[:, 0:n], qt[:, 0:n], c1[:, 0:n], ALU.mult)
        p.tt(qrt[:, 0:n], qrt[:, 0:n], c2[:, 0:n], ALU.mult, eng="gpsimd")
        p.phase()
        p.tt(qt[:, 0:n], qt[:, 0:n], qrt[:, 0:n], ALU.add)
        p.phase()
        p.tt(dst, qt[:, 0:n], inb[:, 0:n], ALU.mult)

    if upto < 4:
        return p.finish()
    for pt in range(4):
        p.phase()
        for k in range(DK):
            p.dma("gpsimd", wmain[:, k * 128:(k + 1) * 128],
                  wqkv[k * 128:(k + 1) * 128, pt * 128:(pt + 1) * 128])
            p.dma("gpsimd", wrot[:, k * 128:(k + 1) * 128],
                  wqrot[k * 128:(k + 1) * 128, pt * 128:(pt + 1) * 128])
        qk_build(pt, True)
    for pt in range(4):
        p.phase()
        for k in range(DK):
            p.dma("gpsimd", wmain[:, k * 128:(k + 1) * 128],
                  wqkv[k * 128:(k + 1) * 128,
                       INNER + pt * 128:INNER + (pt + 1) * 128])
            p.dma("gpsimd", wrot[:, k * 128:(k + 1) * 128],
                  wkrot[k * 128:(k + 1) * 128, pt * 128:(pt + 1) * 128])
        qk_build(pt, False)

    if DBG:
        p.phase()
        for i in range(4):
            p.dma("gpsimd", dbg_qh[i * 128:(i + 1) * 128, :], qh[i])
            p.dma("gpsimd", dbg_kh[i * 128:(i + 1) * 128, :], kh[i])
        p.dma("sync", dbg_nb, nbcol[:, 0:1])
    if upto < 5:
        return p.finish()
    for grp in range(3):
        p.phase()
        for j in range(8):
            ti = grp * 8 + j
            tok0 = KO + ti * 128
            for k in range(DK):
                sh = tok0 - (1 if k >= 4 else 0)
                p.mm(pp[j][0:128, 0:512], xh[k][:, sh:sh + 128],
                     wvb[:, k * 512:(k + 1) * 512],
                     start=(k == 0), stop=(k == DK - 1))
        p.phase()
        for j in range(8):
            p.cpy(vt[grp * 8 + j], pp[j][0:128, 0:512])

    p.phase()
    for kt in range(4):
        p.dma("gpsimd", wob[:, kt * 1024:(kt + 1) * 1024],
              wo_in[kt * 128:(kt + 1) * 128, :])

    if upto < 6:
        return p.finish()
    for w in range(5):
        for h in range(8):
            pt, rb = h // 2, (h % 2) * 64
            p.phase()
            for kc in range(8):
                kcol = QO + (w - 1) * 512 + kc * 128 - KO
                p.mm(pp[kc][0:128, 0:512],
                     kh[pt][rb:rb + 64, kcol:kcol + 128],
                     qh[pt][rb:rb + 64, w * 512:(w + 1) * 512],
                     start=True, stop=False)
                p.mm(pp[kc][0:128, 0:512], identb[:, :], masks[kc],
                     start=False, stop=True)
            p.phase()
            for kc in range(8):
                p.act(sexp[kc], pp[kc][0:128, 0:512], AF.Exp,
                      bias=nbcol[:, 0:1])
            p.phase()
            for kc in range(8):
                ti = 4 * w + kc
                p.mm(pp[0][0:64, 0:512], vt[ti][:, h * 64:(h + 1) * 64],
                     sexp[kc], start=(kc == 0), stop=(kc == 7))
            for kc in range(8):
                ti = 4 * w + kc
                p.mm(pp[1][0:1, 0:512], kvt[:, ti:ti + 1], sexp[kc],
                     start=(kc == 0), stop=(kc == 7))
            p.phase()
            p.ts(drow[0:1, 0:512], pp[1][0:1, 0:512], 1e-30, ALU.max)
            p.phase()
            p.act(drow2[0:1, 0:512], drow[0:1, 0:512], AF.Ln)
            p.phase()
            p.act(drow2[0:1, 0:512], drow2[0:1, 0:512], AF.Exp, scale=-1.0)
            p.phase()
            p.mm(pp[2][0:64, 0:512], onef[0:1, 0:64], drow2[0:1, 0:512])
            p.phase()
            p.cpy(recb[0:64, 0:512], pp[2][0:64, 0:512])
            if DBG and w == 1:
                for kc in range(8):
                    p.dma("gpsimd",
                          dbg_sexp[h * 1024 + kc * 128:h * 1024 + (kc + 1) * 128, :],
                          sexp[kc])
                p.dma("sync", dbg_den[2 * h:2 * h + 1, :], drow[0:1, 0:512])
                p.dma("sync", dbg_den[2 * h + 1:2 * h + 2, :], drow2[0:1, 0:512])
            p.act(av_sb[0:64, 0:512], pp[0][0:64, 0:512], AF.Copy)
            p.phase()
            p.tt(av2_sb[0:64, 0:512], av_sb[0:64, 0:512],
                 recb[0:64, 0:512], ALU.mult)
            p.phase()
            p.act(aow[pt][rb:rb + 64, 0:512], av2_sb[0:64, 0:512], AF.Copy)
            if DBG and w == 1 and h == 0:
                p.dma("sync", dbg_av2, av2_sb[0:64, 0:512])
            if DBG and w == 1:
                p.phase()
                p.dma("gpsimd", dbg_tr[h:h + 1, :], aow[0][0:1, 0:512])
        if (DBG or DBG2) and w == 1:
            p.phase()
            for i in range(4):
                p.dma("sync", dbg_ao[i * 128:(i + 1) * 128, :].bitcast(BF16) if False else dbg_bits[i * 128:(i + 1) * 128, :], aow[i].bitcast(F32))
        lo = QO + 512 * w
        p.phase()
        for m in range(DK):
            for kt in range(4):
                p.mm(pp[m][0:128, 0:512],
                     wob[:, kt * 1024 + m * 128:kt * 1024 + (m + 1) * 128],
                     aow[kt], start=(kt == 0), stop=(kt == 3))
            if not with_preln:
                p.dma("sync", x1buf[:, m * 512:(m + 1) * 512],
                      x1e[m * 128:(m + 1) * 128, lo:lo + 512])
        for m in range(DK):
            wb_ = wst[m % 2]
            p.phase()
            if with_preln:
                p.tt(wb_[:, 0:512], pp[m][0:128, 0:512],
                     xh[m][:, lo:lo + 512], ALU.add)
            else:
                p.tt(wb_[:, 0:512], pp[m][0:128, 0:512],
                     x1buf[:, m * 512:(m + 1) * 512], ALU.add)
            p.phase()
            if w == 0:
                p.dma("sync", u_out[m * 128:(m + 1) * 128, 0:1],
                      wb_[:, 511:512], slow=True)
            else:
                p.dma("sync", u_out[m * 128:(m + 1) * 128,
                                    512 * (w - 1) + 1:512 * w + 1],
                      wb_[:, 0:512])
    return p.finish()


def build_L0Fb(add_residual_x, chain_faq=False):
    TT2 = T + 1
    MC = (HIDF + 127) // 128
    p = Prog()
    ue = p.din("ue", [D, TT2])
    w1a_in = p.din("w1a", [D, HIDF])
    w1g_in = p.din("w1g", [D, HIDF])
    b1a_in = p.din("b1a", [HIDF, 1])
    b1g_in = p.din("b1g", [HIDF, 1])
    w2_in = p.din("w2", [HIDF, D])
    b2_in = p.din("b2", [D, 1])
    lg_in = p.din("lg", [D, 1])
    lbt_in = p.din("lbt", [D, 1])
    ones_c = p.din("ones_c", [1, 128])
    ones_col_in = p.din("ones_col", [128, 1])
    if add_residual_x:
        x1own = p.din("x1own", [D, T])
    v_out = p.dout("v", [D, T])

    pp = [p.psum(f"pp{i}", [128, 512]) for i in range(8)]
    A = Arena(p, 209536)
    bu = Bump(209536)

    ub = [A.bf16(bu.take(TT2 * 2), TT2) for _ in range(DK)]
    xn = [A.bf16(bu.take(T * 2), T) for _ in range(DK)]
    h2m = bu.mark()
    h2 = [A.bf16(bu.take(1024 * 2), 1024) for _ in range(MC)]
    xsq = [A.bf16(h2m + ((k * T * 2 + 63) // 64) * 64, T) for k in range(DK)]
    lb = NS(srow=A.f32(bu.take(2 * T * 4), 2 * T, rows=1),
            mu=A.f32(bu.take(T * 4), T, rows=1),
            ab=A.bf16(bu.take(2 * T * 2), 2 * T, rows=1),
            abc=A.bf16(bu.take(2 * T * 2), 2 * T))
    t2ln = A.bf16(bu.take(T * 2), T)
    wsa = [A.bf16(bu.take(DK * 128 * 2), DK * 128) for _ in range(2)]
    wsg = [A.bf16(bu.take(DK * 128 * 2), DK * 128) for _ in range(2)]
    ws2 = [A.bf16(bu.take(MC * 128 * 2), MC * 128) for _ in range(2)]
    b1acol = A.f32(bu.take(MC * 4), MC)
    b1gcol = A.f32(bu.take(MC * 4), MC)
    b2col = A.f32(bu.take(DK * 4), DK)
    lgcol = A.f32(bu.take(DK * 4), DK)
    lbcol = A.f32(bu.take(DK * 4), DK)
    onebf = A.bf16(bu.take(256), 128, rows=1)
    onecol = A.bf16(bu.take(4), 1)
    ares = [A.f32(bu.take(512 * 4), 512) for _ in range(2)]
    wst = [A.f32(bu.take(512 * 4), 512) for _ in range(2)]
    gsc = A.bf16(bu.take(1024 * 2), 1024)

    def ush(k):
        return ub[k][:, 1:TT2] if k < 4 else ub[k][:, 0:T]

    for k in range(DK):
        p.dma("gpsimd", ub[k], ue[k * 128:(k + 1) * 128, :])
    for k in range(DK):
        p.dma("sync", b2col[:, k:k + 1], b2_in[k * 128:(k + 1) * 128, 0:1])
        p.dma("sync", lgcol[:, k:k + 1], lg_in[k * 128:(k + 1) * 128, 0:1])
        p.dma("sync", lbcol[:, k:k + 1], lbt_in[k * 128:(k + 1) * 128, 0:1])
    for m in range(MC):
        rm = min(128, HIDF - m * 128)
        p.dma("sync", b1acol[0:rm, m:m + 1], b1a_in[m * 128:m * 128 + rm, 0:1])
        p.dma("sync", b1gcol[0:rm, m:m + 1], b1g_in[m * 128:m * 128 + rm, 0:1])
    p.dma("gpsimd", onebf[0:1, :], ones_c)
    p.dma("gpsimd", onecol[:, 0:1], ones_col_in)

    emit_layernorm(p, lb, T,
                   x_of=ush, xbf_of=ush, xsq_of=lambda k: xsq[k],
                   out_of=lambda k: xn[k],
                   pp=pp, ones_col=onecol, ones_row=onebf,
                   gamma_col=lgcol, beta_col=lbcol, t2=t2ln)

    ct = tchunks(T)
    for half in range(2):
        hc = [(o - 1024 * half, n, o) for (o, n) in ct[2 * half:2 * half + 2]]
        p.phase()
        for k in range(DK):
            p.dma("gpsimd", wsa[0][:, k * 128:(k + 1) * 128],
                  w1a_in[k * 128:(k + 1) * 128, 0:128])
            p.dma("gpsimd", wsg[0][:, k * 128:(k + 1) * 128],
                  w1g_in[k * 128:(k + 1) * 128, 0:128])
        for m in range(MC):
            rm = min(128, HIDF - m * 128)
            p.phase()
            if m + 1 < MC:
                rm2 = min(128, HIDF - (m + 1) * 128)
                for k in range(DK):
                    p.dma("gpsimd", wsa[(m + 1) % 2][:, k * 128:k * 128 + rm2],
                          w1a_in[k * 128:(k + 1) * 128,
                                 (m + 1) * 128:(m + 1) * 128 + rm2])
                    p.dma("gpsimd", wsg[(m + 1) % 2][:, k * 128:k * 128 + rm2],
                          w1g_in[k * 128:(k + 1) * 128,
                                 (m + 1) * 128:(m + 1) * 128 + rm2])
            for i, (lo, n, o) in enumerate(hc):
                for k in range(DK):
                    p.mm(pp[2 * i][0:rm, 0:n],
                         wsa[m % 2][:, k * 128:k * 128 + rm],
                         xn[k][:, o:o + n], start=(k == 0), stop=(k == DK - 1))
                for k in range(DK):
                    p.mm(pp[2 * i + 1][0:rm, 0:n],
                         wsg[m % 2][:, k * 128:k * 128 + rm],
                         xn[k][:, o:o + n], start=(k == 0), stop=(k == DK - 1))
            p.phase()
            for i, (lo, n, o) in enumerate(hc):
                p.act(gsc[0:rm, lo:lo + n], pp[2 * i + 1][0:rm, 0:n],
                      AF.Gelu, bias=b1gcol[0:rm, m:m + 1])
                p.ts(h2[m][0:rm, lo:lo + n], pp[2 * i][0:rm, 0:n],
                     b1acol[0:rm, m:m + 1], ALU.add)
            p.phase()
            p.tt(h2[m][0:rm, 0:1024], h2[m][0:rm, 0:1024], gsc[0:rm, 0:1024],
                 ALU.mult)
        p.phase()
        for kt in range(MC):
            rk = min(128, HIDF - kt * 128)
            p.dma("gpsimd", ws2[0][0:rk, kt * 128:kt * 128 + 128],
                  w2_in[kt * 128:kt * 128 + rk, 0:128])
        for m2 in range(DK):
            p.phase()
            if m2 + 1 < DK:
                for kt in range(MC):
                    rk = min(128, HIDF - kt * 128)
                    p.dma("gpsimd",
                          ws2[(m2 + 1) % 2][0:rk, kt * 128:kt * 128 + 128],
                          w2_in[kt * 128:kt * 128 + rk,
                                (m2 + 1) * 128:(m2 + 2) * 128])
            for i, (lo, n, o) in enumerate(hc):
                for kt in range(MC):
                    rk = min(128, HIDF - kt * 128)
                    p.mm(pp[i][0:128, 0:n],
                         ws2[m2 % 2][0:rk, kt * 128:kt * 128 + 128],
                         h2[kt][0:rk, lo:lo + n],
                         start=(kt == 0), stop=(kt == MC - 1))
                p.dma("sync", ares[i][:, 0:n],
                      ue[m2 * 128:(m2 + 1) * 128, 1 + o:1 + o + n])
            p.phase()
            for i, (lo, n, o) in enumerate(hc):
                p.ts(wst[i][:, 0:n], pp[i][0:128, 0:n], b2col[:, m2:m2 + 1],
                     ALU.add)
            p.phase()
            for i, (lo, n, o) in enumerate(hc):
                p.tt(wst[i][:, 0:n], wst[i][:, 0:n], ares[i][:, 0:n], ALU.add)
            if add_residual_x:
                p.phase()
                for i, (lo, n, o) in enumerate(hc):
                    p.dma("sync", ares[i][:, 0:n],
                          x1own[m2 * 128:(m2 + 1) * 128, o:o + n])
                p.phase()
                for i, (lo, n, o) in enumerate(hc):
                    p.tt(wst[i][:, 0:n], wst[i][:, 0:n], ares[i][:, 0:n],
                         ALU.add)
            p.phase()
            for i, (lo, n, o) in enumerate(hc):
                p.dma("sync", v_out[m2 * 128:(m2 + 1) * 128, o:o + n],
                      wst[i][:, 0:n])
    if chain_faq:
        build_FAq(1, p=p, A=A, pp=pp, x_src=v_out, pre="q1_")
    return p.finish()



def build_FAq(blk, p=None, A=None, pp=None, x_src=None, pre=""):
    has_bias = blk == 1
    has_g2 = blk == 2
    qsc = 1.0 if blk == 1 else 0.125
    standalone = p is None
    if standalone:
        p = Prog()
        x_in = p.din("x_in", [D, T])
    else:
        x_in = x_src
    wq_in = p.din(pre + "wq", [D, D])
    wk_in = p.din(pre + "wk", [D, D])
    wv_in = p.din(pre + "wv", [D, D])
    ones_c = p.din(pre + "ones_c", [1, 128])
    ones_col_in = p.din(pre + "ones_col", [128, 1])
    if has_bias:
        bqk_in = p.din(pre + "bqk", [2 * D, 1])
        bv_in = p.din(pre + "bv", [1, D])
    if has_g2:
        g2_in = p.din(pre + "g2", [D, 1])
        b2_in = p.din(pre + "b2", [D, 1])
    q_out = p.dout(pre + "q", [D, T], BF16)
    k_out = p.dout(pre + "k", [D, T], BF16)
    v_out = p.dout(pre + "v", [T, D], BF16)

    if standalone:
        pp = [p.psum(f"pp{i}", [128, 512]) for i in range(8)]
        A = Arena(p, 209536)
    else:
        p.phase()
    bu = Bump(209536)
    xb = [A.bf16(bu.take(T * 2), T) for _ in range(DK)]
    xn = [A.bf16(bu.take(T * 2), T) for _ in range(DK)]
    xsq = [A.bf16(bu.take(T * 2), T) for _ in range(DK)]
    lb = NS(srow=A.f32(bu.take(2 * T * 4), 2 * T, rows=1),
            mu=A.f32(bu.take(T * 4), T, rows=1),
            ab=A.bf16(bu.take(2 * T * 2), 2 * T, rows=1),
            abc=A.bf16(bu.take(2 * T * 2), 2 * T))
    t2ln = A.bf16(bu.take(T * 2), T)
    ws = [A.bf16(bu.take(DK * 128 * 2), DK * 128) for _ in range(2)]
    wvb = [A.bf16(bu.take(D * 2), D) for _ in range(DK)]
    qkbuf = A.bf16(bu.take(T * 2), T)
    vbt = [A.bf16(bu.take(512 * 2), 512) for _ in range(8)]
    bqkc = A.f32(bu.take(16 * 4), 16)
    bvrow = A.f32(bu.take(D * 4), D, rows=1)
    bvc = A.f32(bu.take(D * 4), D)
    g2c = A.f32(bu.take(DK * 4), DK)
    b2c = A.f32(bu.take(DK * 4), DK)
    onebf = A.bf16(bu.take(256), 128, rows=1)
    onef = A.f32(bu.take(512), 128, rows=1)
    onecol = A.bf16(bu.take(4), 1)

    for k in range(DK):
        p.dma("gpsimd", xb[k], x_in[k * 128:(k + 1) * 128, :])
    p.dma("gpsimd", onebf[0:1, :], ones_c)
    p.dma("sync", onef[0:1, :], ones_c)
    p.dma("gpsimd", onecol[:, 0:1], ones_col_in)
    if has_bias:
        for m in range(16):
            p.dma("sync", bqkc[:, m:m + 1], bqk_in[m * 128:(m + 1) * 128, 0:1])
        p.dma("sync", bvrow[0:1, :], bv_in)
        emit_row_bcast(p, [(bvrow[0:1, 0:512], bvc[:, 0:512], 512),
                           (bvrow[0:1, 512:1024], bvc[:, 512:1024], 512)],
                       pp, onef)
    if has_g2:
        load_col(p, g2c, g2_in)
        load_col(p, b2c, b2_in)

    emit_layernorm(p, lb, T,
                   x_of=lambda k: xb[k], xbf_of=lambda k: xb[k],
                   xsq_of=lambda k: xsq[k], out_of=lambda k: xn[k],
                   pp=pp, ones_col=onecol, ones_row=onebf, t2=t2ln)
    if has_g2:
        emit_layernorm(p, lb, T,
                       x_of=lambda k: xn[k], xbf_of=lambda k: xn[k],
                       xsq_of=lambda k: xsq[k], out_of=lambda k: xn[k],
                       pp=pp, ones_col=onecol, ones_row=onebf,
                       gamma_col=g2c, beta_col=b2c, t2=t2ln)

    ct = tchunks(T)
    for m in range(16):
        is_q = m < 8
        mm_ = m % 8
        w_in = wq_in if is_q else wk_in
        p.phase()
        for k in range(DK):
            p.dma("gpsimd", ws[m % 2][:, k * 128:(k + 1) * 128],
                  w_in[k * 128:(k + 1) * 128, mm_ * 128:(mm_ + 1) * 128])
        p.phase()
        for i, (o, n) in enumerate(ct):
            for k in range(DK):
                p.mm(pp[i][0:128, 0:n], ws[m % 2][:, k * 128:(k + 1) * 128],
                     xn[k][:, o:o + n], start=(k == 0), stop=(k == DK - 1))
        p.phase()
        for i, (o, n) in enumerate(ct):
            if has_bias:
                p.ts(qkbuf[:, o:o + n], pp[i][0:128, 0:n],
                     bqkc[:, m:m + 1], ALU.add)
            elif is_q:
                p.ts(qkbuf[:, o:o + n], pp[i][0:128, 0:n], qsc, ALU.mult)
            else:
                p.cpy(qkbuf[:, o:o + n], pp[i][0:128, 0:n])
        p.phase()
        dst = q_out if is_q else k_out
        p.dma("gpsimd", dst[mm_ * 128:(mm_ + 1) * 128, :], qkbuf[:, 0:T])

    p.phase()
    for k in range(DK):
        p.dma("gpsimd", wvb[k], wv_in[k * 128:(k + 1) * 128, :])
    npp = min(8, len(pp))
    tiles = [(half, ti) for half in range(2) for ti in range(16)]
    for g0_ in range(0, len(tiles), npp):
        sub = tiles[g0_:g0_ + npp]
        p.phase()
        for j, (half, ti) in enumerate(sub):
            tt0 = ti * 128
            for k in range(DK):
                p.mm(pp[j][0:128, 0:512], xn[k][:, tt0:tt0 + 128],
                     wvb[k][:, half * 512:(half + 1) * 512],
                     start=(k == 0), stop=(k == DK - 1))
        p.phase()
        for j, (half, ti) in enumerate(sub):
            if has_bias:
                p.tt(vbt[j][0:128, 0:512], pp[j][0:128, 0:512],
                     bvc[:, half * 512:(half + 1) * 512], ALU.add)
            else:
                p.cpy(vbt[j][0:128, 0:512], pp[j][0:128, 0:512])
        p.phase()
        for j, (half, ti) in enumerate(sub):
            tt0 = ti * 128
            p.dma("gpsimd", v_out[tt0:tt0 + 128,
                                  half * 512:(half + 1) * 512],
                  vbt[j][0:128, 0:512])
    if standalone:
        return p.finish()
    return None


def build_FAatt1():
    p = Prog()
    nc = p.nc
    qb_in = p.din("qb", [D, T], BF16)
    ko_in = p.din("ko", [D, SEQ], BF16)
    vtl_in = p.din("vtl", [16 * 128, 32 * 65], BF16)
    x3_in = p.din("x3o", [D, T])
    wp_in = p.din("wp", [D, D])
    bp_in = p.din("bp", [D, 1])
    temp_in = p.din("temp", [1, 1])
    r2c_in = p.din("r2c", [16, 2560])
    ones_c = p.din("ones_c", [1, 128])
    ones_col_in = p.din("ones_col", [128, 1])
    x4_out = p.dout("x4", [D, T])
    xn4_out = p.dout("xn4", [D, T], BF16)
    pmax_out = p.dout("pmax", [1, 1])
    r3d = nc.dram_tensor("r3d", [16, 2560], BF16, kind="Internal")
    scq = nc.dram_tensor("scq", [128, 1], F32, kind="Internal").ap()

    pp = [p.psum(f"pp{i}", [128, 512]) for i in range(8)]
    A = Arena(p, 209536)
    qb = [A.bf16(k * 4096, T) for k in range(DK)]
    Kb = [A.bf16(32768 + i * 8192, SEQ) for i in range(2)]
    Vb = [A.bf16(49152 + i * 8320, 2 * 32 * 65) for i in range(2)]
    sexp = [A.bf16(65792 + i * 1024, 512) for i in range(32)]
    ao = [A.bf16(98560 + k * 4096, T) for k in range(DK)]
    wpb = [A.bf16(131328 + k * 2048, DK * 128) for k in range(DK)]
    SC = 147712
    av_sb = A.f32(SC, 512)
    av2 = A.f32(SC + 2048, 512)
    recb = A.f32(SC + 4096, 512, rows=64)
    hk = [A.bf16(SC + 6144 + i * 1024, 512) for i in range(2)]
    dr2 = A.t[64:65, SC + 8192:SC + 8192 + 2048].bitcast(F32)
    onef64 = A.t[64:65, SC + 10240:SC + 10240 + 512].bitcast(F32)
    wst = [A.f32(SC + 10752, 512), A.f32(SC + 12800, 512)]
    xres = A.f32(SC + 14848, T)
    r2s = A.f32(170752, 2560, rows=16)
    r3f = A.f32(181056, 2560, rows=16)
    r3b = A.bf16(191296, 2560, rows=16)
    SM = 196416
    onebf = A.bf16(SM, 128, rows=1)
    onef = A.f32(SM + 256, 128, rows=1)
    tcell = A.f32(SM + 768, 4, rows=1)
    invccol = A.f32(SM + 784, 1)
    invc16 = A.f32(SM + 788, 1, rows=16)
    bpcol = A.f32(SM + 792, DK)
    pcol = A.f32(SM + 832, 2)
    brow = A.f32(SM + 840, 132, rows=1)
    onecol = A.bf16(SM + 1376, 1)
    pcol8 = A.f32(SM + 1440, 8)
    assert SM + 1440 + 32 <= 209536
    x4f = [A.f32(k * 8192, T) for k in range(DK)]
    xbf = [A.bf16(98560 + k * 4096, T) for k in range(DK)]
    xsq2 = [A.bf16(65792 + k * 4096, T) for k in range(DK)]
    LB = 131328
    lb = NS(srow=A.f32(LB, 2 * T, rows=1),
            mu=A.f32(LB + 16384, T, rows=1),
            ab=A.bf16(LB + 24576, 2 * T, rows=1),
            abc=A.bf16(LB + 32768, 2 * T))
    t2ln = A.bf16(172288, T)
    xnb = [A.bf16(k * 4096, T) for k in range(DK)]

    for k in range(DK):
        p.dma("sync", qb[k], qb_in[k * 128:(k + 1) * 128, :])
        p.dma("gpsimd", wpb[k], wp_in[k * 128:(k + 1) * 128, :])
    p.dma("sync", r2s[0:16, :], r2c_in)
    p.dma("sync", tcell[0:1, 0:1], temp_in)
    p.dma("gpsimd", onebf[0:1, :], ones_c)
    p.dma("sync", onef[0:1, :], ones_c)
    p.dma("sync", onef64[0:1, 0:64], ones_c[0:1, 0:64])
    p.dma("gpsimd", onecol[:, 0:1], ones_col_in)
    load_col(p, bpcol, bp_in)
    p.phase()
    p.ts(tcell[0:1, 0:1], tcell[0:1, 0:1], 0.1, ALU.max)
    p.phase()
    p.ts(tcell[0:1, 0:1], tcell[0:1, 0:1], 10.0, ALU.min)
    p.phase()
    p.ts(tcell[0:1, 0:1], tcell[0:1, 0:1], 32.0, ALU.mult)
    p.phase()
    p.act(tcell[0:1, 1:2], tcell[0:1, 0:1], AF.Ln)
    p.phase()
    p.act(tcell[0:1, 2:3], tcell[0:1, 1:2], AF.Exp, scale=-1.0)
    p.phase()
    p.mm(pp[0][0:128, 0:1], onef[0:1, 0:128], tcell[0:1, 2:3])
    p.mm(pp[1][0:16, 0:1], onef[0:1, 0:16], tcell[0:1, 2:3])
    p.phase()
    p.cpy(invccol[:, 0:1], pp[0][0:128, 0:1])
    p.cpy(invc16[0:16, 0:1], pp[1][0:16, 0:1])
    p.phase()
    for k in range(DK):
        p.ts(qb[k], qb[k], invccol[:, 0:1], ALU.mult,
             eng=("vector" if k % 2 == 0 else "gpsimd"))
    p.phase()
    p.ts(r3f[0:16, :], r2s[0:16, :], r2s[0:16, 2558:2559], ALU.subtract)
    p.phase()
    p.ts(r3f[0:16, :], r3f[0:16, :], invc16[0:16, 0:1], ALU.mult)
    p.phase()
    p.act(r3b[0:16, :], r3f[0:16, :], AF.Exp)
    p.phase()
    p.dma("sync", r3d.ap(), r3b[0:16, :])

    p.phase()
    p.dma("sync", Kb[0], ko_in[0:128, :])
    p.dma("sync", Vb[0][:, 0:2080], vtl_in[0:128, :])
    p.dma("sync", Vb[0][:, 2080:4160], vtl_in[128:256, :])
    for m in range(DK):
        first = True
        for hl in range(2):
            rb = hl * 64
            for tqc in range(4):
                i0 = tqc * 512
                for g in range(4):
                    p.phase()
                    if first and m + 1 < DK:
                        nm = m + 1
                        p.dma("sync", Kb[nm % 2],
                              ko_in[nm * 128:(nm + 1) * 128, :])
                        p.dma("sync", Vb[nm % 2][:, 0:2080],
                              vtl_in[2 * nm * 128:(2 * nm + 1) * 128, :])
                        p.dma("sync", Vb[nm % 2][:, 2080:4160],
                              vtl_in[(2 * nm + 1) * 128:(2 * nm + 2) * 128, :])
                    first = False
                    for j in range(8):
                        kt = g * 8 + j
                        p.mm(pp[j][0:128, 0:512],
                             Kb[m % 2][rb:rb + 64, kt * 128:(kt + 1) * 128],
                             qb[m][rb:rb + 64, i0:i0 + 512])
                    p.phase()
                    for j in range(8):
                        kt = g * 8 + j
                        p.act(sexp[kt], pp[j][0:128, 0:512], AF.Exp)
                kts = [kt for kt in range(32) if kt * 128 - i0 >= 2048]
                for vi in range(len(kts) + 1):
                    p.phase()
                    if vi < len(kts):
                        kt = kts[vi]
                        V0 = i0 - kt * 128 + 3968
                        h = 2 * m + hl
                        p.dma("sync", hk[vi % 2],
                              bass.AP(r3d, h * 2560 + V0, [[1, 128], [1, 512]]))
                    if vi > 0:
                        kp = kts[vi - 1]
                        p.tt(sexp[kp], sexp[kp], hk[(vi - 1) % 2], ALU.mult)
                p.phase()
                for kt in range(32):
                    p.mm(pp[0][0:65, 0:512],
                         Vb[m % 2][:, hl * 2080 + kt * 65:hl * 2080 + (kt + 1) * 65],
                         sexp[kt], start=(kt == 0), stop=(kt == 31))
                p.phase()
                p.cpy(av_sb[0:65, :], pp[0][0:65, 0:512])
                p.phase()
                p.ts(av_sb[64:65, :], av_sb[64:65, :], 1e-30, ALU.max)
                p.phase()
                p.act(av_sb[64:65, :], av_sb[64:65, :], AF.Ln)
                p.phase()
                p.act(dr2[0:1, 0:512], av_sb[64:65, :], AF.Exp, scale=-1.0)
                p.phase()
                p.mm(pp[2][0:64, 0:512], onef64[0:1, 0:64], dr2[0:1, 0:512])
                p.phase()
                p.cpy(recb[0:64, :], pp[2][0:64, 0:512])
                p.phase()
                p.tt(av2[0:64, :], av_sb[0:64, :], recb[0:64, :], ALU.mult)
                p.phase()
                p.act(ao[m][rb:rb + 64, i0:i0 + 512], av2[0:64, :], AF.Copy)

    for mo in range(DK):
        p.phase()
        p.dma("sync", xres[:, :], x3_in[mo * 128:(mo + 1) * 128, :])
        for i, (o, n) in enumerate(tchunks(T)):
            for k in range(DK):
                p.mm(pp[i][0:128, 0:n],
                     wpb[k][:, mo * 128:(mo + 1) * 128],
                     ao[k][:, o:o + n], start=(k == 0), stop=(k == DK - 1))
        p.phase()
        for i, (o, n) in enumerate(tchunks(T)):
            p.ts(x4f[mo][:, o:o + n], pp[i][0:128, 0:n],
                 bpcol[:, mo:mo + 1], ALU.add)
        p.phase()
        p.tt(x4f[mo], x4f[mo], xres[:, :], ALU.add)
        p.phase()
        p.dma("sync", x4_out[mo * 128:(mo + 1) * 128, :], x4f[mo])

    emit_layernorm(p, lb, T,
                   x_of=lambda k: x4f[k], xbf_of=lambda k: xbf[k],
                   xsq_of=lambda k: xsq2[k], out_of=lambda k: xnb[k],
                   pp=pp, ones_col=onecol,
                   ones_row=onebf, x_is_fp32=True, t2=t2ln)
    p.phase()
    for k in range(DK):
        p.dma("gpsimd", xn4_out[k * 128:(k + 1) * 128, :], xnb[k])
        p.red(pcol8[:, k:k + 1], xnb[k], ALU.max, absval=True)
    p.phase()
    p.red(pcol[:, 0:1], pcol8[:, 0:8], ALU.max)
    p.phase()
    p.dma("sync", scq, pcol[:, 0:1])
    p.phase()
    p.dma("sync", brow[0:1, 0:128], scq.rearrange("p o -> o p"))
    p.phase()
    p.red(brow[0:1, 128:129], brow[0:1, 0:128], ALU.max)
    p.phase()
    p.dma("sync", pmax_out, brow[0:1, 128:129])
    return p.finish()


def stage_L01(inputs, core):
    b, h = core // 2, core % 2
    x = inputs["L0_attn_x"]
    s0 = h * T
    xe = np.zeros((TE, D), np.float32)
    if h == 0:
        xe[HALO:] = x[b, 0:T]
    else:
        xe[:] = x[b, s0 - HALO:s0 + T]
    sel = np.zeros((NST, NST * 128), np.float32)
    for n in range(NST):
        sel[n, n * 128:(n + 1) * 128] = 1.0
    return {
        "x0e": np.ascontiguousarray(xe.T),
        "a_log": inputs["L0_attn_A_log"],
        "bwT": np.ascontiguousarray(inputs["L0_attn_B_w"].T),
        "cwT": np.ascontiguousarray(inputs["L0_attn_C_w"].T),
        "d_vec": inputs["L0_attn_D"].reshape(D, 1),
        "gwT": np.ascontiguousarray(inputs["L0_attn_gate_w"].T),
        "gate_b": inputs["L0_attn_gate_b"].reshape(D, 1),
        "ln_g": inputs["L0_attn_ln_gamma"].reshape(D, 1),
        "ln_b": inputs["L0_attn_ln_beta"].reshape(D, 1),
        "ones_c": np.ones((1, 128), np.float32),
        "ones_col": np.ones((128, 1), np.float32),
        "sel16": sel,
    }


def _rope_tables(G0):
    j = np.arange(32)
    inv_freq = 1.0 / 10000.0 ** (2 * j / 64.0)
    base = (2 * j + 0.4 * 64) / (1.4 * 64)
    sign = np.concatenate([-np.ones(32), np.ones(32)])
    freq64 = inv_freq[np.arange(64) % 32]
    base64 = base[np.arange(64) % 32]
    sign128 = np.tile(sign, 2)

    def tables(t, qk_scale, inv_scale):
        f = freq64[:, None] * t[None, :]
        power = (t - SEQ // 2) / 256.0
        sc = base64[:, None] ** power[None, :]
        if inv_scale:
            sc = 1.0 / sc
        c1 = np.cos(f) * sc
        c2 = np.sin(f) * sc
        c1 = np.tile(c1, (2, 1)) * qk_scale
        c2 = np.tile(c2, (2, 1)) * qk_scale * sign128[:, None]
        return c1.astype(np.float32), c2.astype(np.float32)

    tq = G0 - FH + QO + np.arange(NQ)
    tk = G0 - FH + KO + np.arange(NK)
    c1q, c2q = tables(tq, 8.0, False)
    c1k, c2k = tables(tk, 1.0, True)
    return c1q, c2q, c1k, c2k


_ROT_PERM = None


def _rot_perm():
    global _ROT_PERM
    if _ROT_PERM is None:
        idx = np.arange(INNER)
        _ROT_PERM = (idx // 64) * 64 + ((idx % 64) + 32) % 64
    return _ROT_PERM


def stage_L0Fa(inputs, core, l, xfull):
    b, h = core // 2, core % 2
    G0 = h * T
    xe = np.zeros((TL, D), np.float32)
    if h == 0:
        xe[FH:] = xfull[b, 0:T]
    else:
        xe[:] = xfull[b, G0 - FH:G0 + T]
    wqkv = np.ascontiguousarray(inputs["L0_ffn_wqkv"][l])
    perm = _rot_perm()
    c1q, c2q, c1k, c2k = _rope_tables(G0)
    jj = np.arange(1024)[:, None]
    rr = np.arange(512)[None, :]
    amask = np.where((jj >= rr) & (jj <= rr + 512), 0.0, NEG_BIG
                     ).astype(np.float32)
    kvt = np.zeros((128, 24), np.float32)
    pidx = np.arange(128)[:, None]
    tidx = np.arange(24)[None, :]
    gk = G0 - FH + KO + tidx * 128 + pidx
    kvt[:] = (gk >= 0).astype(np.float32)
    e2 = np.zeros((128, 2), np.float32)
    e2[0:64, 0] = 1.0
    e2[64:128, 1] = 1.0
    e2t = np.ascontiguousarray(e2.T)
    return {
        "x1e": np.ascontiguousarray(xe.T),
        "wqkv": wqkv,
        "wqrot": np.ascontiguousarray(wqkv[:, 0:INNER][:, perm]),
        "wkrot": np.ascontiguousarray(wqkv[:, INNER:2 * INNER][:, perm]),
        "wo_in": np.ascontiguousarray(inputs["L0_ffn_wo"][l]),
        "qs2": np.tile(inputs["L0_ffn_q_scale"][l], 2).reshape(128, 1),
        "ks2": np.tile(inputs["L0_ffn_k_scale"][l], 2).reshape(128, 1),
        "qs2p": np.tile(inputs["L0_ffn_q_scale"][l][(np.arange(64) + 32) % 64],
                        2).reshape(128, 1),
        "ks2p": np.tile(inputs["L0_ffn_k_scale"][l][(np.arange(64) + 32) % 64],
                        2).reshape(128, 1),
        "c1q": c1q, "c2q": c2q, "c1k": c1k, "c2k": c2k,
        "amask": amask,
        "ident": np.eye(128, dtype=np.float32),
        "kvt": kvt,
        "ones_c": np.ones((1, 128), np.float32),
        "ones_col": np.ones((128, 1), np.float32),
        "e2": e2, "e2t": e2t,
    }


def stage_L0Fb(inputs, core, l, u_core, xfull=None):
    w1 = inputs["L0_ffn_w1"][l]
    b1 = inputs["L0_ffn_b1"][l]
    d = {
        "ue": u_core,
        "w1a": np.ascontiguousarray(w1[:, 0:HIDF]),
        "w1g": np.ascontiguousarray(w1[:, HIDF:]),
        "b1a": np.ascontiguousarray(b1[0:HIDF]).reshape(HIDF, 1),
        "b1g": np.ascontiguousarray(b1[HIDF:]).reshape(HIDF, 1),
        "w2": np.ascontiguousarray(inputs["L0_ffn_w2"][l]),
        "b2": inputs["L0_ffn_b2"][l].reshape(D, 1),
        "lg": inputs["L0_ffn_ln_g"][l].reshape(D, 1),
        "lbt": inputs["L0_ffn_ln_b"][l].reshape(D, 1),
        "ones_c": np.ones((1, 128), np.float32),
        "ones_col": np.ones((128, 1), np.float32),
    }
    if xfull is not None:
        b, h = core // 2, core % 2
        d["x1own"] = np.ascontiguousarray(xfull[b, h * T:(h + 1) * T].T)
    return d


QMAX = 32767.0
HID3 = 3072
MC3 = 24


def emit_scale_from_partials(p, prow, n, scol, rcol, pp, onef, eps=1e-12,
                             qdiv=QMAX):
    p.phase()
    p.red(prow[0:1, n:n + 1], prow[0:1, 0:n], ALU.max)
    p.phase()
    p.ts(prow[0:1, n:n + 1], prow[0:1, n:n + 1], 1.0 / qdiv, ALU.mult)
    p.phase()
    p.ts(prow[0:1, n:n + 1], prow[0:1, n:n + 1], eps, ALU.max)
    p.phase()
    p.act(prow[0:1, n + 1:n + 2], prow[0:1, n:n + 1], AF.Ln)
    p.phase()
    p.act(prow[0:1, n + 2:n + 3], prow[0:1, n + 1:n + 2], AF.Exp, scale=-1.0)
    p.phase()
    p.mm(pp[0][0:128, 0:1], onef[0:1, 0:128], prow[0:1, n:n + 1])
    p.mm(pp[1][0:128, 0:1], onef[0:1, 0:128], prow[0:1, n + 2:n + 3])
    p.phase()
    p.cpy(scol[:, 0:1], pp[0][0:128, 0:1])
    p.cpy(rcol[:, 0:1], pp[1][0:128, 0:1])


def emit_qdq(p, tf, ti, src, scol, lo, hi, dst=None, rcol=None):
    p.phase()
    p.ts(tf, src, rcol[:, 0:1], ALU.mult)
    p.phase()
    p.cpy(ti, tf)
    p.phase()
    p.cpy(tf, ti)
    p.phase()
    p.ts(tf, tf, lo, ALU.max)
    p.phase()
    p.ts(tf, tf, hi, ALU.min)
    p.phase()
    out = dst if dst is not None else tf
    p.ts(out, tf, scol[:, 0:1], ALU.mult)


def emit_absmax_acc(p, acc_col, tmp_col, src, first):
    p.red(tmp_col, src, ALU.max, absval=True)
    p.phase()
    if first:
        p.cpy(acc_col, tmp_col)
    else:
        p.tt(acc_col, acc_col, tmp_col, ALU.max)


def emit_pmax_out(p, acc_col, brow, scq, out):
    p.phase()
    p.dma("sync", scq, acc_col)
    p.phase()
    p.dma("sync", brow[0:1, 0:128], scq.rearrange("p o -> o p"))
    p.phase()
    p.red(brow[0:1, 128:129], brow[0:1, 0:128], ALU.max)
    p.phase()
    p.dma("sync", out, brow[0:1, 128:129])


def emit_lpbq_chunk(p, wnat, sm, rm, tf, ti, wqb, ptb, identb, nblk):
    C = 32 * nblk
    w3 = wnat[0:128, 0:C].rearrange("p (a b) -> p a b", b=32)
    t3 = tf[0:128, 0:C].rearrange("p (a b) -> p a b", b=32)
    smb = sm[:, 0:nblk].unsqueeze(2).to_broadcast((128, nblk, 32))
    rmb = rm[:, 0:nblk].unsqueeze(2).to_broadcast((128, nblk, 32))
    p.phase()
    p.red(sm[:, 0:nblk], w3, ALU.max, absval=True)
    p.phase()
    p.ts(sm[:, 0:nblk], sm[:, 0:nblk], 1.0 / 7.0, ALU.mult)
    p.phase()
    p.ts(sm[:, 0:nblk], sm[:, 0:nblk], 1e-12, ALU.max)
    p.phase()
    p.act(rm[:, 0:nblk], sm[:, 0:nblk], AF.Ln)
    p.phase()
    p.act(rm[:, 0:nblk], rm[:, 0:nblk], AF.Exp, scale=-1.0)
    p.phase()
    p.tt(t3, w3, rmb, ALU.mult)
    p.phase()
    p.cpy(ti[:, 0:C], tf[:, 0:C])
    p.phase()
    p.cpy(tf[:, 0:C], ti[:, 0:C])
    p.phase()
    p.ts(tf[:, 0:C], tf[:, 0:C], -8.0, ALU.max)
    p.phase()
    p.ts(tf[:, 0:C], tf[:, 0:C], 7.0, ALU.min)
    p.phase()
    p.tt(t3, t3, smb, ALU.mult)
    p.phase()
    p.dma("gpsimd", wqb[:, 0:C], tf[:, 0:C])


def build_L1FB():
    p = Prog()
    xn_in = p.din("xn", [D, T], BF16)
    pmax8_in = p.din("pmax8", [1, 8])
    wup_in = p.din("wup", [HID3, D])
    wg_in = p.din("wg", [HID3, D])
    ident_in = p.din("ident", [128, 128])
    ones_c = p.din("ones_c", [1, 128])
    up_out = p.dout("up_raw", [HID3, T])
    g_out = p.dout("gate_raw", [HID3, T])
    pmu_out = p.dout("pmu", [1, 1])
    pmg_out = p.dout("pmg", [1, 1])
    scq = p.nc.dram_tensor("scq", [128, 1], F32, kind="Internal").ap()

    pp = [p.psum(f"pp{i}", [128, 512]) for i in range(6)]
    ptb = [p.psum(f"ptb{i}", [128, 512], BF16) for i in range(2)]
    A = Arena(p, 209536)
    bu = Bump(209536)
    xqb = [A.bf16(bu.take(T * 2), T) for _ in range(DK)]
    wupT = [A.bf16(bu.take(HID3 * 2), HID3) for _ in range(DK)]
    wgT = [A.bf16(bu.take(HID3 * 2), HID3) for _ in range(DK)]
    wnat = [A.f32(bu.take(D * 4), D) for _ in range(2)]
    tf = A.f32(bu.take(T * 4), T)
    ti = A.i32(bu.take(T * 4), T)
    wqb = A.bf16(bu.take(D * 2), D)
    sm = A.f32(bu.take(32 * 4), 32)
    rm = A.f32(bu.take(32 * 4), 32)
    of = [A.f32(bu.take(T * 4), T) for _ in range(2)]
    identb = A.bf16(bu.take(256), 128)
    onef = A.f32(bu.take(512), 128, rows=1)
    prow = A.f32(bu.take(16 * 4), 16, rows=1)
    scol = A.f32(bu.take(4), 1)
    rcol = A.f32(bu.take(4), 1)
    ucol = A.f32(bu.take(8), 2)
    gcol = A.f32(bu.take(8), 2)
    brow = A.f32(bu.take(132 * 4), 132, rows=1)

    for k in range(DK):
        p.dma("sync", xqb[k], xn_in[k * 128:(k + 1) * 128, :])
    p.dma("sync", prow[0:1, 0:8], pmax8_in)
    p.dma("sync", onef[0:1, :], ones_c)
    p.dma("gpsimd", identb[:, :], ident_in)
    emit_scale_from_partials(p, prow, 8, scol, rcol, pp, onef)
    for k in range(DK):
        emit_qdq(p, tf[:, 0:T], ti[:, 0:T], xqb[k], scol,
                 -(QMAX + 1), QMAX, rcol=rcol)
        p.phase()
        p.dma("gpsimd", xqb[k], tf[:, 0:T])
    for wi, (w_in, wT) in enumerate(((wup_in, wupT), (wg_in, wgT))):
        p.phase()
        p.dma("sync", wnat[0][:, :], w_in[0:128, :])
        for mc in range(MC3):
            if mc + 1 < MC3:
                p.phase()
                p.dma("sync", wnat[(mc + 1) % 2][:, :],
                      w_in[(mc + 1) * 128:(mc + 2) * 128, :])
            emit_lpbq_chunk(p, wnat[mc % 2], sm, rm, tf, ti, wqb, ptb, identb, 32)
            p.phase()
            for j in range(DK):
                p.transp(ptb[j // 4][:, (j % 4) * 128:(j % 4 + 1) * 128],
                         wqb[:, j * 128:(j + 1) * 128], identb[:, :])
            p.phase()
            for j in range(DK):
                p.cpy(wT[j][:, mc * 128:(mc + 1) * 128],
                      ptb[j // 4][:, (j % 4) * 128:(j % 4 + 1) * 128])
    ct = tchunks(T)
    for wi, (wT, out_d, acc) in enumerate(((wupT, up_out, ucol),
                                           (wgT, g_out, gcol))):
        for mo in range(MC3):
            p.phase()
            for i, (o, n) in enumerate(ct):
                for k in range(DK):
                    p.mm(pp[i][0:128, 0:n], wT[k][:, mo * 128:(mo + 1) * 128],
                         xqb[k][:, o:o + n], start=(k == 0), stop=(k == DK - 1))
            p.phase()
            ob = of[mo % 2]
            for i, (o, n) in enumerate(ct):
                p.cpy(ob[:, o:o + n], pp[i][0:128, 0:n])
            p.phase()
            emit_absmax_acc(p, acc[:, 0:1], acc[:, 1:2], ob[:, 0:T], mo == 0)
            p.phase()
            p.dma("sync", out_d[mo * 128:(mo + 1) * 128, :], ob[:, 0:T])
    emit_pmax_out(p, ucol[:, 0:1], brow, scq, pmu_out)
    emit_pmax_out(p, gcol[:, 0:1], brow, scq, pmg_out)
    return p.finish()


def build_L1FC():
    p = Prog()
    g_in = p.din("gate_raw", [HID3, T])
    pmg8_in = p.din("pmg8", [1, 8])
    ones_c = p.din("ones_c", [1, 128])
    g2_out = p.dout("gate2", [HID3, T])
    pm2_out = p.dout("pm2", [1, 1])
    scq = p.nc.dram_tensor("scq", [128, 1], F32, kind="Internal").ap()

    pp = [p.psum(f"pp{i}", [128, 512]) for i in range(2)]
    A = Arena(p, 209536)
    bu = Bump(209536)
    gr = [A.f32(bu.take(T * 4), T) for _ in range(2)]
    tf = A.f32(bu.take(T * 4), T)
    ti = A.i32(bu.take(T * 4), T)
    gq = A.f32(bu.take(T * 4), T)
    sg = A.f32(bu.take(T * 4), T)
    g2 = [A.f32(bu.take(T * 4), T) for _ in range(2)]
    onef = A.f32(bu.take(512), 128, rows=1)
    prow = A.f32(bu.take(16 * 4), 16, rows=1)
    scol = A.f32(bu.take(4), 1)
    rcol = A.f32(bu.take(4), 1)
    acc = A.f32(bu.take(8), 2)
    brow = A.f32(bu.take(132 * 4), 132, rows=1)

    p.dma("sync", prow[0:1, 0:8], pmg8_in)
    p.dma("sync", onef[0:1, :], ones_c)
    p.dma("sync", gr[0][:, :], g_in[0:128, :])
    emit_scale_from_partials(p, prow, 8, scol, rcol, pp, onef)
    for mc in range(MC3):
        if mc + 1 < MC3:
            p.phase()
            p.dma("sync", gr[(mc + 1) % 2][:, :],
                  g_in[(mc + 1) * 128:(mc + 2) * 128, :])
        emit_qdq(p, tf[:, 0:T], ti[:, 0:T], gr[mc % 2][:, 0:T], scol,
                 -(QMAX + 1), QMAX, dst=gq[:, 0:T], rcol=rcol)
        p.phase()
        p.act(sg[:, 0:T], gq[:, 0:T], AF.Sigmoid)
        p.phase()
        p.ts(tf[:, 0:T], sg[:, 0:T], 65536.0, ALU.mult)
        p.phase()
        p.cpy(ti[:, 0:T], tf[:, 0:T])
        p.phase()
        p.cpy(tf[:, 0:T], ti[:, 0:T])
        p.phase()
        p.ts(tf[:, 0:T], tf[:, 0:T], 0.0, ALU.max)
        p.phase()
        p.ts(tf[:, 0:T], tf[:, 0:T], 65535.0, ALU.min)
        p.phase()
        p.ts(tf[:, 0:T], tf[:, 0:T], 1.0 / 65536.0, ALU.mult)
        p.phase()
        gb = g2[mc % 2]
        p.tt(gb[:, 0:T], gq[:, 0:T], tf[:, 0:T], ALU.mult)
        p.phase()
        emit_absmax_acc(p, acc[:, 0:1], acc[:, 1:2], gb[:, 0:T], mc == 0)
        p.phase()
        p.dma("sync", g2_out[mc * 128:(mc + 1) * 128, :], gb[:, 0:T])
    emit_pmax_out(p, acc[:, 0:1], brow, scq, pm2_out)
    return p.finish()


def build_L1FD():
    p = Prog()
    u_in = p.din("up_raw", [HID3, T])
    g2_in = p.din("gate2", [HID3, T])
    pmu8_in = p.din("pmu8", [1, 8])
    pm28_in = p.din("pm28", [1, 8])
    ones_c = p.din("ones_c", [1, 128])
    o_out = p.dout("o_raw", [HID3, T])
    pmo_out = p.dout("pmo", [1, 1])
    scq = p.nc.dram_tensor("scq", [128, 1], F32, kind="Internal").ap()

    pp = [p.psum(f"pp{i}", [128, 512]) for i in range(2)]
    A = Arena(p, 209536)
    bu = Bump(209536)
    ur = [A.f32(bu.take(T * 4), T) for _ in range(2)]
    g2r = [A.f32(bu.take(T * 4), T) for _ in range(2)]
    tf = A.f32(bu.take(T * 4), T)
    ti = A.i32(bu.take(T * 4), T)
    uq = A.f32(bu.take(T * 4), T)
    ob = [A.f32(bu.take(T * 4), T) for _ in range(2)]
    onef = A.f32(bu.take(512), 128, rows=1)
    prow = A.f32(bu.take(16 * 4), 16, rows=1)
    prow2 = A.f32(bu.take(16 * 4), 16, rows=1)
    scol = A.f32(bu.take(4), 1)
    rcol = A.f32(bu.take(4), 1)
    scol2 = A.f32(bu.take(4), 1)
    rcol2 = A.f32(bu.take(4), 1)
    acc = A.f32(bu.take(8), 2)
    brow = A.f32(bu.take(132 * 4), 132, rows=1)

    p.dma("sync", prow[0:1, 0:8], pmu8_in)
    p.dma("sync", prow2[0:1, 0:8], pm28_in)
    p.dma("sync", onef[0:1, :], ones_c)
    p.dma("sync", ur[0][:, :], u_in[0:128, :])
    p.dma("sync", g2r[0][:, :], g2_in[0:128, :])
    emit_scale_from_partials(p, prow, 8, scol, rcol, pp, onef)
    emit_scale_from_partials(p, prow2, 8, scol2, rcol2, pp, onef)
    for mc in range(MC3):
        if mc + 1 < MC3:
            p.phase()
            p.dma("sync", ur[(mc + 1) % 2][:, :],
                  u_in[(mc + 1) * 128:(mc + 2) * 128, :])
            p.dma("sync", g2r[(mc + 1) % 2][:, :],
                  g2_in[(mc + 1) * 128:(mc + 2) * 128, :])
        emit_qdq(p, tf[:, 0:T], ti[:, 0:T], ur[mc % 2][:, 0:T], scol,
                 -(QMAX + 1), QMAX, dst=uq[:, 0:T], rcol=rcol)
        emit_qdq(p, tf[:, 0:T], ti[:, 0:T], g2r[mc % 2][:, 0:T], scol2,
                 -(QMAX + 1), QMAX, rcol=rcol2)
        p.phase()
        o2 = ob[mc % 2]
        p.tt(o2[:, 0:T], tf[:, 0:T], uq[:, 0:T], ALU.mult)
        p.phase()
        emit_absmax_acc(p, acc[:, 0:1], acc[:, 1:2], o2[:, 0:T], mc == 0)
        p.phase()
        p.dma("sync", o_out[mc * 128:(mc + 1) * 128, :], o2[:, 0:T])
    emit_pmax_out(p, acc[:, 0:1], brow, scq, pmo_out)
    return p.finish()


def build_L1FE(chain_faq=False):
    p = Prog()
    o_in = p.din("o_raw", [HID3, T])
    pmo8_in = p.din("pmo8", [1, 8])
    wd_in = p.din("wd", [D, HID3])
    x4_in = p.din("x4o", [D, T])
    ident_in = p.din("ident", [128, 128])
    ones_c = p.din("ones_c", [1, 128])
    x5_out = p.dout("x5", [D, T])

    pp = [p.psum(f"pp{i}", [128, 512]) for i in range(6)]
    ptb = [p.psum(f"ptb{i}", [128, 512], BF16) for i in range(2)]
    A = Arena(p, 209536)
    bu = Bump(209536)
    oqb = [A.bf16(bu.take(T * 2), T) for _ in range(MC3)]
    wdT = [A.bf16(bu.take(D * 2), D) for _ in range(MC3)]
    wnat = [A.f32(bu.take(HID3 * 4), HID3)]
    tf = A.f32(bu.take(HID3 * 4), HID3)
    ti = A.i32(bu.take(HID3 * 4), HID3)
    wqb = A.bf16(bu.take(HID3 * 2), HID3)
    sm = A.f32(bu.take(96 * 4), 96)
    rm = A.f32(bu.take(96 * 4), 96)
    xres = A.f32(bu.take(T * 4), T)
    wst = [A.f32(bu.take(512 * 4), 512) for _ in range(4)]
    identb = A.bf16(bu.take(256), 128)
    onef = A.f32(bu.take(512), 128, rows=1)
    prow = A.f32(bu.take(16 * 4), 16, rows=1)
    scol = A.f32(bu.take(4), 1)
    rcol = A.f32(bu.take(4), 1)

    p.dma("sync", prow[0:1, 0:8], pmo8_in)
    p.dma("sync", onef[0:1, :], ones_c)
    p.dma("gpsimd", identb[:, :], ident_in)
    emit_scale_from_partials(p, prow, 8, scol, rcol, pp, onef)
    for mc in range(MC3):
        p.phase()
        p.dma("sync", wnat[0][:, 0:T], o_in[mc * 128:(mc + 1) * 128, :])
        emit_qdq(p, tf[:, 0:T], ti[:, 0:T], wnat[0][:, 0:T], scol,
                 -(QMAX + 1), QMAX, rcol=rcol)
        p.phase()
        p.dma("gpsimd", oqb[mc], tf[:, 0:T])
    for m in range(DK):
        p.phase()
        p.dma("sync", wnat[0][:, :], wd_in[m * 128:(m + 1) * 128, :])
        emit_lpbq_chunk(p, wnat[0], sm, rm, tf, ti, wqb, ptb, identb, 96)
        for jg in range(3):
            p.phase()
            for j in range(8):
                kt = jg * 8 + j
                p.transp(ptb[j // 4][:, (j % 4) * 128:(j % 4 + 1) * 128],
                         wqb[:, kt * 128:(kt + 1) * 128], identb[:, :])
            p.phase()
            for j in range(8):
                kt = jg * 8 + j
                p.cpy(wdT[kt][:, m * 128:(m + 1) * 128],
                      ptb[j // 4][:, (j % 4) * 128:(j % 4 + 1) * 128])
    ct = tchunks(T)
    for mo in range(DK):
        p.phase()
        p.dma("sync", xres[:, :], x4_in[mo * 128:(mo + 1) * 128, :])
        for i, (o, n) in enumerate(ct):
            for kt in range(MC3):
                p.mm(pp[i][0:128, 0:n], wdT[kt][:, mo * 128:(mo + 1) * 128],
                     oqb[kt][:, o:o + n], start=(kt == 0), stop=(kt == MC3 - 1))
        p.phase()
        for i, (o, n) in enumerate(ct):
            p.cpy(wst[i][:, 0:n], pp[i][0:128, 0:n])
        p.phase()
        for i, (o, n) in enumerate(ct):
            p.tt(wst[i][:, 0:n], wst[i][:, 0:n], xres[:, o:o + n], ALU.add)
        p.phase()
        for i, (o, n) in enumerate(ct):
            p.dma("sync", x5_out[mo * 128:(mo + 1) * 128, o:o + n],
                  wst[i][:, 0:n])
    if chain_faq:
        build_FAq(2, p=p, A=A, pp=pp, x_src=x5_out, pre="q2_")
    return p.finish()



def build_FAatt2():
    p = Prog()
    qh_in = p.din("qh", [512, SEQ], BF16)
    kh_in = p.din("kh", [512, SEQ], BF16)
    vtl_in = p.din("vtl", [8 * 128, 32 * 65], BF16)
    slc_in = p.din("slc", [128, 8])
    slr_in = p.din("slr", [1, 8])
    drow_in = p.din("drow", [128, 512])
    i2_in = p.din("i2", [128, 32])
    ir_in = p.din("ir", [1, 512])
    ones_c = p.din("ones_c", [1, 128])
    ao_out = p.dout("ao", [512, SEQ], BF16)

    pp = [p.psum(f"pp{i}", [128, 512]) for i in range(8)]
    A = Arena(p, 209536)
    qhb = [A.bf16(k * 8192, SEQ) for k in range(4)]
    khb = [A.bf16(32768 + k * 8192, SEQ) for k in range(4)]
    sexp = [A.bf16(65536 + i * 1024, 512) for i in range(32)]
    aob = [A.bf16(98304 + k * 8192, SEQ) for k in range(4)]
    Vb = [A.bf16(131072 + i * 4160, 32 * 65) for i in range(2)]
    Pc = [A.bf16(139392 + i * 1024, 512) for i in range(4)]
    frow_bel = A.f32(143488, 512)
    frow_abv = A.f32(145536, 512)
    drf = A.f32(147584, 512)
    tp = A.f32(149632, 512)
    colm_bel = A.f32(151680, 32)
    colm_abv = A.f32(151808, 32)
    t2c = A.f32(151936, 32)
    i2f = A.f32(152064, 32)
    av_sb = A.f32(152192, 512)
    av2 = A.f32(154240, 512)
    recb = A.f32(156288, 512, rows=64)
    SM = 158336
    irow = A.f32(SM, 512, rows=1)
    rt = A.f32(SM + 2048, 512, rows=1)
    rt2 = A.f32(SM + 4096, 512, rows=1)
    dr2 = A.t[64:65, SM + 6144:SM + 6144 + 2048].bitcast(F32)
    onef64 = A.t[64:65, SM + 8192:SM + 8192 + 512].bitcast(F32)
    onef = A.f32(SM + 8704, 128, rows=1)
    slc = A.f32(SM + 9216, 8)
    slr = A.f32(SM + 9248, 8, rows=1)

    for k in range(4):
        p.dma("sync", qhb[k], qh_in[k * 128:(k + 1) * 128, :])
        p.dma("sync", khb[k], kh_in[k * 128:(k + 1) * 128, :])
    p.dma("sync", drf[:, :], drow_in)
    p.dma("sync", i2f[:, :], i2_in)
    p.dma("sync", irow[0:1, :], ir_in)
    p.dma("sync", slc[:, :], slc_in)
    p.dma("sync", slr[0:1, :], slr_in)
    p.dma("sync", onef[0:1, :], ones_c)
    p.dma("sync", onef64[0:1, 0:64], ones_c[0:1, 0:64])
    p.dma("sync", Vb[0][:, :], vtl_in[0:128, :])

    for s in range(8):
        m, rb = s // 2, (s % 2) * 64
        s_keep = 2.0 ** (-8.0 / (8 - s)) if s < 8 else None
        p.phase()
        if s + 1 < 8:
            p.dma("sync", Vb[(s + 1) % 2][:, :],
                  vtl_in[(s + 1) * 128:(s + 2) * 128, :])
        for di in range(4):
            p.phase()
            p.ts(tp[:, :], drf[:, :], float(di * 128), ALU.add)
            p.phase()
            p.act(tp[:, :], tp[:, :], AF.Abs)
            p.phase()
            p.ts(tp[:, :], tp[:, :], slc[:, s:s + 1], ALU.mult)
            p.phase()
            p.act(Pc[di], tp[:, :], AF.Exp, scale=-1.0)
        p.phase()
        p.ts(rt[0:1, :], irow[0:1, :], slr[0:1, s:s + 1], ALU.mult)
        p.ts(rt2[0:1, :], irow[0:1, :], 511.0, ALU.subtract, eng="gpsimd")
        p.phase()
        p.act(rt[0:1, :], rt[0:1, :], AF.Exp, scale=-1.0)
        p.ts(rt2[0:1, :], rt2[0:1, :], slr[0:1, s:s + 1], ALU.mult)
        p.phase()
        p.act(rt2[0:1, :], rt2[0:1, :], AF.Exp)
        p.phase()
        p.mm(pp[4][0:128, 0:512], onef[0:1, 0:128], rt[0:1, :])
        p.mm(pp[5][0:128, 0:512], onef[0:1, 0:128], rt2[0:1, :])
        p.phase()
        p.cpy(frow_bel[:, :], pp[4][0:128, 0:512])
        p.cpy(frow_abv[:, :], pp[5][0:128, 0:512])
        for tqc in range(8):
            i0 = tqc * 512
            p.phase()
            p.ts(t2c[:, :], i2f[:, :], float(-i0), ALU.add)
            p.phase()
            p.ts(t2c[:, :], t2c[:, :], slc[:, s:s + 1], ALU.mult)
            p.phase()
            p.ts(t2c[:, :], t2c[:, :], 60.0, ALU.min)
            p.phase()
            p.act(colm_bel[:, :], t2c[:, :], AF.Exp)
            p.phase()
            p.ts(t2c[:, :], i2f[:, :], float(-(i0 + 511)), ALU.add)
            p.phase()
            p.ts(t2c[:, :], t2c[:, :], slc[:, s:s + 1], ALU.mult)
            p.phase()
            p.ts(t2c[:, :], t2c[:, :], -60.0, ALU.max)
            p.phase()
            p.act(colm_abv[:, :], t2c[:, :], AF.Exp, scale=-1.0)
            kept = []
            for kc in range(32):
                dd = kc * 128 - i0
                if dd in (0, 128, 256, 384):
                    kept.append((kc, "x", dd // 128))
                elif dd < 0:
                    mind = i0 - (kc * 128 + 127)
                    if s_keep * mind <= 103.0:
                        kept.append((kc, "b", 0))
                else:
                    mind = kc * 128 - (i0 + 511)
                    if s_keep * mind <= 103.0:
                        kept.append((kc, "a", 0))
            for g0_ in range(0, len(kept), 8):
                sub = kept[g0_:g0_ + 8]
                p.phase()
                for j, (kc, tag, di) in enumerate(sub):
                    p.mm(pp[j][0:128, 0:512],
                         khb[m][rb:rb + 64, kc * 128:(kc + 1) * 128],
                         qhb[m][rb:rb + 64, i0:i0 + 512])
                p.phase()
                for j, (kc, tag, di) in enumerate(sub):
                    p.act(sexp[kc], pp[j][0:128, 0:512], AF.Exp)
                p.phase()
                for j, (kc, tag, di) in enumerate(sub):
                    if tag == "x":
                        p.tt(sexp[kc], sexp[kc], Pc[di], ALU.mult)
                    elif tag == "b":
                        p.stt(sexp[kc], sexp[kc], colm_bel[:, kc:kc + 1],
                              frow_bel[:, :], ALU.mult, ALU.mult)
                    else:
                        p.stt(sexp[kc], sexp[kc], colm_abv[:, kc:kc + 1],
                              frow_abv[:, :], ALU.mult, ALU.mult)
            p.phase()
            nk = len(kept)
            for ki, (kc, tag, di) in enumerate(kept):
                p.mm(pp[0][0:65, 0:512],
                     Vb[s % 2][:, kc * 65:(kc + 1) * 65],
                     sexp[kc], start=(ki == 0), stop=(ki == nk - 1))
            p.phase()
            p.cpy(av_sb[0:65, :], pp[0][0:65, 0:512])
            p.phase()
            p.ts(av_sb[64:65, :], av_sb[64:65, :], 1e-30, ALU.max)
            p.phase()
            p.act(av_sb[64:65, :], av_sb[64:65, :], AF.Ln)
            p.phase()
            p.act(dr2[0:1, 0:512], av_sb[64:65, :], AF.Exp, scale=-1.0)
            p.phase()
            p.mm(pp[2][0:64, 0:512], onef64[0:1, 0:64], dr2[0:1, 0:512])
            p.phase()
            p.cpy(recb[0:64, :], pp[2][0:64, 0:512])
            p.phase()
            p.tt(av2[0:64, :], av_sb[0:64, :], recb[0:64, :], ALU.mult)
            p.phase()
            p.act(aob[m][rb:rb + 64, i0:i0 + 512], av2[0:64, :], AF.Copy)
    p.phase()
    for k in range(4):
        p.dma("gpsimd", ao_out[k * 128:(k + 1) * 128, :], aob[k])
    return p.finish()


CAP = 2560.0


def build_M1():
    p = Prog()
    x5_in = p.din("x5o", [D, T])
    ao_in = p.din("aoR", [D, T], BF16)
    wo_in = p.din("wo", [D, D])
    wr_in = p.din("wrT", [D, 8])
    tri_in = p.din("tri", [128, 128])
    i8_in = p.din("i8r", [1, 128])
    ones_c = p.din("ones_c", [1, 128])
    ones_col_in = p.din("ones_col", [128, 1])
    x6_out = p.dout("x6", [D, T])
    xn6_out = p.dout("xn6", [D, T], BF16)
    gcols_out = p.nc.dram_tensor("gcols", [128, 64], F32,
                                 kind="ExternalOutput")
    pos_out = p.nc.dram_tensor("pos_loc", [T, 8], F32, kind="ExternalOutput")
    counts_out = p.dout("counts", [1, 8])

    pp = [p.psum(f"pp{i}", [128, 512]) for i in range(8)]
    A = Arena(p, 209536)
    x6f = [A.f32(k * 8192, T) for k in range(DK)]
    aor = [A.bf16(65536 + k * 4096, T) for k in range(DK)]
    wob = [A.bf16(98304 + k * 2048, D) for k in range(DK)]
    xnb = [A.bf16(114688 + k * 4096, T) for k in range(DK)]
    xbf = [A.bf16(65536 + k * 4096, T) for k in range(DK)]
    xsqg = [A.bf16(98304 + k * 2048, 1024) for k in range(DK)]
    LB = 147456
    lb = NS(srow=A.f32(LB, 2 * T, rows=1),
            mu=A.f32(LB + 16384, T, rows=1),
            ab=A.bf16(LB + 24576, 2 * T, rows=1),
            abc=A.bf16(LB + 32768, 2 * T))
    t2ln = A.bf16(188416, T)
    wrb = A.bf16(192512, 8 * DK)
    lrow = A.f32(192704, T, rows=8)
    SM = 200896
    gl = A.f32(SM, 128)
    identf = A.f32(SM + 512, 128)
    gmax = A.f32(SM + 1024, 16)
    gsum = A.f32(SM + 1088, 16)
    eqb = A.f32(SM + 1152, 128)
    t1b = A.f32(SM + 1664, 128)
    gates2 = A.f32(SM + 2176, 128)
    eq0 = A.f32(SM + 2688, 128)
    eq1 = A.f32(SM + 3200, 128)
    maskf = A.f32(SM + 3712, 128)
    maskb = A.bf16(SM + 4224, 128)
    gquad = A.f32(SM + 4480, 64)
    iot8b = A.f32(SM + 4736, 128)
    iotm = A.f32(SM + 5248, 128)
    trib = A.bf16(SM + 5760, 128)
    posf = A.f32(SM + 6016, 8)
    runbc = A.f32(SM + 6048, 8)
    runrow = A.f32(SM + 6080, 8, rows=1)
    onebf = A.bf16(SM + 6144, 128, rows=1)
    onef = A.f32(SM + 6400, 128, rows=1)
    onecol = A.bf16(SM + 6912, 1)
    i8row = A.f32(SM + 6976, 128, rows=1)
    assert SM + 6976 + 512 <= 209536

    for k in range(DK):
        p.dma("sync", x6f[k], x5_in[k * 128:(k + 1) * 128, :])
        p.dma("sync", aor[k], ao_in[k * 128:(k + 1) * 128, :])
        p.dma("gpsimd", wob[k], wo_in[k * 128:(k + 1) * 128, :])
        p.dma("gpsimd", wrb[:, k * 8:(k + 1) * 8],
              wr_in[k * 128:(k + 1) * 128, :])
    p.dma("gpsimd", trib[:, :], tri_in)
    p.dma("sync", i8row[0:1, :], i8_in)
    p.dma("gpsimd", onebf[0:1, :], ones_c)
    p.dma("sync", onef[0:1, :], ones_c)
    p.dma("gpsimd", onecol[:, 0:1], ones_col_in)

    ident_in = p.din("ident", [128, 128])
    zr_in = p.din("zrow", [1, 8])
    p.dma("sync", identf[:, :], ident_in)
    p.dma("sync", runrow[0:1, :], zr_in)

    ct = tchunks(T)
    for mo in range(DK):
        p.phase()
        for i, (o, n) in enumerate(ct):
            for k in range(DK):
                p.mm(pp[i][0:128, 0:n], wob[k][:, mo * 128:(mo + 1) * 128],
                     aor[k][:, o:o + n], start=(k == 0), stop=(k == DK - 1))
        p.phase()
        for i, (o, n) in enumerate(ct):
            p.tt(x6f[mo][:, o:o + n], x6f[mo][:, o:o + n], pp[i][0:128, 0:n],
                 ALU.add)
        p.phase()
        p.dma("sync", x6_out[mo * 128:(mo + 1) * 128, :], x6f[mo])
    emit_layernorm(p, lb, T,
                   x_of=lambda k: x6f[k], xbf_of=lambda k: xbf[k],
                   xsq_of=lambda k: xsqg[k], out_of=lambda k: xnb[k],
                   pp=pp, ones_col=onecol, ones_row=onebf,
                   x_is_fp32=True, t2=t2ln, xsq_group=True)
    p.phase()
    for k in range(DK):
        p.dma("gpsimd", xn6_out[k * 128:(k + 1) * 128, :], xnb[k])
    p.phase()
    for i, (o, n) in enumerate(ct):
        for k in range(DK):
            p.mm(pp[i][0:8, 0:n], wrb[:, k * 8:(k + 1) * 8],
                 xnb[k][:, o:o + n], start=(k == 0), stop=(k == DK - 1))
    p.phase()
    for i, (o, n) in enumerate(ct):
        p.cpy(lrow[0:8, o:o + n], pp[i][0:8, 0:n])
    p.phase()
    for tt in range(16):
        p.transp(pp[4][0:128, tt * 8:(tt + 1) * 8],
                 lrow[0:8, tt * 128:(tt + 1) * 128], identf[0:8, 0:8])
    p.phase()
    p.cpy(gl[:, :], pp[4][0:128, 0:128])
    p.phase()
    p.mm(pp[5][0:128, 0:128], onef[0:1, 0:128], i8row[0:1, :])
    p.phase()
    p.cpy(iot8b[:, :], pp[5][0:128, 0:128])
    p.phase()
    p.ts(iotm[:, :], iot8b[:, :], 999.0, ALU.subtract)
    g3 = gl[:, :].rearrange("p (a b) -> p a b", b=8)

    def b16(x):
        return x.unsqueeze(2).to_broadcast((128, 16, 8))

    p.phase()
    p.red(gmax[:, :], g3, ALU.max)
    p.phase()
    p.tt(g3, g3, b16(gmax[:, :]), ALU.subtract)
    p.phase()
    p.act(gl[:, :], gl[:, :], AF.Exp)
    p.phase()
    p.red(gsum[:, :], g3, ALU.add)
    p.phase()
    p.act(gsum[:, :], gsum[:, :], AF.Ln)
    p.phase()
    p.act(gsum[:, :], gsum[:, :], AF.Exp, scale=-1.0)
    p.phase()
    p.tt(g3, g3, b16(gsum[:, :]), ALU.mult)
    g0c, g1c = gquad[:, 0:16], gquad[:, 16:32]
    i0c, i1c = gquad[:, 32:48], gquad[:, 48:64]
    e3 = eqb[:, :].rearrange("p (a b) -> p a b", b=8)
    t3 = t1b[:, :].rearrange("p (a b) -> p a b", b=8)
    g23 = gates2[:, :].rearrange("p (a b) -> p a b", b=8)
    e03 = eq0[:, :].rearrange("p (a b) -> p a b", b=8)
    e13 = eq1[:, :].rearrange("p (a b) -> p a b", b=8)
    i83 = iot8b[:, :].rearrange("p (a b) -> p a b", b=8)
    p.phase()
    p.red(g0c, g3, ALU.max)
    p.phase()
    p.tt(e3, g3, b16(g0c), ALU.is_equal)
    p.phase()
    p.tt(t1b[:, :], eqb[:, :], iotm[:, :], ALU.mult)
    p.phase()
    p.ts(t1b[:, :], t1b[:, :], 999.0, ALU.add)
    p.phase()
    p.red(i0c, t3, ALU.min)
    p.phase()
    p.tt(e03, i83, b16(i0c), ALU.is_equal)
    p.phase()
    p.stt(gates2[:, :], eq0[:, :], -2.0, gl[:, :], ALU.mult, ALU.add)
    p.phase()
    p.red(g1c, g23, ALU.max)
    p.phase()
    p.tt(e3, g23, b16(g1c), ALU.is_equal)
    p.phase()
    p.tt(t1b[:, :], eqb[:, :], iotm[:, :], ALU.mult)
    p.phase()
    p.ts(t1b[:, :], t1b[:, :], 999.0, ALU.add)
    p.phase()
    p.red(i1c, t3, ALU.min)
    p.phase()
    p.tt(e13, i83, b16(i1c), ALU.is_equal)
    p.phase()
    p.tt(maskf[:, :], eq0[:, :], eq1[:, :], ALU.add)
    p.phase()
    p.dma("gpsimd", maskb[:, :], maskf[:, :])
    p.phase()
    p.dma("sync", gcols_out.ap(), gquad[:, :])
    for tt in range(16):
        p.phase()
        p.mm(pp[5][0:128, 0:8], trib[:, :], maskb[:, tt * 8:(tt + 1) * 8])
        p.mm(pp[6][0:128, 0:8], onef[0:1, 0:128], runrow[0:1, :])
        p.phase()
        p.cpy(posf[:, :], pp[5][0:128, 0:8])
        p.cpy(runbc[:, :], pp[6][0:128, 0:8])
        p.phase()
        p.tt(posf[:, :], posf[:, :], runbc[:, :], ALU.add)
        p.phase()
        p.dma("sync", pos_out.ap()[tt * 128:(tt + 1) * 128, :], posf[:, :])
        p.dma("gpsimd", runrow[0:1, :], posf[127:128, :])
    p.phase()
    p.dma("sync", counts_out, runrow[0:1, :])
    return p.finish()


def build_M2(chain_faq=False):
    p = Prog()
    xn_in = p.din("xn6", [D, T], BF16)
    x6_in = p.din("x6o", [D, T])
    gcols_in = p.nc.dram_tensor("gcols", [128, 64], F32,
                                kind="ExternalInput")
    pos_in = p.nc.dram_tensor("pos_loc", [T, 8], F32, kind="ExternalInput")
    prevc_in = p.din("prevc", [8, 8])
    wgT_in = p.din("wgT", [D, 2048])
    wdT_in = p.din("wdTs", [8 * 2048, D])
    i8_in = p.din("i8r", [1, 128])
    ident_in = p.din("ident", [128, 128])
    ones_c = p.din("ones_c", [1, 128])
    ones_col_in = p.din("ones_col", [128, 1])
    x7_out = p.dout("x7", [D, T])

    pp = [p.psum(f"pp{i}", [128, 512]) for i in range(8)]
    A = Arena(p, 209536)
    hidb = [A.bf16(k * 4096, T) for k in range(16)]
    yacc = [A.f32(65536 + k * 8192, T) for k in range(DK)]
    WZ = 131072
    xnb = [A.bf16(WZ + k * 4096, T) for k in range(DK)]
    wgTb = [A.bf16(WZ + 32768 + k * 4096, 2048) for k in range(DK)]
    wdT = [A.bf16(WZ + kt * 2048, D) for kt in range(16)]
    crows = A.f32(WZ + 32768, T, rows=8)
    combbc = A.f32(WZ + 40960, T)
    crow0 = A.f32(WZ + 49152, T, rows=1)
    xres = A.f32(WZ + 40960, T)
    SM = WZ + 57344
    posb = A.f32(SM, 128)
    gq = A.f32(SM + 512, 64)
    posg = A.f32(SM + 768, 128)
    keb = A.f32(SM + 1280, 128)
    eq0 = A.f32(SM + 1792, 128)
    eq1 = A.f32(SM + 2304, 128)
    me = A.f32(SM + 2816, 128)
    comb = A.f32(SM + 3328, 128)
    iot8b = A.f32(SM + 3840, 128)
    identf = A.f32(SM + 4352, 128)
    offb = A.f32(SM + 4864, 8)
    kp0 = A.f32(SM + 4896, 16)
    kp1 = A.f32(SM + 4960, 16)
    g0k = A.f32(SM + 5024, 16)
    g1k = A.f32(SM + 5088, 16)
    den = A.f32(SM + 5152, 16)
    ok8 = A.f32(SM + 5216, 1, rows=8)
    prevcb = A.f32(SM + 5248, 8, rows=8)
    offrow = A.f32(SM + 5280, 8, rows=1)
    onef = A.f32(SM + 5312, 128, rows=1)
    i8row = A.f32(SM + 5824, 128, rows=1)
    tmpf = [A.f32(196608 + i * 2048, 512) for i in range(4)]
    assert 196608 + 4 * 2048 <= 209536

    for k in range(DK):
        p.dma("sync", xnb[k], xn_in[k * 128:(k + 1) * 128, :])
        p.dma("gpsimd", wgTb[k], wgT_in[k * 128:(k + 1) * 128, :])
    ct = tchunks(T)
    for hm in range(16):
        p.phase()
        for i, (o, n) in enumerate(ct):
            for k in range(DK):
                p.mm(pp[i][0:128, 0:n],
                     wgTb[k][:, hm * 128:(hm + 1) * 128],
                     xnb[k][:, o:o + n], start=(k == 0), stop=(k == DK - 1))
        p.phase()
        for i, (o, n) in enumerate(ct):
            p.act(hidb[hm][:, o:o + n], pp[i][0:128, 0:n], AF.Silu)
    p.phase()
    p.dma("sync", prevcb[0:8, :], prevc_in)
    p.dma("sync", ok8[0:8, 0:1], ones_col_in[0:8, 0:1])
    p.dma("sync", onef[0:1, :], ones_c)
    p.dma("sync", i8row[0:1, :], i8_in)
    p.dma("sync", identf[:, :], ident_in)
    p.dma("sync", posb[:, :],
          bass.AP(pos_in, 0, [[8, 128], [1024, 16], [1, 8]]))
    p.dma("sync", gq[:, :], gcols_in.ap())
    p.phase()
    p.mm(pp[0][0:1, 0:8], ok8[0:8, 0:1], prevcb[0:8, :])
    p.mm(pp[1][0:128, 0:128], onef[0:1, 0:128], i8row[0:1, :])
    p.phase()
    p.cpy(offrow[0:1, 0:8], pp[0][0:1, 0:8])
    p.cpy(iot8b[:, :], pp[1][0:128, 0:128])
    p.phase()
    p.mm(pp[2][0:128, 0:8], onef[0:1, 0:128], offrow[0:1, 0:8])
    p.phase()
    p.cpy(offb[:, :], pp[2][0:128, 0:8])
    pos3 = posb[:, :].rearrange("p (a b) -> p a b", b=8)
    pg3 = posg[:, :].rearrange("p (a b) -> p a b", b=8)
    i83 = iot8b[:, :].rearrange("p (a b) -> p a b", b=8)
    e03 = eq0[:, :].rearrange("p (a b) -> p a b", b=8)
    e13 = eq1[:, :].rearrange("p (a b) -> p a b", b=8)
    cb3 = comb[:, :].rearrange("p (a b) -> p a b", b=8)

    def b16(x):
        return x.unsqueeze(2).to_broadcast((128, 16, 8))

    p.phase()
    p.tt(pg3, pos3, offb[:, :].unsqueeze(1).to_broadcast((128, 16, 8)),
         ALU.add)
    p.phase()
    p.ts(keb[:, :], posg[:, :], CAP, ALU.is_le)
    p.phase()
    p.tt(e03, i83, gq[:, 32:48].unsqueeze(2).to_broadcast((128, 16, 8)),
         ALU.is_equal)
    p.phase()
    p.tt(e13, i83, gq[:, 48:64].unsqueeze(2).to_broadcast((128, 16, 8)),
         ALU.is_equal)
    p.phase()
    p.tt(me[:, :], eq0[:, :], eq1[:, :], ALU.add)
    p.phase()
    p.tt(me[:, :], me[:, :], keb[:, :], ALU.mult)
    p.phase()
    p.tt(eq0[:, :], eq0[:, :], me[:, :], ALU.mult)
    p.phase()
    p.tt(eq1[:, :], eq1[:, :], me[:, :], ALU.mult)
    p.phase()
    p.red(kp0[:, :], e03, ALU.max)
    p.phase()
    p.red(kp1[:, :], e13, ALU.max)
    p.phase()
    p.tt(g0k[:, :], gq[:, 0:16], kp0[:, :], ALU.mult)
    p.phase()
    p.tt(g1k[:, :], gq[:, 16:32], kp1[:, :], ALU.mult)
    p.phase()
    p.tt(den[:, :], g0k[:, :], g1k[:, :], ALU.add)
    p.phase()
    p.ts(den[:, :], den[:, :], 1e-6, ALU.add)
    p.phase()
    p.act(den[:, :], den[:, :], AF.Ln)
    p.phase()
    p.act(den[:, :], den[:, :], AF.Exp, scale=-1.0)
    p.phase()
    p.tt(g0k[:, :], g0k[:, :], den[:, :], ALU.mult)
    p.phase()
    p.tt(g1k[:, :], g1k[:, :], den[:, :], ALU.mult)
    p.phase()
    p.tt(e03, e03, b16(g0k[:, :]), ALU.mult)
    p.phase()
    p.tt(e13, e13, b16(g1k[:, :]), ALU.mult)
    p.phase()
    p.tt(cb3, e03, e13, ALU.add)
    for grp in range(4):
        p.phase()
        for j in range(4):
            tt = grp * 4 + j
            p.transp(pp[4][0:8, j * 128:(j + 1) * 128],
                     comb[:, tt * 8:(tt + 1) * 8], identf[:, :])
        p.phase()
        p.cpy(crows[0:8, grp * 512:(grp + 1) * 512], pp[4][0:8, 0:512])
    for e in range(8):
        p.phase()
        for kt in range(16):
            p.dma("gpsimd", wdT[kt],
                  wdT_in[e * 2048 + kt * 128:e * 2048 + (kt + 1) * 128, :])
        p.dma("sync", crow0[0:1, :], crows[e:e + 1, 0:T])
        p.phase()
        for i in range(4):
            p.mm(pp[4 + i][0:128, 0:512], onef[0:1, 0:128],
                 crow0[0:1, i * 512:(i + 1) * 512])
        p.phase()
        for i in range(4):
            p.cpy(combbc[:, i * 512:(i + 1) * 512], pp[4 + i][0:128, 0:512])
        for mo in range(DK):
            p.phase()
            for i, (o, n) in enumerate(ct):
                for kt in range(16):
                    p.mm(pp[i][0:128, 0:n],
                         wdT[kt][:, mo * 128:(mo + 1) * 128],
                         hidb[kt][:, o:o + n],
                         start=(kt == 0), stop=(kt == 15))
            if e == 0:
                p.phase()
                for i, (o, n) in enumerate(ct):
                    p.tt(yacc[mo][:, o:o + n], pp[i][0:128, 0:n],
                         combbc[:, o:o + n], ALU.mult)
            else:
                p.phase()
                for i, (o, n) in enumerate(ct):
                    p.tt(tmpf[i][:, 0:n], pp[i][0:128, 0:n],
                         combbc[:, o:o + n], ALU.mult)
                p.phase()
                for i, (o, n) in enumerate(ct):
                    p.tt(yacc[mo][:, o:o + n], yacc[mo][:, o:o + n],
                         tmpf[i][:, 0:n], ALU.add,
                         eng=("vector" if i % 2 == 0 else "gpsimd"))
    for mo in range(DK):
        p.phase()
        p.dma("sync", xres[:, :], x6_in[mo * 128:(mo + 1) * 128, :])
        p.phase()
        p.tt(yacc[mo], yacc[mo], xres[:, :], ALU.add)
        p.phase()
        p.dma("sync", x7_out[mo * 128:(mo + 1) * 128, :], yacc[mo])
    if chain_faq:
        build_FAq(3, p=p, A=A, pp=pp, x_src=x7_out, pre="q3_")
    return p.finish()



def build_FAatt3():
    p = Prog()
    qh_in = p.din("qh", [512, SEQ], BF16)
    kh_in = p.din("kh", [512, SEQ], BF16)
    vtl_in = p.din("vtl", [8 * 128, 32 * 65], BF16)
    cm_in = p.din("cm", [4 * 128, 512])
    ones_c = p.din("ones_c", [1, 128])
    ao_out = p.dout("ao", [512, SEQ], BF16)

    pp = [p.psum(f"pp{i}", [128, 512]) for i in range(8)]
    A = Arena(p, 209536)
    qhb = [A.bf16(k * 8192, SEQ) for k in range(4)]
    khb = [A.bf16(32768 + k * 8192, SEQ) for k in range(4)]
    sexp = [A.bf16(65536 + i * 1024, 512) for i in range(32)]
    aob = [A.bf16(98304 + k * 8192, SEQ) for k in range(4)]
    Vb = [A.bf16(131072 + i * 4160, 32 * 65) for i in range(2)]
    Pc = [A.bf16(139392 + i * 1024, 512) for i in range(4)]
    av_sb = A.f32(143488, 512)
    av2 = A.f32(145536, 512)
    recb = A.f32(147584, 512, rows=64)
    SM = 149632
    dr2 = A.t[64:65, SM:SM + 2048].bitcast(F32)
    onef64 = A.t[64:65, SM + 2048:SM + 2048 + 512].bitcast(F32)
    onef = A.f32(SM + 2560, 128, rows=1)

    for k in range(4):
        p.dma("sync", qhb[k], qh_in[k * 128:(k + 1) * 128, :])
        p.dma("sync", khb[k], kh_in[k * 128:(k + 1) * 128, :])
    for di in range(4):
        p.dma("gpsimd", Pc[di], cm_in[di * 128:(di + 1) * 128, :])
    p.dma("sync", onef[0:1, :], ones_c)
    p.dma("sync", onef64[0:1, 0:64], ones_c[0:1, 0:64])
    p.dma("sync", Vb[0][:, :], vtl_in[0:128, :])

    for s in range(8):
        m, rb = s // 2, (s % 2) * 64
        p.phase()
        if s + 1 < 8:
            p.dma("sync", Vb[(s + 1) % 2][:, :],
                  vtl_in[(s + 1) * 128:(s + 2) * 128, :])
        for tqc in range(8):
            i0 = tqc * 512
            kept = []
            for kc in range(32):
                dd = kc * 128 - i0
                if dd in (0, 128, 256, 384):
                    kept.append((kc, dd // 128))
                elif dd < 0:
                    kept.append((kc, -1))
            for g0_ in range(0, len(kept), 8):
                sub = kept[g0_:g0_ + 8]
                p.phase()
                for j, (kc, di) in enumerate(sub):
                    p.mm(pp[j][0:128, 0:512],
                         khb[m][rb:rb + 64, kc * 128:(kc + 1) * 128],
                         qhb[m][rb:rb + 64, i0:i0 + 512])
                p.phase()
                for j, (kc, di) in enumerate(sub):
                    p.act(sexp[kc], pp[j][0:128, 0:512], AF.Exp)
                if any(di >= 0 for (_kc, di) in sub):
                    p.phase()
                    for j, (kc, di) in enumerate(sub):
                        if di >= 0:
                            p.tt(sexp[kc], sexp[kc], Pc[di], ALU.mult)
            p.phase()
            nk = len(kept)
            for ki, (kc, di) in enumerate(kept):
                p.mm(pp[0][0:65, 0:512],
                     Vb[s % 2][:, kc * 65:(kc + 1) * 65],
                     sexp[kc], start=(ki == 0), stop=(ki == nk - 1))
            p.phase()
            p.cpy(av_sb[0:65, :], pp[0][0:65, 0:512])
            p.phase()
            p.ts(av_sb[64:65, :], av_sb[64:65, :], 1e-30, ALU.max)
            p.phase()
            p.act(av_sb[64:65, :], av_sb[64:65, :], AF.Ln)
            p.phase()
            p.act(dr2[0:1, 0:512], av_sb[64:65, :], AF.Exp, scale=-1.0)
            p.phase()
            p.mm(pp[2][0:64, 0:512], onef64[0:1, 0:64], dr2[0:1, 0:512])
            p.phase()
            p.cpy(recb[0:64, :], pp[2][0:64, 0:512])
            p.phase()
            p.tt(av2[0:64, :], av_sb[0:64, :], recb[0:64, :], ALU.mult)
            p.phase()
            p.act(aob[m][rb:rb + 64, i0:i0 + 512], av2[0:64, :], AF.Copy)
    p.phase()
    for k in range(4):
        p.dma("gpsimd", ao_out[k * 128:(k + 1) * 128, :], aob[k])
    return p.finish()


def build_L3F():
    p = Prog()
    ao_in = p.din("aoR", [D, T], BF16)
    wout_in = p.din("wout", [D, D])
    x7_in = p.din("x7o", [D, T])
    wtT_in = p.din("wtT", [D, D])
    alpha_in = p.din("alpha", [1, 1])
    bias_in = p.din("bias", [D, 1])
    ones_c = p.din("ones_c", [1, 128])
    ones_col_in = p.din("ones_col", [128, 1])
    x8_out = p.dout("x8", [D, T])
    if L3DBG:
        dbg_wt = p.dout("dbg_wt", [D, D], BF16)
        dbg_xa = p.dout("dbg_xa", [D, T])
        dbg_xn = p.dout("dbg_xn", [D, T], BF16)
        dbg_d = p.dout("dbg_d", [128, 4])

    pp = [p.psum(f"pp{i}", [128, 512]) for i in range(8)]
    A = Arena(p, 209536)
    xa = [A.f32(k * 8192, T) for k in range(DK)]
    aor = [A.bf16(65536 + k * 4096, T) for k in range(DK)]
    woutb = [A.bf16(98304 + k * 2048, D) for k in range(DK)]
    xnb = [A.bf16(114688 + k * 4096, T) for k in range(DK)]
    xbf = [A.bf16(65536 + k * 4096, T) for k in range(DK)]
    xsqg = [A.bf16(98304 + k * 2048, 1024) for k in range(DK)]
    LB = 147456
    lb = NS(srow=A.f32(LB, 2 * T, rows=1),
            mu=A.f32(LB + 16384, T, rows=1),
            ab=A.bf16(LB + 24576, 2 * T, rows=1),
            abc=A.bf16(LB + 32768, 2 * T))
    t2ln = A.bf16(LB, T)
    wnat = A.f32(LB, D)
    tw1 = A.f32(LB + 4096, D)
    tw2 = A.f32(LB + 8192, D)
    wtb = [A.bf16(188416 + k * 2048, D) for k in range(DK)]
    SM = 204800
    wsum = A.f32(SM, 2)
    dcol = A.f32(SM + 8, 1)
    ndcol = A.f32(SM + 12, 1)
    acol = A.f32(SM + 16, 1)
    bcol = A.f32(SM + 20, DK)
    onef = A.f32(SM + 64, 128, rows=1)
    onecolf = A.f32(SM + 576, 1)
    onebf = A.bf16(SM + 640, 128, rows=1)
    onecol = A.bf16(SM + 896, 1)
    wst4 = [A.f32(LB + 12288 + i * 2048, 512) for i in range(4)]

    for k in range(DK):
        p.dma("sync", xa[k], x7_in[k * 128:(k + 1) * 128, :])
        p.dma("sync", aor[k], ao_in[k * 128:(k + 1) * 128, :])
        p.dma("gpsimd", woutb[k], wout_in[k * 128:(k + 1) * 128, :])
    p.dma("sync", onef[0:1, :], ones_c)
    p.dma("sync", onecolf[:, 0:1], ones_col_in)
    p.dma("gpsimd", onebf[0:1, :], ones_c)
    p.dma("gpsimd", onecol[:, 0:1], ones_col_in)
    load_col(p, bcol, bias_in)
    for k in range(DK):
        p.phase()
        p.dma("sync", wnat[:, :], wtT_in[k * 128:(k + 1) * 128, :])
        p.phase()
        p.red(wsum[:, 1:2], wnat[:, :], ALU.add, absval=True)
        p.phase()
        if k == 0:
            p.cpy(wsum[:, 0:1], wsum[:, 1:2])
        else:
            p.tt(wsum[:, 0:1], wsum[:, 0:1], wsum[:, 1:2], ALU.add)
    p.phase()
    p.mm(pp[0][0:1, 0:1], wsum[:, 0:1], onecolf[:, 0:1])
    p.phase()
    p.ts(wsum[0:1, 0:1], pp[0][0:1, 0:1], 0.7 / (D * D), ALU.mult)
    p.phase()
    p.mm(pp[1][0:128, 0:1], onef[0:1, 0:128], wsum[0:1, 0:1])
    p.phase()
    p.cpy(dcol[:, 0:1], pp[1][0:128, 0:1])
    p.phase()
    p.ts(ndcol[:, 0:1], dcol[:, 0:1], -1.0, ALU.mult)
    acell = A.f32(SM + 900, 1, rows=1)
    p.dma("sync", acell[0:1, 0:1], alpha_in)
    p.phase()
    p.mm(pp[3][0:128, 0:1], onef[0:1, 0:128], acell[0:1, 0:1])
    p.phase()
    p.cpy(acol[:, 0:1], pp[3][0:128, 0:1])
    for k in range(DK):
        p.phase()
        p.dma("sync", wnat[:, :], wtT_in[k * 128:(k + 1) * 128, :])
        p.phase()
        p.ts(tw1[:, :], wnat[:, :], dcol[:, 0:1], ALU.subtract)
        p.ts(tw2[:, :], wnat[:, :], ndcol[:, 0:1], ALU.subtract, eng="gpsimd")
        p.phase()
        p.ts(tw1[:, :], tw1[:, :], 0.0, ALU.is_gt)
        p.ts(tw2[:, :], tw2[:, :], 0.0, ALU.is_lt, eng="gpsimd")
        p.phase()
        p.tt(tw1[:, :], tw1[:, :], tw2[:, :], ALU.subtract)
        p.phase()
        p.dma("gpsimd", wtb[k], tw1[:, :])
        if L3DBG:
            p.phase()
            p.dma("sync", dbg_wt[k * 128:(k + 1) * 128, :], wtb[k])
    ct = tchunks(T)
    for mo in range(DK):
        p.phase()
        for i, (o, n) in enumerate(ct):
            for k in range(DK):
                p.mm(pp[i][0:128, 0:n], woutb[k][:, mo * 128:(mo + 1) * 128],
                     aor[k][:, o:o + n], start=(k == 0), stop=(k == DK - 1))
        p.phase()
        for i, (o, n) in enumerate(ct):
            p.tt(xa[mo][:, o:o + n], xa[mo][:, o:o + n], pp[i][0:128, 0:n],
                 ALU.add)
    if L3DBG:
        p.phase()
        for k in range(DK):
            p.dma("sync", dbg_xa[k * 128:(k + 1) * 128, :], xa[k])
        p.dma("sync", dbg_d[:, 0:1], dcol[:, 0:1], slow=True)
        p.dma("sync", dbg_d[:, 1:2], acol[:, 0:1], slow=True)
        p.dma("sync", dbg_d[:, 2:3], wsum[:, 0:1], slow=True)
        p.dma("sync", dbg_d[:, 3:4], wsum[:, 1:2], slow=True)
    emit_layernorm(p, lb, T,
                   x_of=lambda k: xa[k], xbf_of=lambda k: xbf[k],
                   xsq_of=lambda k: xsqg[k], out_of=lambda k: xnb[k],
                   pp=pp, ones_col=onecol, ones_row=onebf,
                   x_is_fp32=True, t2=t2ln, xsq_group=True)
    if L3DBG:
        p.phase()
        for k in range(DK):
            p.dma("gpsimd", dbg_xn[k * 128:(k + 1) * 128, :], xnb[k])
    for mo in range(DK):
        p.phase()
        for i, (o, n) in enumerate(ct):
            for k in range(DK):
                p.mm(pp[i][0:128, 0:n], wtb[k][:, mo * 128:(mo + 1) * 128],
                     xnb[k][:, o:o + n], start=(k == 0), stop=(k == DK - 1))
        p.phase()
        for i, (o, n) in enumerate(ct):
            p.ts(wst4[i][:, 0:n], pp[i][0:128, 0:n], acol[:, 0:1],
                 ALU.mult, s2=bcol[:, mo:mo + 1], op2=ALU.add)
        p.phase()
        for i, (o, n) in enumerate(ct):
            p.tt(wst4[i][:, 0:n], wst4[i][:, 0:n], xa[mo][:, o:o + n],
                 ALU.add)
        p.phase()
        for i, (o, n) in enumerate(ct):
            p.dma("sync", x8_out[mo * 128:(mo + 1) * 128, o:o + n],
                  wst4[i][:, 0:n])
    return p.finish()


def stage_L1FB(inputs, core, xn8, pmax8):
    return {"xn": xn8[core], "pmax8": pmax8,
            "wup": inputs["L1_ffn_w_up"], "wg": inputs["L1_ffn_w_gate"],
            "ident": np.eye(128, dtype=np.float32),
            "ones_c": _ONES["ones_c"]}


def stage_L1FC(core, g8, pmg8):
    return {"gate_raw": g8[core], "pmg8": pmg8, "ones_c": _ONES["ones_c"]}


def stage_L1FD(core, u8, g28, pmu8, pm28):
    return {"up_raw": u8[core], "gate2": g28[core], "pmu8": pmu8,
            "pm28": pm28, "ones_c": _ONES["ones_c"]}


def stage_L1FE(inputs, core, o8, pmo8, x4_8):
    return {"o_raw": o8[core], "pmo8": pmo8, "wd": inputs["L1_ffn_w_down"],
            "x4o": x4_8[core], "ident": np.eye(128, dtype=np.float32),
            "ones_c": _ONES["ones_c"]}


_ONES = {"ones_c": np.ones((1, 128), np.float32),
         "ones_col": np.ones((128, 1), np.float32)}

_REV128_IDX = None


def _rev128():
    global _REV128_IDX
    if _REV128_IDX is None:
        idx = np.arange(SEQ)
        _REV128_IDX = (idx // 128) * 128 + (127 - idx % 128)
    return _REV128_IDX


def _vtl_ones(vfull, heads):
    nh = len(heads)
    out = np.zeros((nh * 128, 32 * 65), vfull.dtype)
    vr = vfull.reshape(32, 128, 16, 64)
    for i, hh in enumerate(heads):
        o3 = out[i * 128:(i + 1) * 128].reshape(128, 32, 65)
        o3[:, :, 0:64] = vr[:, :, hh, :].transpose(1, 0, 2)
        o3[:, :, 64] = 1.0
    return out


def stage_FAq(inputs, core, blk, xfull, pre="", skip_x=False):
    b, h = core // 2, core % 2
    d = {}
    if not skip_x:
        d["x_in"] = np.ascontiguousarray(xfull[b, h * T:(h + 1) * T].T)
    d.update(_ONES)
    if blk == 1:
        d["wq"] = inputs["L1_attn_Wq"]
        d["wk"] = inputs["L1_attn_Wk"]
        d["wv"] = inputs["L1_attn_Wv"]
        d["bqk"] = np.concatenate(
            [inputs["L1_attn_bq"], inputs["L1_attn_bk"]]).reshape(2 * D, 1)
        d["bv"] = inputs["L1_attn_bv"].reshape(1, D)
    elif blk == 2:
        d["wq"] = inputs["L2_attn_Wq"]
        d["wk"] = inputs["L2_attn_Wk"]
        d["wv"] = inputs["L2_attn_Wv"]
        d["g2"] = inputs["L2_attn_ln_gamma"].reshape(D, 1)
        d["b2"] = inputs["L2_attn_ln_beta"].reshape(D, 1)
    else:
        W = inputs["L3_attn_W_qkv"]
        d["wq"] = np.ascontiguousarray(W[:, 0:D])
        d["wk"] = np.ascontiguousarray(W[:, D:2 * D])
        d["wv"] = np.ascontiguousarray(W[:, 2 * D:3 * D])
    if pre:
        d = {pre + k: v for k, v in d.items()}
    return d


def stage_FAatt1(inputs, core, q8, k8, v8, x3full):
    b, h = core // 2, core % 2
    G0 = h * T
    rev = _rev128()
    kfull = np.concatenate([k8[2 * b], k8[2 * b + 1]], axis=1)[:, rev]
    vfull = np.concatenate([v8[2 * b], v8[2 * b + 1]], axis=0)[rev]
    table = inputs["L1_attn_bias_table"]
    idxs = np.minimum(G0 + np.arange(2560), 1998)
    r2c = np.ascontiguousarray(table[idxs].T).astype(np.float32)
    d = {"qb": q8[core],
         "ko": np.ascontiguousarray(kfull),
         "vtl": _vtl_ones(vfull, list(range(16))),
         "x3o": np.ascontiguousarray(x3full[b, G0:G0 + T].T),
         "wp": inputs["L1_attn_Wp"],
         "bp": inputs["L1_attn_bp"].reshape(D, 1),
         "temp": np.asarray(inputs["L1_attn_temperature"],
                            np.float32).reshape(1, 1),
         "r2c": r2c}
    d.update(_ONES)
    return d


def stage_FAatt2(inputs, core, q8, k8, v8):
    b, hh = core // 2, core % 2
    qfull = np.concatenate([q8[2 * b], q8[2 * b + 1]], axis=1)
    kfull = np.concatenate([k8[2 * b], k8[2 * b + 1]], axis=1)
    vfull = np.concatenate([v8[2 * b], v8[2 * b + 1]], axis=0)
    heads = [hh * 8 + s for s in range(8)]
    slopes = np.array([2.0 ** (-8.0 / (16 - hg)) for hg in heads], np.float32)
    jj = np.arange(128)[:, None]
    ii = np.arange(512)[None, :]
    d = {"qh": np.ascontiguousarray(qfull[hh * 512:(hh + 1) * 512]),
         "kh": np.ascontiguousarray(kfull[hh * 512:(hh + 1) * 512]),
         "vtl": _vtl_ones(vfull, heads),
         "slc": np.tile(slopes[None, :], (128, 1)),
         "slr": slopes[None, :].copy(),
         "drow": (jj - ii).astype(np.float32),
         "i2": (np.arange(128)[:, None]
                + 128 * np.arange(32)[None, :]).astype(np.float32),
         "ir": np.arange(512, dtype=np.float32)[None, :]}
    d.update(_ONES)
    return d


def stage_M1(inputs, core, ao2, x5full):
    b, h = core // 2, core % 2
    aoRf = np.concatenate([ao2[2 * b], ao2[2 * b + 1]], axis=0)
    tri = (np.arange(128)[:, None] <= np.arange(128)[None, :]
           ).astype(np.float32)
    d = {"x5o": np.ascontiguousarray(x5full[b, h * T:(h + 1) * T].T),
         "aoR": np.ascontiguousarray(aoRf[:, h * T:(h + 1) * T]),
         "wo": inputs["L2_attn_Wo"],
         "wrT": np.ascontiguousarray(inputs["L2_ffn_w_router"].T),
         "tri": tri,
         "i8r": (np.arange(128) % 8).astype(np.float32)[None, :],
         "ident": np.eye(128, dtype=np.float32),
         "zrow": np.zeros((1, 8), np.float32)}
    d.update(_ONES)
    return d


def stage_M2(inputs, core, r1, counts8):
    prevc = np.zeros((8, 8), np.float32)
    for cc in range(core):
        prevc[cc] = counts8[cc]
    d = {"xn6": r1[core]["xn6"], "x6o": r1[core]["x6"],
         "gcols": r1[core]["gcols"], "pos_loc": r1[core]["pos_loc"],
         "prevc": prevc,
         "wgT": np.ascontiguousarray(inputs["L2_ffn_w_gate"].T),
         "wdTs": np.ascontiguousarray(
             inputs["L2_ffn_w_down"].transpose(0, 2, 1).reshape(8 * 2048, D)),
         "i8r": (np.arange(128) % 8).astype(np.float32)[None, :],
         "ident": np.eye(128, dtype=np.float32)}
    d.update(_ONES)
    return d


def stage_FAatt3(inputs, core, q8, k8, v8):
    b, hh = core // 2, core % 2
    qfull = np.concatenate([q8[2 * b], q8[2 * b + 1]], axis=1)
    kfull = np.concatenate([k8[2 * b], k8[2 * b + 1]], axis=1)
    vfull = np.concatenate([v8[2 * b], v8[2 * b + 1]], axis=0)
    heads = [hh * 8 + s for s in range(8)]
    jc = np.arange(128)[:, None]
    ic = np.arange(512)[None, :]
    cm = np.concatenate(
        [(jc + di * 128 <= ic).astype(np.float32) for di in range(4)], 0)
    d = {"qh": np.ascontiguousarray(qfull[hh * 512:(hh + 1) * 512]),
         "kh": np.ascontiguousarray(kfull[hh * 512:(hh + 1) * 512]),
         "vtl": _vtl_ones(vfull, heads),
         "cm": cm}
    d.update(_ONES)
    return d


def stage_L3F(inputs, core, ao3, x7full):
    b, h = core // 2, core % 2
    aoRf = np.concatenate([ao3[2 * b], ao3[2 * b + 1]], axis=0)
    d = {"aoR": np.ascontiguousarray(aoRf[:, h * T:(h + 1) * T]),
         "wout": inputs["L3_attn_W_out"],
         "x7o": np.ascontiguousarray(x7full[b, h * T:(h + 1) * T].T),
         "wtT": np.ascontiguousarray(inputs["L3_ffn_weight"].T),
         "alpha": np.asarray(inputs["L3_ffn_alpha"],
                             np.float32).reshape(1, 1),
         "bias": inputs["L3_ffn_bias"].reshape(D, 1)}
    d.update(_ONES)
    return d


def gather_full(results, key):
    full = np.zeros((B_SZ, SEQ, D), np.float32)
    for c in range(NCORE):
        b, h = c // 2, c % 2
        full[b, h * T:(h + 1) * T] = results[c][key].T
    return full


def run_l0ffn(inputs, x1full):
    ra = run_launch(lambda: build_L0Fa(True),
                    lambda ii, c: stage_L0Fa(ii, c, 0, x1full), inputs)
    rb = run_launch(lambda: build_L0Fb(False),
                    lambda ii, c: stage_L0Fb(ii, c, 0, ra[c]["u"]), inputs)
    vfull = gather_full(rb, "v")
    ra2 = run_launch(lambda: build_L0Fa(False),
                     lambda ii, c: stage_L0Fa(ii, c, 1, vfull), inputs)
    rb2 = run_launch(lambda: build_L0Fb(True),
                     lambda ii, c: stage_L0Fb(ii, c, 1, ra2[c]["u"], x1full),
                     inputs)
    return gather_full(rb2, "v")


def run_launch(build_fn, stage_fn, inputs):
    nc = build_fn()
    in_maps = [stage_fn(inputs, c) for c in range(NCORE)]
    res = run_bass_kernel_spmd(nc, in_maps, core_ids=list(range(NCORE)))
    return res.results


def kernel(**inputs):
    r1 = run_launch(build_L01, stage_L01, inputs)
    x1 = gather_full(r1, "x1")
    ra0 = run_launch(lambda: build_L0Fa(True),
                     lambda ii, c: stage_L0Fa(ii, c, 0, x1), inputs)
    rb0 = run_launch(lambda: build_L0Fb(False),
                     lambda ii, c: stage_L0Fb(ii, c, 0, ra0[c]["u"]), inputs)
    vfull = gather_full(rb0, "v")
    ra1 = run_launch(lambda: build_L0Fa(False),
                     lambda ii, c: stage_L0Fa(ii, c, 1, vfull), inputs)
    rb1 = run_launch(
        lambda: build_L0Fb(True, chain_faq=True),
        lambda ii, c: {**stage_L0Fb(ii, c, 1, ra1[c]["u"], x1),
                       **stage_FAq(ii, c, 1, None, pre="q1_", skip_x=True)},
        inputs)
    x2 = gather_full(rb1, "v")
    q8 = [rb1[c]["q1_q"] for c in range(NCORE)]
    k8 = [rb1[c]["q1_k"] for c in range(NCORE)]
    v8 = [rb1[c]["q1_v"] for c in range(NCORE)]
    ra = run_launch(build_FAatt1,
                    lambda ii, c: stage_FAatt1(ii, c, q8, k8, v8, x2), inputs)
    x3_8 = [ra[c]["x4"] for c in range(NCORE)]
    xn8 = [ra[c]["xn4"] for c in range(NCORE)]
    pmax8 = np.concatenate([ra[c]["pmax"] for c in range(NCORE)], 1)
    rb = run_launch(build_L1FB,
                    lambda ii, c: stage_L1FB(ii, c, xn8, pmax8), inputs)
    pmu8 = np.concatenate([rb[c]["pmu"] for c in range(NCORE)], 1)
    pmg8 = np.concatenate([rb[c]["pmg"] for c in range(NCORE)], 1)
    g8 = [rb[c]["gate_raw"] for c in range(NCORE)]
    u8 = [rb[c]["up_raw"] for c in range(NCORE)]
    rc = run_launch(build_L1FC, lambda ii, c: stage_L1FC(c, g8, pmg8), inputs)
    pm28 = np.concatenate([rc[c]["pm2"] for c in range(NCORE)], 1)
    g28 = [rc[c]["gate2"] for c in range(NCORE)]
    rd = run_launch(build_L1FD,
                    lambda ii, c: stage_L1FD(c, u8, g28, pmu8, pm28), inputs)
    pmo8 = np.concatenate([rd[c]["pmo"] for c in range(NCORE)], 1)
    o8 = [rd[c]["o_raw"] for c in range(NCORE)]
    re_ = run_launch(
        lambda: build_L1FE(chain_faq=True),
        lambda ii, c: {**stage_L1FE(ii, c, o8, pmo8, x3_8),
                       **stage_FAq(ii, c, 2, None, pre="q2_", skip_x=True)},
        inputs)
    x4 = gather_full(re_, "x5")
    q8 = [re_[c]["q2_q"] for c in range(NCORE)]
    k8 = [re_[c]["q2_k"] for c in range(NCORE)]
    v8 = [re_[c]["q2_v"] for c in range(NCORE)]
    ra2 = run_launch(build_FAatt2,
                     lambda ii, c: stage_FAatt2(ii, c, q8, k8, v8), inputs)
    ao2 = [ra2[c]["ao"] for c in range(NCORE)]
    r1m = run_launch(build_M1, lambda ii, c: stage_M1(ii, c, ao2, x4), inputs)
    counts8 = [r1m[c]["counts"][0] for c in range(NCORE)]
    r2m = run_launch(
        lambda: build_M2(chain_faq=True),
        lambda ii, c: {**stage_M2(ii, c, r1m, counts8),
                       **stage_FAq(ii, c, 3, None, pre="q3_", skip_x=True)},
        inputs)
    x6 = gather_full(r2m, "x7")
    q8 = [r2m[c]["q3_q"] for c in range(NCORE)]
    k8 = [r2m[c]["q3_k"] for c in range(NCORE)]
    v8 = [r2m[c]["q3_v"] for c in range(NCORE)]
    ra3 = run_launch(build_FAatt3,
                     lambda ii, c: stage_FAatt3(ii, c, q8, k8, v8), inputs)
    ao3 = [ra3[c]["ao"] for c in range(NCORE)]
    rf = run_launch(build_L3F, lambda ii, c: stage_L3F(ii, c, ao3, x6),
                    inputs)
    return gather_full(rf, "x8")
